# version 04
# baseline (speedup 1.0000x reference)
import numpy as np
import concourse.bass as bass
import concourse.mybir as mybir
from concourse.bass_utils import run_bass_kernel_spmd

F32 = mybir.dt.float32
BF16 = mybir.dt.bfloat16
AF = mybir.ActivationFunctionType
ALU = mybir.AluOpType
AX = mybir.AxisListType

NCORE = 8
D = 2048
T = 2048
P = 1024
NPASS = 2
NK = 16
NE = 8
NJ = 11
NM1 = 22
NM2 = 16
QD = 512
SQRT_HS = float(np.sqrt(np.float32(128.0)))
EPS = 1e-5
NEG = -1.0e30

ENGINES = ("tensor", "vector", "scalar", "gpsimd", "sync")
SEMS = ("t", "v", "s", "w", "y", "g")


class Sched:
    def __init__(self):
        self.items = {e: [] for e in ENGINES}
        self.c = {s: 0 for s in SEMS}
        self._waited = {e: {s: 0 for s in SEMS} for e in ENGINES}

    def emit(self, eng, fn, waits=(), inc=None, n=1):
        if eng == "vector":
            waits = list(waits) + [("v", self.c["v"])]
        elif eng == "scalar":
            waits = list(waits) + [("s", self.c["s"])]
        wl = []
        for s, v in waits:
            if v is None or v <= 0:
                continue
            if v > self._waited[eng][s]:
                self._waited[eng][s] = v
                wl.append((s, v))
        if inc is not None:
            self.c[inc] += n
        self.items[eng].append((fn, tuple(wl), inc, n))
        return self.c[inc] if inc is not None else None

    def barrier(self):
        snap = dict(self.c)
        for e in ENGINES:
            w = [(s, snap[s]) for s in SEMS if snap[s] > self._waited[e][s]]
            if w:
                self.emit(e, None, waits=w)

    def check_deadlock(self):
        import collections
        pos = {e: 0 for e in ENGINES}
        cnt = {s: 0 for s in SEMS}
        n = sum(len(v) for v in self.items.values())
        done = 0
        while done < n:
            progressed = False
            for e in ENGINES:
                while pos[e] < len(self.items[e]):
                    fn, waits, inc, k = self.items[e][pos[e]]
                    if all(cnt[s] >= v for s, v in waits):
                        if inc is not None:
                            cnt[inc] += k
                        pos[e] += 1
                        done += 1
                        progressed = True
                    else:
                        break
            if not progressed:
                for e in ENGINES:
                    if pos[e] < len(self.items[e]):
                        fn, waits, inc, k = self.items[e][pos[e]]
                        print(f"STUCK {e} at item {pos[e]}: waits {waits}, have "
                              f"{[(s, cnt[s]) for s, v in waits]}")
                raise RuntimeError("schedule deadlock")
        return True

    def replay(self, eng_name, eng, sems):
        for fn, waits, inc, n in self.items[eng_name]:
            for s, v in waits:
                eng.wait_ge(sems[s], v)
            if fn is None:
                continue
            inst = fn(eng)
            if inc is not None:
                inst.then_inc(sems[inc], n)


def build_nc(taps=(), stage="full"):
    taps = set(taps)
    nc = bass.Bass("TRN2", target_bir_lowering=False)

    xin = nc.dram_tensor("xin", [NPASS, NK, 128, P], F32, kind="ExternalInput").ap()
    w1t = nc.dram_tensor("w1t", [NE, NM1, 128, NK * 128], F32, kind="ExternalInput").ap()
    w2t = nc.dram_tensor("w2t", [NE, NM2, 128, NJ * 128], F32, kind="ExternalInput").ap()
    sw1t = nc.dram_tensor("sw1t", [NJ, 128, NK * 128], F32, kind="ExternalInput").ap()
    sw2t = nc.dram_tensor("sw2t", [NJ, 128, NK * 128], F32, kind="ExternalInput").ap()
    sw3t = nc.dram_tensor("sw3t", [NM2, 128, NJ * 128], F32, kind="ExternalInput").ap()
    gwt_in = nc.dram_tensor("gwt_in", [128, NK * NE], F32, kind="ExternalInput").ap()
    wqT_in = nc.dram_tensor("wqT_in", [NK, 128, D], F32, kind="ExternalInput").ap()
    wkT_in = nc.dram_tensor("wkT_in", [NK, 128, QD], F32, kind="ExternalInput").ap()
    wvT_in = nc.dram_tensor("wvT_in", [NK, 128, QD], F32, kind="ExternalInput").ap()
    projt_in = nc.dram_tensor("projt_in", [NK, 128, D], F32, kind="ExternalInput").ap()
    bq_in = nc.dram_tensor("bq_in", [1, D], F32, kind="ExternalInput").ap()
    bk_in = nc.dram_tensor("bk_in", [1, QD], F32, kind="ExternalInput").ap()
    bv_in = nc.dram_tensor("bv_in", [1, QD], F32, kind="ExternalInput").ap()
    cmask_in = nc.dram_tensor("cmask_in", [128, 128], F32, kind="ExternalInput").ap()
    ident_in = nc.dram_tensor("ident_in", [128, 128], F32, kind="ExternalInput").ap()
    yout = nc.dram_tensor("yout", [T, D], F32, kind="ExternalOutput").ap()

    tap_drams = {}
    def tapd(name, shape):
        if name in taps:
            tap_drams[name] = nc.dram_tensor("tap_" + name, list(shape), F32,
                                             kind="ExternalOutput").ap()
    tapd("lnb", (128, NK * P)); tapd("rb", (128, P)); tapd("nb", (128, P))
    tapd("cw", (128, 64)); tapd("act", (128, NJ * P)); tapd("x1", (128, NK * P))
    tapd("x2", (128, NK * P)); tapd("qs", (128, 3 * P)); tapd("k5", (128, 4 * T))
    tapd("v5", (128, 16 * QD)); tapd("zs", (128, 4 * P)); tapd("xload", (128, NK * P))
    tapd("glog", (128, P)); tapd("ltm", (128, 64)); tapd("ee", (128, 64))
    tapd("wqst", (128, 16 * 512)); tapd("bqs", (1, 1536))
    tapd("inds", (128, 64)); tapd("m1m2", (128, 24)); tapd("gws", (128, 1))

    from contextlib import ExitStack
    with ExitStack() as stack:
        E = stack.enter_context

        x_sb = E(nc.sbuf_tensor("x_sb", [128, NK * P], F32))
        lnb = E(nc.sbuf_tensor("lnb", [128, NK * P], BF16))
        k5 = E(nc.sbuf_tensor("k5", [128, 4 * T], BF16))
        v5 = E(nc.sbuf_tensor("v5", [128, 16 * QD], BF16))
        rb = E(nc.sbuf_tensor("rb", [128, P], F32))
        nb = E(nc.sbuf_tensor("nb", [128, P], F32))
        xb = E(nc.sbuf_tensor("xb", [128, 2 * P], BF16))
        gsc = E(nc.sbuf_tensor("gsc", [128, P], F32))
        cw_tm = E(nc.sbuf_tensor("cw_tm", [128, 64], F32))
        gwt = E(nc.sbuf_tensor("gwt", [128, NK * NE], F32))
        gwsum = E(nc.sbuf_tensor("gwsum", [128, 1], F32))
        identb = E(nc.sbuf_tensor("identb", [128, 128], BF16))
        identf = E(nc.sbuf_tensor("identf", [128, 128], F32))
        cmaskb = E(nc.sbuf_tensor("cmaskb", [128, 128], BF16))
        bias_bf = E(nc.sbuf_tensor("bias_bf", [1, 1536], BF16))
        ones_row_bf = E(nc.sbuf_tensor("ones_row_bf", [1, P], BF16))
        ones_col_bf = E(nc.sbuf_tensor("ones_col_bf", [128, 1], BF16))
        ones1_f = E(nc.sbuf_tensor("ones1_f", [1, 128], F32))
        ones128_f = E(nc.sbuf_tensor("ones128_f", [128, 1], F32))
        eps_col = E(nc.sbuf_tensor("eps_col", [128, 1], F32))
        smx = E(nc.sbuf_tensor("smx", [128, 8], F32))

        A16 = E(nc.sbuf_tensor("A16", [128, 20224], BF16))
        A32 = E(nc.sbuf_tensor("A32", [128, 4096], F32))
        cwsc = A32[:, 0:224]
        bqstage = A16[0:1, 0:2048]

        PS = [E(nc.psum_tensor(f"PS{i}", [128, 1024], F32)) for i in range(4)]
        sems = {s: E(nc.semaphore("sem_" + s)) for s in SEMS}
        block = E(nc.Block())
        S = Sched()

        ACT_O = 0
        W1R_O = 11264
        W1R_SLOTS = 3
        W2R_O = W1R_O + W1R_SLOTS * 2048
        W2R_SLOTS = 2
        WKR_O = W2R_O
        WVR_O = W2R_O + 512
        QS3_O = W2R_O + 1024
        ZR_O = W2R_O + 2048
        WQST_O = 0
        QS_O = 8192
        PB_O = 0
        PT_O = 2048
        ZS_O = 4096
        PJ_O = 8192
        SG_O = 0
        CWB_O = 2048
        TMP_O = 0
        OUT_O = 0

        def a16(off, n):
            return A16[:, off:off + n]

        def qs_ap(m, lo=0, n_=P):
            off = (QS_O + m * P) if m < 3 else QS3_O
            return A16[:, off + lo: off + lo + n_]

        def a32(off, n):
            return A32[:, off:off + n]

        def xc(c, lo=0, hi=P):
            return x_sb[:, c * P + lo: c * P + hi]

        def lnc(c, lo=0, hi=P):
            return lnb[:, c * P + lo: c * P + hi]

        bqs_ap = bias_bf[0:1, 0:512]
        bk_ap = bias_bf[0:1, 512:1024]
        bv_ap = bias_bf[0:1, 1024:1536]

        def prelude():
            S.emit("gpsimd", lambda e: e.memset(ones_row_bf[:], 1.0), inc="g")
            S.emit("gpsimd", lambda e: e.memset(ones_col_bf[:], 1.0), inc="g")
            S.emit("gpsimd", lambda e: e.memset(ones1_f[:], 1.0), inc="g")
            S.emit("gpsimd", lambda e: e.memset(ones128_f[:], 1.0), inc="g")
            S.emit("gpsimd", lambda e: e.memset(eps_col[:], EPS), inc="g")
            S.emit("gpsimd", lambda e: e.dma_start(identb[:], ident_in), inc="w", n=16)
            S.emit("gpsimd", lambda e: e.dma_start(identf[:], ident_in), inc="w", n=16)
            S.emit("gpsimd", lambda e: e.dma_start(cmaskb[:], cmask_in), inc="w", n=16)
            S.emit("gpsimd", lambda e: e.dma_start(gwt[:], gwt_in), inc="w", n=16)
            S.emit("gpsimd", lambda e: e.dma_start(bqstage, bq_in), inc="w", n=16)
            S.emit("gpsimd", lambda e: e.dma_start(bias_bf[0:1, 512:1024], bk_in), inc="w", n=16)
            w_mark = S.emit("gpsimd", lambda e: e.dma_start(bias_bf[0:1, 1024:1536], bv_in),
                            inc="w", n=16)
            tm = None
            for r in range(4):
                tm = S.emit("tensor", (lambda r=r: lambda e:
                                       e.matmul(PS[3][0:1, 0:512], ones_row_bf[0:1, 0:1],
                                                bqstage[0:1, r * 512:(r + 1) * 512],
                                                start=(r == 0), stop=(r == 3)))(),
                            waits=[("w", w_mark)], inc="t")
            S.emit("vector", lambda e: e.tensor_copy(bias_bf[0:1, 0:512], PS[3][0:1, 0:512]),
                   waits=[("t", tm)], inc="v")
            tm2 = None
            for c in range(NK):
                tm2 = S.emit("tensor", (lambda c=c: lambda e:
                                        e.matmul(PS[2][0:8, 0:1], gwt[:, c * NE:(c + 1) * NE],
                                                 ones128_f[:], start=(c == 0),
                                                 stop=(c == NK - 1)))(),
                             waits=[("w", w_mark)], inc="t")
            S.emit("vector", lambda e: e.tensor_copy(gwsum[0:8, 0:1], PS[2][0:8, 0:1]),
                   waits=[("t", tm2)], inc="v")

        def tap(name, src_ap_fn):
            if name not in tap_drams:
                return
            dst = tap_drams[name]
            S.barrier()
            S.emit("gpsimd", (lambda dst=dst, f=src_ap_fn: lambda e: e.dma_start(dst, f()))(),
                   inc="w", n=16)
            S.barrier()

        def emit_ln(x_ready):
            s_marks = [None] * NK
            t_marks = [None] * NK
            for c in range(NK):
                w = [x_ready[c] if isinstance(x_ready, list) else x_ready]
                if c >= 2 and t_marks[c - 2] is not None:
                    w.append(("t", t_marks[c - 2]))
                xbv = xb[:, (c % 2) * P:(c % 2 + 1) * P]
                cast_m = S.emit("scalar", (lambda xbv=xbv, c=c: lambda e: e.copy(xbv, xc(c)))(),
                                waits=w, inc="s")
                tsum = None
                for n in range(2):
                    tsum = S.emit("tensor", (lambda xbv=xbv, n=n, c=c: lambda e:
                                             e.matmul(PS[3][0:1, n * 512:(n + 1) * 512],
                                                      ones_col_bf[:],
                                                      xbv[:, n * 512:(n + 1) * 512],
                                                      start=(c == 0), stop=(c == NK - 1)))(),
                                  waits=[("s", cast_m)], inc="t" if n == 1 else None)
                s_marks[c] = S.emit("scalar", (lambda xbv=xbv: lambda e:
                                               e.activation(xbv, xbv, AF.Square))(),
                                    waits=[("t", tsum)], inc="s")
                for n in range(2):
                    t_marks[c] = S.emit("tensor", (lambda xbv=xbv, n=n, c=c: lambda e:
                                                   e.matmul(PS[2][0:1, n * 512:(n + 1) * 512],
                                                            ones_col_bf[:],
                                                            xbv[:, n * 512:(n + 1) * 512],
                                                            start=(c == 0), stop=(c == NK - 1)))(),
                                        waits=[("s", s_marks[c])], inc="t" if n == 1 else None)
            tm_stats = t_marks[NK - 1]
            S1 = A32[0:1, 0:P]
            S2 = A32[0:1, P:2 * P]
            S3 = A32[0:1, 2 * P:3 * P]
            S4 = A32[0:1, 3 * P:4 * P]
            vm_sums = S.emit("vector", lambda e: e.tensor_copy(S1, PS[3][0:1, 0:P]),
                             waits=[("t", tm_stats)], inc="v")
            vm_sumsq = S.emit("vector", lambda e: e.tensor_copy(S2, PS[2][0:1, 0:P]), inc="v")
            vm_m = S.emit("vector", lambda e: e.tensor_scalar_mul(S3, S1, 1.0 / D), inc="v")
            sm = S.emit("scalar", lambda e: e.activation(S4, S3, AF.Square),
                        waits=[("v", vm_m)], inc="s")
            vm_var = S.emit("vector", lambda e: e.scalar_tensor_tensor(S2, S2, 1.0 / D, S4,
                                                                      ALU.mult, ALU.subtract),
                            waits=[("s", sm)], inc="v")
            sm = S.emit("scalar", lambda e: e.activation(S4, S2, AF.Ln, bias=eps_col[0:1, 0:1]),
                        waits=[("v", vm_var)], inc="s")
            sm_r = S.emit("scalar", lambda e: e.activation(S1, S4, AF.Exp, scale=-0.5),
                          waits=[("v", vm_m)], inc="s")
            vm_nmr = S.emit("vector", lambda e: e.scalar_tensor_tensor(S3, S3, -1.0, S1,
                                                                      ALU.mult, ALU.mult),
                            waits=[("s", sm_r)], inc="v")
            for n in range(2):
                S.emit("tensor", (lambda n=n, S1=S1: lambda e:
                                  e.matmul(PS[1][:, n * 512:(n + 1) * 512], ones1_f[:],
                                           S1[0:1, n * 512:(n + 1) * 512],
                                           start=True, stop=True))(),
                       waits=[("s", sm_r)], inc=None)
            tm = None
            for n in range(2):
                tm = S.emit("tensor", (lambda n=n, S3=S3: lambda e:
                                       e.matmul(PS[2][:, n * 512:(n + 1) * 512], ones1_f[:],
                                                S3[0:1, n * 512:(n + 1) * 512],
                                                start=True, stop=True))(),
                            waits=[("v", vm_nmr), ("v", vm_sumsq)], inc="t" if n == 1 else None)
            S.emit("vector", lambda e: e.tensor_copy(rb[:], PS[1][:]), waits=[("t", tm)], inc="v")
            vb = S.emit("vector", lambda e: e.tensor_copy(nb[:], PS[2][:]), inc="v")
            return vb

        def emit_ln_apply():
            lnb_marks = [None] * NK
            tmpv = a32(TMP_O, P)
            for c in range(NK):
                S.emit("vector", (lambda c=c, tmpv=tmpv: lambda e:
                                  e.tensor_mul(tmpv, xc(c), rb[:]))(), inc="v")
                lnb_marks[c] = S.emit("vector", (lambda c=c, tmpv=tmpv: lambda e:
                                                 e.tensor_add(lnc(c), tmpv, nb[:]))(), inc="v")
            return lnb_marks

        w1ring = [("g", 0)] * W1R_SLOTS
        w2ring = [("g", 0)] * W2R_SLOTS
        ring_idx = {"w1": 0, "w2": 0}

        def w1_slot_ap(i):
            return a16(W1R_O + i * 2048, 2048)

        def w2_slot_ap(i):
            return a16(W2R_O + i * 1408, 1408)

        def dma_w1_block(src_ap):
            i = ring_idx["w1"] % W1R_SLOTS
            ring_idx["w1"] += 1
            dst = w1_slot_ap(i)
            wm = S.emit("gpsimd", (lambda dst=dst, src=src_ap: lambda e:
                                   e.dma_start(dst, src))(),
                        waits=[w1ring[i]], inc="w", n=16)
            return i, wm

        def dma_w2_block(src_ap):
            i = ring_idx["w2"] % W2R_SLOTS
            ring_idx["w2"] += 1
            dst = w2_slot_ap(i)
            wm = S.emit("gpsimd", (lambda dst=dst, src=src_ap: lambda e:
                                   e.dma_start(dst, src))(),
                        waits=[w2ring[i]], inc="w", n=16)
            return i, wm

        def emit_gate_mms(x_waits):
            tm = None
            for c in range(NK):
                for n in range(2):
                    w = list(x_waits[c]) if n == 0 else []
                    last = (c == NK - 1 and n == 1)
                    tm = S.emit("tensor", (lambda c=c, n=n: lambda e:
                                           e.matmul(PS[0][0:8, n * 512:(n + 1) * 512],
                                                    gwt[:, c * NE:(c + 1) * NE],
                                                    xc(c, n * 512, (n + 1) * 512),
                                                    start=(c == 0), stop=(c == NK - 1)))(),
                                waits=w, inc="t" if last else None)
            return tm

        def emit_gate(lnb_vb, gate_tm):
            g0 = gsc[0:8, 0:P]
            g1 = A32[0:8, 0:P]
            S.emit("vector", lambda e: e.tensor_mul(A32[0:8, P:2 * P], PS[0][0:8, 0:P], rb[0:8, 0:P]),
                   waits=[("t", gate_tm)], inc="v")
            S.emit("vector", lambda e: e.tensor_scalar_mul(g1, nb[0:8, 0:P], gwsum[0:8, 0:1]),
                   inc="v")
            t1_mark = S.emit("vector", lambda e: e.tensor_add(g0, A32[0:8, P:2 * P], g1), inc="v")
            vm = t1_mark
            tm = None
            for tt in range(8):
                tm = S.emit("tensor", (lambda tt=tt: lambda e:
                                       e.transpose(PS[2][:, tt * 8:(tt + 1) * 8],
                                                   g0[0:8, tt * 128:(tt + 1) * 128],
                                                   identf[0:8, 0:8]))(),
                            waits=[("v", vm)], inc="t" if tt == 7 else None)
            L = cwsc[:, 0:64]
            E_ = cwsc[:, 64:128]
            SCR = cwsc[:, 128:192]
            mA = smx[:, 0:8]
            mB = cwsc[:, 192:200]
            mC = cwsc[:, 200:208]
            vm = S.emit("vector", lambda e: e.tensor_copy(L, PS[2][:, 0:64]),
                        waits=[("t", tm)], inc="v")
            L3 = L.rearrange("p (a b) -> p a b", b=8)
            E3 = E_.rearrange("p (a b) -> p a b", b=8)
            S3 = SCR.rearrange("p (a b) -> p a b", b=8)
            mA3 = mA.rearrange("p (a one) -> p a one", one=1)
            mB3 = mB.rearrange("p (a one) -> p a one", one=1)
            mC3 = mC.rearrange("p (a one) -> p a one", one=1)
            def bcast(small3, big3):
                b1, _ = bass.broadcast_tensor_aps(small3, big3)
                return b1
            S.emit("vector", lambda e: e.reduce_max(mA, L3, axis=AX.X, negate=True), inc="v")
            vm = S.emit("vector", lambda e: e.tensor_tensor(L3, L3, bcast(mA3, L3), ALU.add),
                        inc="v")
            sm = S.emit("scalar", lambda e: e.activation(E_, L, AF.Exp), waits=[("v", vm)], inc="s")
            S.emit("vector", lambda e: e.reduce_max(mA, E3, axis=AX.X), waits=[("s", sm)], inc="v")
            S.emit("vector", lambda e: e.tensor_tensor(S3, E3, bcast(mA3, E3), ALU.is_ge), inc="v")
            S.emit("vector", lambda e: e.tensor_scalar(SCR, SCR, -1.0, 1.0, ALU.mult, ALU.add),
                   inc="v")
            S.emit("vector", lambda e: e.tensor_tensor(SCR, E_, SCR, ALU.mult), inc="v")
            S.emit("vector", lambda e: e.reduce_max(mB, S3, axis=AX.X), inc="v")
            S.emit("vector", lambda e: e.tensor_tensor(S3, E3, bcast(mB3, E3), ALU.is_ge), inc="v")
            S.emit("vector", lambda e: e.scalar_tensor_tensor(mC, mA, 1e-20, mB, ALU.add, ALU.add),
                   inc="v")
            mD = cwsc[:, 208:216]
            mD3 = mD.rearrange("p (a one) -> p a one", one=1)
            S.emit("vector", lambda e: e.reciprocal(mD, mC), inc="v")
            S.emit("vector", lambda e: e.tensor_tensor(SCR, E_, SCR, ALU.mult), inc="v")
            vm = S.emit("vector", lambda e: e.tensor_tensor(
                cw_tm[:].rearrange("p (a b) -> p a b", b=8), S3, bcast(mD3, S3), ALU.mult), inc="v")
            return vm

        def emit_cwb_prep(e_idx, wait_marks):
            tm = None
            for tt in range(8):
                col = tt * 8 + e_idx
                w = list(wait_marks) if tt == 0 else []
                tm = S.emit("tensor", (lambda col=col, tt=tt: lambda e:
                                       e.transpose(PS[3][0:1, tt * 128:(tt + 1) * 128],
                                                   cw_tm[:, col:col + 1], identf[:]))(),
                            waits=w, inc="t" if tt == 7 else None)
            cwrow = gsc[0:1, 0:P]
            vm = S.emit("vector", (lambda cwrow=cwrow: lambda e:
                                   e.tensor_copy(cwrow, PS[3][0:1, 0:P]))(),
                        waits=[("t", tm)], inc="v")
            tm = None
            for n in range(2):
                tm = S.emit("tensor", (lambda n=n, cwrow=cwrow: lambda e:
                                       e.matmul(PS[3][:, n * 512:(n + 1) * 512], ones1_f[:],
                                                cwrow[0:1, n * 512:(n + 1) * 512],
                                                start=True, stop=True))(),
                            waits=[("v", vm)], inc="t" if n == 1 else None)
            dstv = a32(CWB_O + (e_idx % 2) * P, P)
            vm = S.emit("vector", (lambda dstv=dstv: lambda e:
                                   e.tensor_copy(dstv, PS[3][:]))(),
                        waits=[("t", tm)], inc="v")
            return vm

        def emit_expert_w1(eidx, lnb_marks, cwb_mark, psum_free, shared=False):
            act_marks = [None] * NJ
            last_sg = None
            last_tt1 = None
            tmk = None
            for j in range(NJ):
                if shared:
                    blocks = [(sw1t[j], 0), (sw2t[j], 1)]
                else:
                    blocks = [(w1t[eidx, j], 0), (w1t[eidx, NJ + j], 1)]
                for src, gu in blocks:
                    slot, wm = dma_w1_block(src)
                    slotap = w1_slot_ap(slot)
                    first = True
                    for k in range(NK):
                        for n in range(2):
                            w = [("v", lnb_marks[k])]
                            if first:
                                w.append(("w", wm))
                                if j == 0 and psum_free.get(gu):
                                    w.append(psum_free[gu])
                                if gu == 0 and last_sg is not None:
                                    w.append(("s", last_sg))
                                if gu == 1 and last_tt1 is not None:
                                    w.append(("v", last_tt1))
                                first = False
                            last = (k == NK - 1 and n == 1)
                            tmk = S.emit("tensor",
                                         (lambda gu=gu, n=n, k=k, slotap=slotap: lambda e:
                                          e.matmul(PS[gu][:, n * 512:(n + 1) * 512],
                                                   slotap[:, k * 128:(k + 1) * 128],
                                                   lnc(k, n * 512, (n + 1) * 512),
                                                   start=(k == 0), stop=(k == NK - 1)))(),
                                         waits=w, inc="t" if last else None)
                    w1ring[slot] = ("t", tmk)
                sgv = a32(SG_O + (j % 2) * P, P)
                last_sg = S.emit("scalar", (lambda sgv=sgv: lambda e:
                                            e.activation(sgv, PS[0][:], AF.Silu))(),
                                 waits=[("t", tmk)], inc="s")
                if shared:
                    last_tt1 = S.emit("vector", (lambda sgv=sgv, j=j: lambda e:
                                                 e.tensor_mul(a16(ACT_O + j * P, P), sgv, PS[1][:]))(),
                                      waits=[("s", last_sg)], inc="v")
                    act_marks[j] = last_tt1
                else:
                    last_tt1 = S.emit("vector", (lambda sgv=sgv: lambda e:
                                                 e.tensor_mul(PS[2][:], sgv, PS[1][:]))(),
                                      waits=[("s", last_sg)], inc="v")
                    cwbv = a32(CWB_O + (eidx % 2) * P, P)
                    w2_ = [("v", cwb_mark)] if (j == 0 and cwb_mark) else []
                    act_marks[j] = S.emit("vector", (lambda j=j, cwbv=cwbv: lambda e:
                                                     e.tensor_mul(a16(ACT_O + j * P, P),
                                                                  PS[2][:], cwbv))(),
                                          waits=w2_, inc="v")
            return act_marks, last_sg, last_tt1

        def emit_expert_w2(eidx, act_marks, w1_last_sg, w1_last_tt1, shared=False):
            prev_stt = [None, None]
            for m in range(NM2):
                psum = PS[m % 2]
                src = sw3t[m] if shared else w2t[eidx, m]
                slot, wm = dma_w2_block(src)
                slotap = w2_slot_ap(slot)
                first = True
                tmk = None
                for j in range(NJ):
                    for n in range(2):
                        w = [("v", act_marks[j])]
                        if first:
                            w.append(("w", wm))
                            if prev_stt[m % 2] is not None:
                                w.append(("v", prev_stt[m % 2]))
                            if m == 0:
                                w.append(("s", w1_last_sg))
                            if m == 1:
                                w.append(("v", w1_last_tt1))
                            first = False
                        last = (j == NJ - 1 and n == 1)
                        tmk = S.emit("tensor",
                                     (lambda n=n, j=j, slotap=slotap, psum=psum: lambda e:
                                      e.matmul(psum[:, n * 512:(n + 1) * 512],
                                               slotap[:, j * 128:(j + 1) * 128],
                                               a16(ACT_O + j * P + n * 512, 512),
                                               start=(j == 0), stop=(j == NJ - 1)))(),
                                     waits=w, inc="t" if last else None)
                w2ring[slot] = ("t", tmk)
                prev_stt[m % 2] = S.emit("vector", (lambda m=m, psum=psum: lambda e:
                                                    e.scalar_tensor_tensor(xc(m), psum[:], 0.5,
                                                                           xc(m), ALU.mult,
                                                                           ALU.add))(),
                                         waits=[("t", tmk)], inc="v")
            return prev_stt

        def emit_ffn(lnb_vb, gate_tm):
            cw_mark = emit_gate(lnb_vb, gate_tm)
            cwb_marks = {0: emit_cwb_prep(0, [("v", cw_mark)])}
            lnb_marks = emit_ln_apply()
            psum_free = {0: ("v", cw_mark)}
            for e_ in range(NE):
                if e_ + 1 < NE:
                    cwb_marks[e_ + 1] = emit_cwb_prep(e_ + 1, [("v", cwb_marks[e_])])
                am, lsg, ltt = emit_expert_w1(e_, lnb_marks, cwb_marks[e_], psum_free)
                stt = emit_expert_w2(e_, am, lsg, ltt)
                psum_free = {0: ("v", stt[0]), 1: ("v", stt[1])}
            am, lsg, ltt = emit_expert_w1(0, lnb_marks, None, psum_free, shared=True)
            emit_expert_w2(0, am, lsg, ltt, shared=True)

        def emit_attn(p, lnb_marks, lnb_vb):
            wq_marks = [None] * NK
            for k in range(NK):
                slot, wm = dma_w1_block(wqT_in[k])
                slotap = w1_slot_ap(slot)
                tmk = None
                for r in range(4):
                    w = [("w", wm)] if r == 0 else []
                    if r == 0:
                        w.append(("v", lnb_vb if k == 0 else wq_marks[k - 1]))
                    tmk = S.emit("tensor", (lambda r=r, slotap=slotap: lambda e:
                                            e.matmul(PS[2][:, 0:512], identb[:],
                                                     slotap[:, r * 512:(r + 1) * 512],
                                                     start=(r == 0), stop=(r == 3)))(),
                                 waits=w, inc="t" if r == 3 else None)
                w1ring[slot] = ("t", tmk)
                wq_marks[k] = S.emit("vector", (lambda k=k: lambda e:
                                               e.tensor_copy(a16(WQST_O + k * 512, 512),
                                                             PS[2][:, 0:512]))(),
                                     waits=[("t", tmk)], inc="v")
            qs_evacs = [None] * 4
            for m in range(4):
                psum = PS[m % 2]
                tmk = None
                for k in range(NK):
                    for n in range(2):
                        w = [("v", wq_marks[k]), ("v", lnb_marks[k])]
                        if k == 0 and n == 0 and m >= 2:
                            w.append(("s", qs_evacs[m - 2]))
                        tmk = S.emit("tensor", (lambda m=m, k=k, n=n, psum=psum: lambda e:
                                                e.matmul(psum[:, n * 512:(n + 1) * 512],
                                                         a16(WQST_O + k * 512 + m * 128, 128),
                                                         lnc(k, n * 512, (n + 1) * 512),
                                                         start=(k == 0), stop=False))(),
                                     waits=w, inc=None)
                for n in range(2):
                    tmk = S.emit("tensor", (lambda m=m, n=n, psum=psum: lambda e:
                                            e.matmul(psum[:, n * 512:(n + 1) * 512],
                                                     bqs_ap[0:1, m * 128:(m + 1) * 128],
                                                     ones_row_bf[0:1, n * 512:(n + 1) * 512],
                                                     start=False, stop=True))(),
                                 inc="t" if n == 1 else None)
                qs_evacs[m] = S.emit("scalar", (lambda m=m, psum=psum: lambda e:
                                               e.copy(qs_ap(m), psum[:]))(),
                                     waits=[("t", tmk)], inc="s")
            wk_ring = [("g", 0)]
            k5_last = None
            for k in range(NK):
                slot = 0
                dst = a16(WKR_O, 512)
                wm = S.emit("gpsimd", (lambda dst=dst, k=k: lambda e:
                                       e.dma_start(dst, wkT_in[k]))(),
                            waits=[wk_ring[slot]], inc="w", n=16)
                for m in range(4):
                    for n in range(2):
                        w = [("w", wm), ("v", lnb_marks[k])]
                        if k == 0 and n == 0:
                            w.append(("s", qs_evacs[m] if m >= 2 else qs_evacs[3]))
                        k5_last = S.emit("tensor",
                                         (lambda k=k, m=m, n=n, dst=dst: lambda e:
                                          e.matmul(PS[m][:, n * 512:(n + 1) * 512],
                                                   dst[:, m * 128:(m + 1) * 128],
                                                   lnc(k, n * 512, (n + 1) * 512),
                                                   start=(k == 0), stop=False))(),
                                         waits=w, inc="t" if (m == 3 and n == 1) else None)
                wk_ring[slot] = ("t", k5_last)
            k5_evacs = [None] * 4
            for m in range(4):
                for n in range(2):
                    k5_last = S.emit("tensor", (lambda m=m, n=n: lambda e:
                                                e.matmul(PS[m][:, n * 512:(n + 1) * 512],
                                                         bk_ap[0:1, m * 128:(m + 1) * 128],
                                                         ones_row_bf[0:1, n * 512:(n + 1) * 512],
                                                         start=False, stop=True))(),
                                     inc="t" if n == 1 else None)
                k5_evacs[m] = S.emit("scalar", (lambda m=m, p=p: lambda e:
                                               e.copy(k5[:, m * T + p * P: m * T + (p + 1) * P],
                                                      PS[m][:]))(),
                                     waits=[("t", k5_last)], inc="s")
            wv_ring = [("g", 0)]
            v5_last = None
            for k in range(NK):
                slot = 0
                dst = a16(WVR_O, 512)
                wm = S.emit("gpsimd", (lambda dst=dst, k=k: lambda e:
                                       e.dma_start(dst, wvT_in[k]))(),
                            waits=[wv_ring[slot]], inc="w", n=16)
                for tt in range(8):
                    psv = PS[tt // 2][:, (tt % 2) * 512:(tt % 2 + 1) * 512]
                    w = [("w", wm), ("v", lnb_marks[k])]
                    if k == 0:
                        w.append(("s", k5_evacs[tt // 2]))
                    v5_last = S.emit("tensor", (lambda k=k, tt=tt, dst=dst, psv=psv: lambda e:
                                                e.matmul(psv, lnc(k, tt * 128, (tt + 1) * 128),
                                                         dst[:], start=(k == 0), stop=False))(),
                                     waits=w, inc="t" if tt == 7 else None)
                wv_ring[slot] = ("t", v5_last)
            v5_evacs = [None] * 8
            for tt in range(8):
                psv = PS[tt // 2][:, (tt % 2) * 512:(tt % 2 + 1) * 512]
                v5_last = S.emit("tensor", (lambda tt=tt, psv=psv: lambda e:
                                            e.matmul(psv, ones_row_bf[0:1, 0:128],
                                                     bv_ap[0:1, 0:512],
                                                     start=False, stop=True))(), inc="t")
                v5_evacs[tt] = S.emit("scalar", (lambda tt=tt, psv=psv, p=p: lambda e:
                                                 e.copy(v5[:, (p * 8 + tt) * QD:(p * 8 + tt + 1) * QD],
                                                        psv))(),
                                      waits=[("t", v5_last)], inc="s")

            pbuf = lnb[:, PB_O:PB_O + 2048]
            ptbuf = lnb[:, PT_O + 0:PT_O + 2048]
            def zsv(kd, lo, n):
                return lnb[:, ZS_O + kd * P + lo: ZS_O + kd * P + lo + n]
            def pjv(kd, lo, n):
                return lnb[:, PJ_O + kd * 2048 + lo: PJ_O + kd * 2048 + lo + n]

            pj_done = None
            for jc in range(4):
                for half in range(2):
                    tmk = None
                    for r in range(4):
                        slot, wm = dma_w1_block(projt_in[r * 4 + jc])
                        slotap = w1_slot_ap(slot)
                        for pc in range(2):
                            w = [("w", wm)] if pc == 0 else []
                            if r == 0 and pc == 0:
                                if jc == 0 and half == 0:
                                    w.append(("s", v5_evacs[7]))
                                if pj_done is not None:
                                    w.append(("v", pj_done))
                            tmk = S.emit("tensor",
                                         (lambda r=r, half=half, pc=pc, slotap=slotap: lambda e:
                                          e.matmul(PS[3][:, pc * 512:(pc + 1) * 512], identb[:],
                                                   slotap[:, half * 1024 + pc * 512:
                                                          half * 1024 + (pc + 1) * 512],
                                                   start=(r == 0), stop=(r == 3)))(),
                                         waits=w, inc="t" if (r == 3 and pc == 1) else None)
                        w1ring[slot] = ("t", tmk)
                    pj_done = S.emit("vector", (lambda jc=jc, half=half, tmk=tmk: lambda e:
                                               e.tensor_copy(pjv(jc, half * 1024, 1024),
                                                             PS[3][:]))(),
                                     waits=[("t", tmk)], inc="v")

            slot_last = {0: ("g", 0), 1: ("g", 0)}
            def tr_slot(sl):
                return (PS[2][:, 0:64].bitcast(BF16) if sl == 0
                        else PS[2][:, 512:576].bitcast(BF16))
            prev_exp = None
            prev_z_evac = None
            last_zs_mark = None
            for qt in range(8):
                kmax = p * P + (qt + 1) * 128
                ntile = kmax // 128
                nkc = (kmax + 511) // 512
                tmk = None
                for kd in range(4):
                    for kc in range(nkc):
                        width = min(512, kmax - kc * 512)
                        psv = PS[kc // 2][:, (kc % 2) * 512:(kc % 2) * 512 + width]
                        w = []
                        if kd == 0 and kc == 0:
                            w.append(("s", v5_evacs[7]))
                            if prev_exp is not None:
                                w.append(("s", prev_exp))
                        tmk = S.emit("tensor",
                                     (lambda kd=kd, kc=kc, width=width, psv=psv, qt=qt: lambda e:
                                      e.matmul(psv, qs_ap(kd, qt * 128, 128),
                                               k5[:, kd * T + kc * 512: kd * T + kc * 512 + width],
                                               start=(kd == 0), stop=False))(),
                                     waits=w, inc=None)
                doff = kmax - 128
                kc_d = doff // 512
                off_d = doff % 512
                psv_d = PS[kc_d // 2][:, (kc_d % 2) * 512 + off_d:(kc_d % 2) * 512 + off_d + 128]
                tmk = S.emit("tensor", (lambda psv_d=psv_d: lambda e:
                                        e.matmul(psv_d, identb[:], cmaskb[:],
                                                 start=False, stop=True))(), inc="t")
                w1_ = min(kmax, 1024)
                vm = S.emit("vector", (lambda w1_=w1_: lambda e:
                                       e.reduce_max(smx[:, 0:1], PS[0][:, 0:w1_], axis=AX.X))(),
                            waits=[("t", tmk)], inc="v")
                if kmax > 1024:
                    vm = S.emit("vector", (lambda kmax=kmax: lambda e:
                                           e.reduce_max(smx[:, 1:2], PS[1][:, 0:kmax - 1024],
                                                        axis=AX.X))(), inc="v")
                    vm = S.emit("vector", lambda e: e.tensor_max(smx[:, 0:1], smx[:, 0:1],
                                                                smx[:, 1:2]), inc="v")
                vm = S.emit("vector", lambda e: e.tensor_scalar_mul(smx[:, 2:3], smx[:, 0:1], -1.0),
                            inc="v")
                em = S.emit("scalar", (lambda w1_=w1_: lambda e:
                                       e.activation(pbuf[:, 0:w1_], PS[0][:, 0:w1_], AF.Exp,
                                                    bias=smx[:, 2:3], accum_out=smx[:, 3:4]))(),
                            waits=[("v", vm)], inc="s")
                ell_col = 3
                if kmax > 1024:
                    em = S.emit("scalar", (lambda kmax=kmax: lambda e:
                                           e.activation(pbuf[:, 1024:kmax], PS[1][:, 0:kmax - 1024],
                                                        AF.Exp, bias=smx[:, 2:3],
                                                        accum_out=smx[:, 4:5]))(), inc="s")
                    S.emit("vector", lambda e: e.tensor_add(smx[:, 3:4], smx[:, 3:4], smx[:, 4:5]),
                           waits=[("s", em)], inc="v")
                prev_exp = em
                vm = S.emit("vector", lambda e: e.tensor_scalar_mul(smx[:, 5:6], smx[:, 3:4],
                                                                   SQRT_HS),
                            waits=[("s", em)], inc="v")
                rec_mark = S.emit("vector", lambda e: e.reciprocal(smx[:, 6:7], smx[:, 5:6]),
                                  inc="v")
                pt_evacs = [None] * ntile
                for i in range(ntile):
                    sl = i % 2
                    w = [("s", em), slot_last[sl]]
                    tmk = S.emit("tensor", (lambda i=i, sl=sl: lambda e:
                                            e.transpose(tr_slot(sl), pbuf[:, i * 128:(i + 1) * 128],
                                                        identb[:]))(),
                                 waits=w, inc="t")
                    pt_evacs[i] = S.emit("vector", (lambda i=i, sl=sl: lambda e:
                                                    e.tensor_copy(ptbuf[:, i * 128:(i + 1) * 128],
                                                                  tr_slot(sl)))(),
                                         waits=[("t", tmk)], inc="v")
                    slot_last[sl] = ("v", pt_evacs[i])
                tmk = None
                for i in range(ntile):
                    w = [("v", pt_evacs[i])]
                    if i == 0:
                        w.append(("s", v5_evacs[7]))
                        w.append(("v", pj_done))
                        if prev_z_evac is not None:
                            w.append(("v", prev_z_evac))
                    tmk = S.emit("tensor", (lambda i=i, ntile=ntile: lambda e:
                                            e.matmul(PS[3][:, 0:512],
                                                     ptbuf[:, i * 128:(i + 1) * 128],
                                                     v5[:, i * QD:(i + 1) * QD],
                                                     start=(i == 0), stop=(i == ntile - 1)))(),
                                 waits=w, inc="t" if i == ntile - 1 else None)
                prev_z_evac = S.emit("vector", lambda e: e.tensor_scalar_mul(A16[:, ZR_O:ZR_O + QD],
                                                                            PS[3][:, 0:512],
                                                                            smx[:, 6:7]),
                                     waits=[("t", tmk), ("v", rec_mark)], inc="v")
                for kd in range(4):
                    sl = kd % 2
                    w = [("v", prev_z_evac), slot_last[sl]]
                    tmk = S.emit("tensor", (lambda kd=kd, sl=sl: lambda e:
                                            e.transpose(tr_slot(sl), A16[:, ZR_O + kd * 128:ZR_O + (kd + 1) * 128],
                                                        identb[:]))(),
                                 waits=w, inc="t")
                    last_zs_mark = S.emit("vector", (lambda kd=kd, sl=sl, qt=qt: lambda e:
                                                     e.tensor_copy(zsv(kd, qt * 128, 128),
                                                                   tr_slot(sl)))(),
                                          waits=[("t", tmk)], inc="v")
                    slot_last[sl] = ("v", last_zs_mark)
            prev_add = [None, None]
            for m in range(NM2):
                psum = PS[m % 2]
                tmk = None
                for kd in range(4):
                    for n in range(2):
                        w = []
                        if kd == 0 and n == 0:
                            w.append(("v", last_zs_mark))
                            w.append(("s", prev_exp))
                            if prev_add[m % 2] is not None:
                                w.append(("v", prev_add[m % 2]))
                        tmk = S.emit("tensor", (lambda m=m, kd=kd, n=n, psum=psum: lambda e:
                                                e.matmul(psum[:, n * 512:(n + 1) * 512],
                                                         pjv(kd, m * 128, 128),
                                                         zsv(kd, n * 512, 512),
                                                         start=(kd == 0), stop=(kd == 3)))(),
                                     waits=w, inc="t" if (kd == 3 and n == 1) else None)
                prev_add[m % 2] = S.emit("vector", (lambda m=m, psum=psum: lambda e:
                                                    e.tensor_add(xc(m), xc(m), psum[:]))(),
                                         waits=[("t", tmk)], inc="v")

        def emit_out(p):
            slot_v = [("g", 0), ("g", 0)]
            dma_marks = {}
            for tt in range(8):
                outv = a32(OUT_O + (tt % 2) * 2048, 2048)
                vm = None
                for c in range(NK):
                    sl = c % 2
                    psv = PS[sl][:, 0:128]
                    w = [slot_v[sl]]
                    if c < 2 and tt >= 2:
                        w.append(("y", dma_marks[tt - 2]))
                    tmk = S.emit("tensor", (lambda c=c, tt=tt, psv=psv: lambda e:
                                            e.transpose(psv,
                                                        x_sb[:, c * P + tt * 128: c * P + (tt + 1) * 128],
                                                        identf[:]))(),
                                 waits=w, inc="t")
                    vm = S.emit("vector", (lambda c=c, psv=psv, outv=outv: lambda e:
                                           e.tensor_copy(outv[:, c * 128:(c + 1) * 128], psv))(),
                                waits=[("t", tmk)], inc="v")
                    slot_v[sl] = ("v", vm)
                dma_marks[tt] = S.emit("sync", (lambda tt=tt, p=p, outv=outv: lambda e:
                                                e.dma_start(yout[p * P + tt * 128:
                                                                 p * P + (tt + 1) * 128, :], outv))(),
                                       waits=[("v", vm)], inc="y", n=16)

        def emit_xload(p):
            marks = []
            for c in range(NK):
                m = S.emit("sync", (lambda c=c, p=p: lambda e:
                                    e.dma_start(x_sb[:, c * P:(c + 1) * P], xin[p, c]))(),
                           inc="y", n=16)
                marks.append(("y", m))
            return marks

        x_marks0 = emit_xload(0)
        prelude()
        S.barrier()
        npass = NPASS if stage == "full" else 1
        for p in range(npass):
            x_marks = x_marks0 if p == 0 else emit_xload(p)
            if p == 0:
                tap("xload", lambda: x_sb[:])
            gate_tm = emit_gate_mms([[xm] for xm in x_marks])
            vb = emit_ln(x_marks)
            if stage == "ln1":
                lm = emit_ln_apply()
                tap("lnb", lambda: lnb[:])
                tap("rb", lambda: rb[:])
                tap("nb", lambda: nb[:])
                break
            if stage == "gate":
                emit_gate(vb, gate_tm)
                lm = emit_ln_apply()
                tap("lnb", lambda: lnb[:])
                tap("rb", lambda: rb[:])
                tap("nb", lambda: nb[:])
                tap("cw", lambda: cw_tm[:])
                tap("glog", lambda: gsc[:])
                tap("ltm", lambda: cwsc[:, 0:64])
                tap("ee", lambda: cwsc[:, 64:128])
                tap("inds", lambda: cwsc[:, 128:192])
                tap("m1m2", lambda: cwsc[:, 192:216])
                tap("gws", lambda: gwsum[:])
                break
            emit_ffn(vb, gate_tm)
            if p == 0:
                tap("cw", lambda: cw_tm[:])
                tap("x1", lambda: x_sb[:])
            if stage == "ffn1":
                break
            S.barrier()
            vb = emit_ln(("y", S.c["y"]))
            lm = emit_ln_apply()
            emit_attn(p, lm, vb)
            if p == 0:
                tap("qs", lambda: A16[:, QS_O:QS_O + 3 * P])
                tap("k5", lambda: k5[:])
                tap("v5", lambda: v5[:])
                tap("zs", lambda: lnb[:, ZS_O:ZS_O + 4 * P])
                tap("x2", lambda: x_sb[:])
            if stage == "attn":
                tap("wqst", lambda: A16[:, WQST_O:WQST_O + 16 * 512])
                tap("bqs", lambda: bias_bf[0:1, 0:1536])
                break
            S.barrier()
            gate_tm = emit_gate_mms([[] for _ in range(NK)])
            vb = emit_ln(("y", S.c["y"]))
            emit_ffn(vb, gate_tm)
            S.barrier()
            emit_out(p)
            S.barrier()
        S.barrier()
        S.emit("sync", None, waits=[("y", S.c["y"])])

        @block.tensor
        def _(e):
            S.replay("tensor", e, sems)

        @block.vector
        def _(e):
            S.replay("vector", e, sems)

        @block.scalar
        def _(e):
            S.replay("scalar", e, sems)

        @block.gpsimd
        def _(e):
            S.replay("gpsimd", e, sems)

        @block.sync
        def _(e):
            S.replay("sync", e, sems)

    return nc


def prep_inputs(L0_attn_x, L0_attn_WQ, L0_attn_bQ, L0_attn_WK, L0_attn_bK,
                L0_attn_WV, L0_attn_bV, L0_attn_proj, L0_ffn_gate_w,
                L0_ffn_w1, L0_ffn_w2, L0_ffn_sw1, L0_ffn_sw2, L0_ffn_sw3):
    C = np.ascontiguousarray
    f = np.float32
    w1 = np.asarray(L0_ffn_w1, f)
    w2 = np.asarray(L0_ffn_w2, f)
    shared = {
        "w1t": C(w1.reshape(NE, NK, 128, NM1, 128).transpose(0, 3, 2, 1, 4)
                 .reshape(NE, NM1, 128, NK * 128)),
        "w2t": C(w2.reshape(NE, NJ, 128, NM2, 128).transpose(0, 3, 2, 1, 4)
                 .reshape(NE, NM2, 128, NJ * 128)),
        "sw1t": C(np.asarray(L0_ffn_sw1, f).reshape(NJ, 128, NK, 128)
                  .transpose(0, 3, 2, 1).reshape(NJ, 128, NK * 128)),
        "sw2t": C(np.asarray(L0_ffn_sw2, f).reshape(NJ, 128, NK, 128)
                  .transpose(0, 3, 2, 1).reshape(NJ, 128, NK * 128)),
        "sw3t": C(np.asarray(L0_ffn_sw3, f).reshape(NM2, 128, NJ, 128)
                  .transpose(0, 3, 2, 1).reshape(NM2, 128, NJ * 128)),
        "gwt_in": C(np.asarray(L0_ffn_gate_w, f).T.reshape(NK, 128, NE)
                    .transpose(1, 0, 2).reshape(128, NK * NE)),
        "wqT_in": C(np.asarray(L0_attn_WQ, f).T.reshape(NK, 128, D)),
        "wkT_in": C(np.asarray(L0_attn_WK, f).T.reshape(NK, 128, QD)),
        "wvT_in": C(np.asarray(L0_attn_WV, f).T.reshape(NK, 128, QD)),
        "projt_in": C(np.asarray(L0_attn_proj, f).reshape(NK, 128, D)),
        "bq_in": C(np.asarray(L0_attn_bQ, f).reshape(1, D)),
        "bk_in": C(np.asarray(L0_attn_bK, f).reshape(1, QD)),
        "bv_in": C(np.asarray(L0_attn_bV, f).reshape(1, QD)),
        "cmask_in": C((np.triu(np.ones((128, 128), f), 1) * NEG).astype(f)),
        "ident_in": np.eye(128, dtype=f),
    }
    x = np.asarray(L0_attn_x, f)
    in_maps = []
    for b in range(NCORE):
        xT = C(x[b].T)
        xr = C(xT.reshape(NK, 128, NPASS, P).transpose(2, 0, 1, 3))
        m = dict(shared)
        m["xin"] = xr
        in_maps.append(m)
    return in_maps


def kernel(**inputs):
    in_maps = prep_inputs(**inputs)
    nc = build_nc()
    res = run_bass_kernel_spmd(nc, in_maps, core_ids=list(range(NCORE)))
    out = np.stack([res.results[i]["yout"] for i in range(NCORE)], axis=0)
    return out


# version 05
# speedup vs baseline: 1.0198x; 1.0198x over previous
import numpy as np
import concourse.bass as bass
import concourse.mybir as mybir
from concourse.bass_utils import run_bass_kernel_spmd

F32 = mybir.dt.float32
BF16 = mybir.dt.bfloat16
AF = mybir.ActivationFunctionType
ALU = mybir.AluOpType
AX = mybir.AxisListType

NCORE = 8
D = 2048
T = 2048
P = 1024
NPASS = 2
NK = 16
NE = 8
NJ = 11
NM1 = 22
NM2 = 16
QD = 512
SQRT_HS = float(np.sqrt(np.float32(128.0)))
EPS = 1e-5
NEG = -1.0e30

ENGINES = ("tensor", "vector", "scalar", "gpsimd", "sync")
SEMS = ("t", "v", "s", "w", "y", "g")


class Sched:
    def __init__(self):
        self.items = {e: [] for e in ENGINES}
        self.c = {s: 0 for s in SEMS}
        self._waited = {e: {s: 0 for s in SEMS} for e in ENGINES}

    def emit(self, eng, fn, waits=(), inc=None, n=1):
        if eng == "vector":
            waits = list(waits) + [("v", self.c["v"])]
        elif eng == "scalar":
            waits = list(waits) + [("s", self.c["s"])]
        wl = []
        for s, v in waits:
            if v is None or v <= 0:
                continue
            if v > self._waited[eng][s]:
                self._waited[eng][s] = v
                wl.append((s, v))
        if inc is not None:
            self.c[inc] += n
        self.items[eng].append((fn, tuple(wl), inc, n))
        return self.c[inc] if inc is not None else None

    def barrier(self):
        snap = dict(self.c)
        for e in ENGINES:
            w = [(s, snap[s]) for s in SEMS if snap[s] > self._waited[e][s]]
            if w:
                self.emit(e, None, waits=w)

    def check_deadlock(self):
        import collections
        pos = {e: 0 for e in ENGINES}
        cnt = {s: 0 for s in SEMS}
        n = sum(len(v) for v in self.items.values())
        done = 0
        while done < n:
            progressed = False
            for e in ENGINES:
                while pos[e] < len(self.items[e]):
                    fn, waits, inc, k = self.items[e][pos[e]]
                    if all(cnt[s] >= v for s, v in waits):
                        if inc is not None:
                            cnt[inc] += k
                        pos[e] += 1
                        done += 1
                        progressed = True
                    else:
                        break
            if not progressed:
                for e in ENGINES:
                    if pos[e] < len(self.items[e]):
                        fn, waits, inc, k = self.items[e][pos[e]]
                        print(f"STUCK {e} at item {pos[e]}: waits {waits}, have "
                              f"{[(s, cnt[s]) for s, v in waits]}")
                raise RuntimeError("schedule deadlock")
        return True

    def replay(self, eng_name, eng, sems):
        for fn, waits, inc, n in self.items[eng_name]:
            for s, v in waits:
                eng.wait_ge(sems[s], v)
            if fn is None:
                continue
            inst = fn(eng)
            if inc is not None:
                inst.then_inc(sems[inc], n)


def build_nc(taps=(), stage="full"):
    taps = set(taps)
    nc = bass.Bass("TRN2", target_bir_lowering=False)

    xin = nc.dram_tensor("xin", [NPASS, NK, 128, P], F32, kind="ExternalInput").ap()
    w1t = nc.dram_tensor("w1t", [NE, NM1, 128, NK * 128], F32, kind="ExternalInput").ap()
    w2t = nc.dram_tensor("w2t", [NE, NM2, 128, NJ * 128], F32, kind="ExternalInput").ap()
    sw1t = nc.dram_tensor("sw1t", [NJ, 128, NK * 128], F32, kind="ExternalInput").ap()
    sw2t = nc.dram_tensor("sw2t", [NJ, 128, NK * 128], F32, kind="ExternalInput").ap()
    sw3t = nc.dram_tensor("sw3t", [NM2, 128, NJ * 128], F32, kind="ExternalInput").ap()
    gwt_in = nc.dram_tensor("gwt_in", [128, NK * NE], F32, kind="ExternalInput").ap()
    wqT_in = nc.dram_tensor("wqT_in", [NK, 128, D], F32, kind="ExternalInput").ap()
    wkT_in = nc.dram_tensor("wkT_in", [NK, 128, QD], F32, kind="ExternalInput").ap()
    wvT_in = nc.dram_tensor("wvT_in", [NK, 128, QD], F32, kind="ExternalInput").ap()
    projt_in = nc.dram_tensor("projt_in", [NK, 128, D], F32, kind="ExternalInput").ap()
    bq_in = nc.dram_tensor("bq_in", [1, D], F32, kind="ExternalInput").ap()
    bk_in = nc.dram_tensor("bk_in", [1, QD], F32, kind="ExternalInput").ap()
    bv_in = nc.dram_tensor("bv_in", [1, QD], F32, kind="ExternalInput").ap()
    cmask_in = nc.dram_tensor("cmask_in", [128, 128], F32, kind="ExternalInput").ap()
    ident_in = nc.dram_tensor("ident_in", [128, 128], F32, kind="ExternalInput").ap()
    yout = nc.dram_tensor("yout", [T, D], F32, kind="ExternalOutput").ap()

    tap_drams = {}
    def tapd(name, shape):
        if name in taps:
            tap_drams[name] = nc.dram_tensor("tap_" + name, list(shape), F32,
                                             kind="ExternalOutput").ap()
    tapd("lnb", (128, NK * P)); tapd("rb", (128, P)); tapd("nb", (128, P))
    tapd("cw", (128, 64)); tapd("act", (128, NJ * P)); tapd("x1", (128, NK * P))
    tapd("x2", (128, NK * P)); tapd("qs", (128, 3 * P)); tapd("k5", (128, 4 * T))
    tapd("v5", (128, 16 * QD)); tapd("zs", (128, 4 * P)); tapd("xload", (128, NK * P))
    tapd("glog", (128, P)); tapd("ltm", (128, 64)); tapd("ee", (128, 64))
    tapd("wqst", (128, 16 * 512)); tapd("bqs", (1, 1536))
    tapd("inds", (128, 64)); tapd("m1m2", (128, 24)); tapd("gws", (128, 1))

    from contextlib import ExitStack
    with ExitStack() as stack:
        E = stack.enter_context

        x_sb = E(nc.sbuf_tensor("x_sb", [128, NK * P], F32))
        lnb = E(nc.sbuf_tensor("lnb", [128, NK * P], BF16))
        k5 = E(nc.sbuf_tensor("k5", [128, 4 * T], BF16))
        v5 = E(nc.sbuf_tensor("v5", [128, 16 * QD], BF16))
        rb = E(nc.sbuf_tensor("rb", [128, P], F32))
        nb = E(nc.sbuf_tensor("nb", [128, P], F32))
        xb = E(nc.sbuf_tensor("xb", [128, 2 * P], BF16))
        gsc = E(nc.sbuf_tensor("gsc", [128, P], F32))
        cw_tm = E(nc.sbuf_tensor("cw_tm", [128, 64], F32))
        gwt = E(nc.sbuf_tensor("gwt", [128, NK * NE], F32))
        gwsum = E(nc.sbuf_tensor("gwsum", [128, 1], F32))
        identb = E(nc.sbuf_tensor("identb", [128, 128], BF16))
        identf = E(nc.sbuf_tensor("identf", [128, 128], F32))
        cmaskb = E(nc.sbuf_tensor("cmaskb", [128, 128], BF16))
        bias_bf = E(nc.sbuf_tensor("bias_bf", [1, 1536], BF16))
        ones_row_bf = E(nc.sbuf_tensor("ones_row_bf", [1, P], BF16))
        ones_col_bf = E(nc.sbuf_tensor("ones_col_bf", [128, 1], BF16))
        ones1_f = E(nc.sbuf_tensor("ones1_f", [1, 128], F32))
        ones128_f = E(nc.sbuf_tensor("ones128_f", [128, 1], F32))
        eps_col = E(nc.sbuf_tensor("eps_col", [128, 1], F32))
        smx = E(nc.sbuf_tensor("smx", [128, 8], F32))

        A16 = E(nc.sbuf_tensor("A16", [128, 20224], BF16))
        A32 = E(nc.sbuf_tensor("A32", [128, 4096], F32))
        cwsc = A32[:, 0:224]
        bqstage = A16[0:1, 0:2048]

        PS = [E(nc.psum_tensor(f"PS{i}", [128, 1024], F32)) for i in range(4)]
        sems = {s: E(nc.semaphore("sem_" + s)) for s in SEMS}
        block = E(nc.Block())
        S = Sched()

        ACT_O = 0
        W1R_O = 11264
        W1R_SLOTS = 3
        W2R_O = W1R_O + W1R_SLOTS * 2048
        W2R_SLOTS = 2
        WKR_O = W2R_O
        WVR_O = W2R_O + 512
        QS3_O = W2R_O + 1024
        ZR_O = W2R_O + 2048
        WQST_O = 0
        QS_O = 8192
        PB_O = 0
        PT_O = 2048
        ZS_O = 4096
        PJ_O = 8192
        SG_O = 0
        CWB_O = 2048
        TMP_O = 0
        OUT_O = 0

        def a16(off, n):
            return A16[:, off:off + n]

        def qs_ap(m, lo=0, n_=P):
            off = (QS_O + m * P) if m < 3 else QS3_O
            return A16[:, off + lo: off + lo + n_]

        def a32(off, n):
            return A32[:, off:off + n]

        def xc(c, lo=0, hi=P):
            return x_sb[:, c * P + lo: c * P + hi]

        def lnc(c, lo=0, hi=P):
            return lnb[:, c * P + lo: c * P + hi]

        bqs_ap = bias_bf[0:1, 0:512]
        bk_ap = bias_bf[0:1, 512:1024]
        bv_ap = bias_bf[0:1, 1024:1536]

        def prelude():
            S.emit("gpsimd", lambda e: e.memset(ones_row_bf[:], 1.0), inc="g")
            S.emit("gpsimd", lambda e: e.memset(ones_col_bf[:], 1.0), inc="g")
            S.emit("gpsimd", lambda e: e.memset(ones1_f[:], 1.0), inc="g")
            S.emit("gpsimd", lambda e: e.memset(ones128_f[:], 1.0), inc="g")
            S.emit("gpsimd", lambda e: e.memset(eps_col[:], EPS), inc="g")
            S.emit("gpsimd", lambda e: e.dma_start(identb[:], ident_in), inc="w", n=16)
            S.emit("gpsimd", lambda e: e.dma_start(identf[:], ident_in), inc="w", n=16)
            S.emit("gpsimd", lambda e: e.dma_start(cmaskb[:], cmask_in), inc="w", n=16)
            S.emit("gpsimd", lambda e: e.dma_start(gwt[:], gwt_in), inc="w", n=16)
            S.emit("gpsimd", lambda e: e.dma_start(bqstage, bq_in), inc="w", n=16)
            S.emit("gpsimd", lambda e: e.dma_start(bias_bf[0:1, 512:1024], bk_in), inc="w", n=16)
            w_mark = S.emit("gpsimd", lambda e: e.dma_start(bias_bf[0:1, 1024:1536], bv_in),
                            inc="w", n=16)
            tm = None
            for r in range(4):
                tm = S.emit("tensor", (lambda r=r: lambda e:
                                       e.matmul(PS[3][0:1, 0:512], ones_row_bf[0:1, 0:1],
                                                bqstage[0:1, r * 512:(r + 1) * 512],
                                                start=(r == 0), stop=(r == 3)))(),
                            waits=[("w", w_mark)], inc="t")
            S.emit("vector", lambda e: e.tensor_copy(bias_bf[0:1, 0:512], PS[3][0:1, 0:512]),
                   waits=[("t", tm)], inc="v")
            tm2 = None
            for c in range(NK):
                tm2 = S.emit("tensor", (lambda c=c: lambda e:
                                        e.matmul(PS[2][0:8, 0:1], gwt[:, c * NE:(c + 1) * NE],
                                                 ones128_f[:], start=(c == 0),
                                                 stop=(c == NK - 1)))(),
                             waits=[("w", w_mark)], inc="t")
            S.emit("vector", lambda e: e.tensor_copy(gwsum[0:8, 0:1], PS[2][0:8, 0:1]),
                   waits=[("t", tm2)], inc="v")

        def tap(name, src_ap_fn):
            if name not in tap_drams:
                return
            dst = tap_drams[name]
            S.barrier()
            S.emit("gpsimd", (lambda dst=dst, f=src_ap_fn: lambda e: e.dma_start(dst, f()))(),
                   inc="w", n=16)
            S.barrier()

        def emit_ln(x_ready):
            s_marks = [None] * NK
            t_marks = [None] * NK
            for c in range(NK):
                w = [x_ready[c] if isinstance(x_ready, list) else x_ready]
                if c >= 2 and t_marks[c - 2] is not None:
                    w.append(("t", t_marks[c - 2]))
                xbv = xb[:, (c % 2) * P:(c % 2 + 1) * P]
                cast_m = S.emit("scalar", (lambda xbv=xbv, c=c: lambda e: e.copy(xbv, xc(c)))(),
                                waits=w, inc="s")
                tsum = None
                for n in range(2):
                    tsum = S.emit("tensor", (lambda xbv=xbv, n=n, c=c: lambda e:
                                             e.matmul(PS[3][0:1, n * 512:(n + 1) * 512],
                                                      ones_col_bf[:],
                                                      xbv[:, n * 512:(n + 1) * 512],
                                                      start=(c == 0), stop=(c == NK - 1)))(),
                                  waits=[("s", cast_m)], inc="t" if n == 1 else None)
                s_marks[c] = S.emit("scalar", (lambda xbv=xbv: lambda e:
                                               e.activation(xbv, xbv, AF.Square))(),
                                    waits=[("t", tsum)], inc="s")
                for n in range(2):
                    t_marks[c] = S.emit("tensor", (lambda xbv=xbv, n=n, c=c: lambda e:
                                                   e.matmul(PS[2][0:1, n * 512:(n + 1) * 512],
                                                            ones_col_bf[:],
                                                            xbv[:, n * 512:(n + 1) * 512],
                                                            start=(c == 0), stop=(c == NK - 1)))(),
                                        waits=[("s", s_marks[c])], inc="t" if n == 1 else None)
            tm_stats = t_marks[NK - 1]
            S1 = A32[0:1, 0:P]
            S2 = A32[0:1, P:2 * P]
            S3 = A32[0:1, 2 * P:3 * P]
            S4 = A32[0:1, 3 * P:4 * P]
            vm_sums = S.emit("vector", lambda e: e.tensor_copy(S1, PS[3][0:1, 0:P]),
                             waits=[("t", tm_stats)], inc="v")
            vm_sumsq = S.emit("vector", lambda e: e.tensor_copy(S2, PS[2][0:1, 0:P]), inc="v")
            vm_m = S.emit("vector", lambda e: e.tensor_scalar_mul(S3, S1, 1.0 / D), inc="v")
            sm = S.emit("scalar", lambda e: e.activation(S4, S3, AF.Square),
                        waits=[("v", vm_m)], inc="s")
            vm_var = S.emit("vector", lambda e: e.scalar_tensor_tensor(S2, S2, 1.0 / D, S4,
                                                                      ALU.mult, ALU.subtract),
                            waits=[("s", sm)], inc="v")
            sm = S.emit("scalar", lambda e: e.activation(S4, S2, AF.Ln, bias=eps_col[0:1, 0:1]),
                        waits=[("v", vm_var)], inc="s")
            sm_r = S.emit("scalar", lambda e: e.activation(S1, S4, AF.Exp, scale=-0.5),
                          waits=[("v", vm_m)], inc="s")
            vm_nmr = S.emit("vector", lambda e: e.scalar_tensor_tensor(S3, S3, -1.0, S1,
                                                                      ALU.mult, ALU.mult),
                            waits=[("s", sm_r)], inc="v")
            for n in range(2):
                S.emit("tensor", (lambda n=n, S1=S1: lambda e:
                                  e.matmul(PS[1][:, n * 512:(n + 1) * 512], ones1_f[:],
                                           S1[0:1, n * 512:(n + 1) * 512],
                                           start=True, stop=True))(),
                       waits=[("s", sm_r)], inc=None)
            tm = None
            for n in range(2):
                tm = S.emit("tensor", (lambda n=n, S3=S3: lambda e:
                                       e.matmul(PS[2][:, n * 512:(n + 1) * 512], ones1_f[:],
                                                S3[0:1, n * 512:(n + 1) * 512],
                                                start=True, stop=True))(),
                            waits=[("v", vm_nmr), ("v", vm_sumsq)], inc="t" if n == 1 else None)
            S.emit("vector", lambda e: e.tensor_copy(rb[:], PS[1][:]), waits=[("t", tm)], inc="v")
            vb = S.emit("vector", lambda e: e.tensor_copy(nb[:], PS[2][:]), inc="v")
            return vb

        def emit_ln_apply():
            lnb_marks = [None] * NK
            tmpv = a32(TMP_O, P)
            for c in range(NK):
                S.emit("vector", (lambda c=c, tmpv=tmpv: lambda e:
                                  e.tensor_mul(tmpv, xc(c), rb[:]))(), inc="v")
                lnb_marks[c] = S.emit("vector", (lambda c=c, tmpv=tmpv: lambda e:
                                                 e.tensor_add(lnc(c), tmpv, nb[:]))(), inc="v")
            return lnb_marks

        w1ring = [("g", 0)] * W1R_SLOTS
        w2ring = [("g", 0)] * W2R_SLOTS
        ring_idx = {"w1": 0, "w2": 0}

        def w1_slot_ap(i):
            return a16(W1R_O + i * 2048, 2048)

        def w2_slot_ap(i):
            return a16(W2R_O + i * 1408, 1408)

        def dma_w1_block(src_ap):
            i = ring_idx["w1"] % W1R_SLOTS
            ring_idx["w1"] += 1
            dst = w1_slot_ap(i)
            wm = S.emit("gpsimd", (lambda dst=dst, src=src_ap: lambda e:
                                   e.dma_start(dst, src))(),
                        waits=[w1ring[i]], inc="w", n=16)
            return i, wm

        def dma_w2_block(src_ap):
            i = ring_idx["w2"] % W2R_SLOTS
            ring_idx["w2"] += 1
            dst = w2_slot_ap(i)
            wm = S.emit("gpsimd", (lambda dst=dst, src=src_ap: lambda e:
                                   e.dma_start(dst, src))(),
                        waits=[w2ring[i]], inc="w", n=16)
            return i, wm

        def emit_gate_mms(x_waits):
            tm = None
            for c in range(NK):
                for n in range(2):
                    w = list(x_waits[c]) if n == 0 else []
                    last = (c == NK - 1 and n == 1)
                    tm = S.emit("tensor", (lambda c=c, n=n: lambda e:
                                           e.matmul(PS[0][0:8, n * 512:(n + 1) * 512],
                                                    gwt[:, c * NE:(c + 1) * NE],
                                                    xc(c, n * 512, (n + 1) * 512),
                                                    start=(c == 0), stop=(c == NK - 1)))(),
                                waits=w, inc="t" if last else None)
            return tm

        def emit_gate(lnb_vb, gate_tm):
            g0 = gsc[0:8, 0:P]
            g1 = A32[0:8, 0:P]
            S.emit("vector", lambda e: e.tensor_mul(A32[0:8, P:2 * P], PS[0][0:8, 0:P], rb[0:8, 0:P]),
                   waits=[("t", gate_tm)], inc="v")
            S.emit("vector", lambda e: e.tensor_scalar_mul(g1, nb[0:8, 0:P], gwsum[0:8, 0:1]),
                   inc="v")
            t1_mark = S.emit("vector", lambda e: e.tensor_add(g0, A32[0:8, P:2 * P], g1), inc="v")
            vm = t1_mark
            tm = None
            for tt in range(8):
                tm = S.emit("tensor", (lambda tt=tt: lambda e:
                                       e.transpose(PS[2][:, tt * 8:(tt + 1) * 8],
                                                   g0[0:8, tt * 128:(tt + 1) * 128],
                                                   identf[0:8, 0:8]))(),
                            waits=[("v", vm)], inc="t" if tt == 7 else None)
            L = cwsc[:, 0:64]
            E_ = cwsc[:, 64:128]
            SCR = cwsc[:, 128:192]
            mA = smx[:, 0:8]
            mB = cwsc[:, 192:200]
            mC = cwsc[:, 200:208]
            vm = S.emit("vector", lambda e: e.tensor_copy(L, PS[2][:, 0:64]),
                        waits=[("t", tm)], inc="v")
            L3 = L.rearrange("p (a b) -> p a b", b=8)
            E3 = E_.rearrange("p (a b) -> p a b", b=8)
            S3 = SCR.rearrange("p (a b) -> p a b", b=8)
            mA3 = mA.rearrange("p (a one) -> p a one", one=1)
            mB3 = mB.rearrange("p (a one) -> p a one", one=1)
            mC3 = mC.rearrange("p (a one) -> p a one", one=1)
            def bcast(small3, big3):
                b1, _ = bass.broadcast_tensor_aps(small3, big3)
                return b1
            S.emit("vector", lambda e: e.reduce_max(mA, L3, axis=AX.X, negate=True), inc="v")
            vm = S.emit("vector", lambda e: e.tensor_tensor(L3, L3, bcast(mA3, L3), ALU.add),
                        inc="v")
            sm = S.emit("scalar", lambda e: e.activation(E_, L, AF.Exp), waits=[("v", vm)], inc="s")
            S.emit("vector", lambda e: e.reduce_max(mA, E3, axis=AX.X), waits=[("s", sm)], inc="v")
            S.emit("vector", lambda e: e.tensor_tensor(S3, E3, bcast(mA3, E3), ALU.is_ge), inc="v")
            S.emit("vector", lambda e: e.tensor_scalar(SCR, SCR, -1.0, 1.0, ALU.mult, ALU.add),
                   inc="v")
            S.emit("vector", lambda e: e.tensor_tensor(SCR, E_, SCR, ALU.mult), inc="v")
            S.emit("vector", lambda e: e.reduce_max(mB, S3, axis=AX.X), inc="v")
            S.emit("vector", lambda e: e.tensor_tensor(S3, E3, bcast(mB3, E3), ALU.is_ge), inc="v")
            S.emit("vector", lambda e: e.scalar_tensor_tensor(mC, mA, 1e-20, mB, ALU.add, ALU.add),
                   inc="v")
            mD = cwsc[:, 208:216]
            mD3 = mD.rearrange("p (a one) -> p a one", one=1)
            S.emit("vector", lambda e: e.reciprocal(mD, mC), inc="v")
            S.emit("vector", lambda e: e.tensor_tensor(SCR, E_, SCR, ALU.mult), inc="v")
            vm = S.emit("vector", lambda e: e.tensor_tensor(
                cw_tm[:].rearrange("p (a b) -> p a b", b=8), S3, bcast(mD3, S3), ALU.mult), inc="v")
            return vm

        def emit_cwb_prep(e_idx, wait_marks):
            tm = None
            for tt in range(8):
                col = tt * 8 + e_idx
                w = list(wait_marks) if tt == 0 else []
                tm = S.emit("tensor", (lambda col=col, tt=tt: lambda e:
                                       e.transpose(PS[3][0:1, tt * 128:(tt + 1) * 128],
                                                   cw_tm[:, col:col + 1], identf[:]))(),
                            waits=w, inc="t" if tt == 7 else None)
            cwrow = gsc[0:1, 0:P]
            vm = S.emit("vector", (lambda cwrow=cwrow: lambda e:
                                   e.tensor_copy(cwrow, PS[3][0:1, 0:P]))(),
                        waits=[("t", tm)], inc="v")
            tm = None
            for n in range(2):
                tm = S.emit("tensor", (lambda n=n, cwrow=cwrow: lambda e:
                                       e.matmul(PS[3][:, n * 512:(n + 1) * 512], ones1_f[:],
                                                cwrow[0:1, n * 512:(n + 1) * 512],
                                                start=True, stop=True))(),
                            waits=[("v", vm)], inc="t" if n == 1 else None)
            dstv = a32(CWB_O + (e_idx % 2) * P, P)
            vm = S.emit("vector", (lambda dstv=dstv: lambda e:
                                   e.tensor_copy(dstv, PS[3][:]))(),
                        waits=[("t", tm)], inc="v")
            return vm

        def emit_expert_w1(eidx, lnb_marks, cwb_mark, psum_free, shared=False):
            act_marks = [None] * NJ
            last_sg = None
            last_tt1 = None
            tmk = None
            for j in range(NJ):
                if shared:
                    blocks = [(sw1t[j], 0), (sw2t[j], 1)]
                else:
                    blocks = [(w1t[eidx, j], 0), (w1t[eidx, NJ + j], 1)]
                for src, gu in blocks:
                    slot, wm = dma_w1_block(src)
                    slotap = w1_slot_ap(slot)
                    first = True
                    for k in range(NK):
                        for n in range(2):
                            w = [("v", lnb_marks[k])]
                            if first:
                                w.append(("w", wm))
                                if j == 0 and psum_free.get(gu):
                                    w.append(psum_free[gu])
                                if gu == 0 and last_sg is not None:
                                    w.append(("s", last_sg))
                                if gu == 1 and last_tt1 is not None:
                                    w.append(("v", last_tt1))
                                first = False
                            last = (k == NK - 1 and n == 1)
                            tmk = S.emit("tensor",
                                         (lambda gu=gu, n=n, k=k, slotap=slotap: lambda e:
                                          e.matmul(PS[gu][:, n * 512:(n + 1) * 512],
                                                   slotap[:, k * 128:(k + 1) * 128],
                                                   lnc(k, n * 512, (n + 1) * 512),
                                                   start=(k == 0), stop=(k == NK - 1)))(),
                                         waits=w, inc="t" if last else None)
                    w1ring[slot] = ("t", tmk)
                sgv = a32(SG_O + (j % 2) * P, P)
                last_sg = S.emit("scalar", (lambda sgv=sgv: lambda e:
                                            e.activation(sgv, PS[0][:], AF.Silu))(),
                                 waits=[("t", tmk)], inc="s")
                if shared:
                    last_tt1 = S.emit("vector", (lambda sgv=sgv, j=j: lambda e:
                                                 e.tensor_mul(a16(ACT_O + j * P, P), sgv, PS[1][:]))(),
                                      waits=[("s", last_sg)], inc="v")
                    act_marks[j] = last_tt1
                else:
                    last_tt1 = S.emit("vector", (lambda sgv=sgv: lambda e:
                                                 e.tensor_mul(PS[2][:], sgv, PS[1][:]))(),
                                      waits=[("s", last_sg)], inc="v")
                    cwbv = a32(CWB_O + (eidx % 2) * P, P)
                    w2_ = [("v", cwb_mark)] if (j == 0 and cwb_mark) else []
                    act_marks[j] = S.emit("vector", (lambda j=j, cwbv=cwbv: lambda e:
                                                     e.tensor_mul(a16(ACT_O + j * P, P),
                                                                  PS[2][:], cwbv))(),
                                          waits=w2_, inc="v")
            return act_marks, last_sg, last_tt1

        def emit_expert_w2(eidx, act_marks, w1_last_sg, w1_last_tt1, shared=False):
            prev_stt = [None, None]
            for m in range(NM2):
                psum = PS[m % 2]
                src = sw3t[m] if shared else w2t[eidx, m]
                slot, wm = dma_w2_block(src)
                slotap = w2_slot_ap(slot)
                first = True
                tmk = None
                for j in range(NJ):
                    for n in range(2):
                        w = [("v", act_marks[j])]
                        if first:
                            w.append(("w", wm))
                            if prev_stt[m % 2] is not None:
                                w.append(("v", prev_stt[m % 2]))
                            if m == 0:
                                w.append(("s", w1_last_sg))
                            if m == 1:
                                w.append(("v", w1_last_tt1))
                            first = False
                        last = (j == NJ - 1 and n == 1)
                        tmk = S.emit("tensor",
                                     (lambda n=n, j=j, slotap=slotap, psum=psum: lambda e:
                                      e.matmul(psum[:, n * 512:(n + 1) * 512],
                                               slotap[:, j * 128:(j + 1) * 128],
                                               a16(ACT_O + j * P + n * 512, 512),
                                               start=(j == 0), stop=(j == NJ - 1)))(),
                                     waits=w, inc="t" if last else None)
                w2ring[slot] = ("t", tmk)
                prev_stt[m % 2] = S.emit("vector", (lambda m=m, psum=psum: lambda e:
                                                    e.scalar_tensor_tensor(xc(m), psum[:], 0.5,
                                                                           xc(m), ALU.mult,
                                                                           ALU.add))(),
                                         waits=[("t", tmk)], inc="v")
            return prev_stt

        def emit_ffn(lnb_vb, gate_tm):
            cw_mark = emit_gate(lnb_vb, gate_tm)
            cwb_marks = {0: emit_cwb_prep(0, [("v", cw_mark)])}
            lnb_marks = emit_ln_apply()
            psum_free = {0: ("v", cw_mark)}
            for e_ in range(NE):
                if e_ + 1 < NE:
                    cwb_marks[e_ + 1] = emit_cwb_prep(e_ + 1, [("v", cwb_marks[e_])])
                am, lsg, ltt = emit_expert_w1(e_, lnb_marks, cwb_marks[e_], psum_free)
                stt = emit_expert_w2(e_, am, lsg, ltt)
                psum_free = {0: ("v", stt[0]), 1: ("v", stt[1])}
            am, lsg, ltt = emit_expert_w1(0, lnb_marks, None, psum_free, shared=True)
            emit_expert_w2(0, am, lsg, ltt, shared=True)

        def emit_attn(p, lnb_marks, lnb_vb):
            wq_marks = [None] * NK
            for k in range(NK):
                slot, wm = dma_w1_block(wqT_in[k])
                slotap = w1_slot_ap(slot)
                tmk = None
                for r in range(4):
                    w = [("w", wm)] if r == 0 else []
                    if r == 0:
                        w.append(("v", lnb_vb if k == 0 else wq_marks[k - 1]))
                    tmk = S.emit("tensor", (lambda r=r, slotap=slotap: lambda e:
                                            e.matmul(PS[2][:, 0:512], identb[:],
                                                     slotap[:, r * 512:(r + 1) * 512],
                                                     start=(r == 0), stop=(r == 3)))(),
                                 waits=w, inc="t" if r == 3 else None)
                w1ring[slot] = ("t", tmk)
                wq_marks[k] = S.emit("vector", (lambda k=k: lambda e:
                                               e.tensor_copy(a16(WQST_O + k * 512, 512),
                                                             PS[2][:, 0:512]))(),
                                     waits=[("t", tmk)], inc="v")
            qs_evacs = [None] * 4
            for m in range(4):
                psum = PS[m % 2]
                tmk = None
                for k in range(NK):
                    for n in range(2):
                        w = [("v", wq_marks[k]), ("v", lnb_marks[k])]
                        if k == 0 and n == 0 and m >= 2:
                            w.append(("s", qs_evacs[m - 2]))
                        tmk = S.emit("tensor", (lambda m=m, k=k, n=n, psum=psum: lambda e:
                                                e.matmul(psum[:, n * 512:(n + 1) * 512],
                                                         a16(WQST_O + k * 512 + m * 128, 128),
                                                         lnc(k, n * 512, (n + 1) * 512),
                                                         start=(k == 0), stop=False))(),
                                     waits=w, inc=None)
                for n in range(2):
                    tmk = S.emit("tensor", (lambda m=m, n=n, psum=psum: lambda e:
                                            e.matmul(psum[:, n * 512:(n + 1) * 512],
                                                     bqs_ap[0:1, m * 128:(m + 1) * 128],
                                                     ones_row_bf[0:1, n * 512:(n + 1) * 512],
                                                     start=False, stop=True))(),
                                 inc="t" if n == 1 else None)
                qs_evacs[m] = S.emit("scalar", (lambda m=m, psum=psum: lambda e:
                                               e.copy(qs_ap(m), psum[:]))(),
                                     waits=[("t", tmk)], inc="s")
            wk_ring = [("g", 0)]
            k5_last = None
            for k in range(NK):
                slot = 0
                dst = a16(WKR_O, 512)
                wm = S.emit("gpsimd", (lambda dst=dst, k=k: lambda e:
                                       e.dma_start(dst, wkT_in[k]))(),
                            waits=[wk_ring[slot]], inc="w", n=16)
                for m in range(4):
                    for n in range(2):
                        w = [("w", wm), ("v", lnb_marks[k])]
                        if k == 0 and n == 0:
                            w.append(("s", qs_evacs[m] if m >= 2 else qs_evacs[3]))
                        k5_last = S.emit("tensor",
                                         (lambda k=k, m=m, n=n, dst=dst: lambda e:
                                          e.matmul(PS[m][:, n * 512:(n + 1) * 512],
                                                   dst[:, m * 128:(m + 1) * 128],
                                                   lnc(k, n * 512, (n + 1) * 512),
                                                   start=(k == 0), stop=False))(),
                                         waits=w, inc="t" if (m == 3 and n == 1) else None)
                wk_ring[slot] = ("t", k5_last)
            k5_evacs = [None] * 4
            for m in range(4):
                for n in range(2):
                    k5_last = S.emit("tensor", (lambda m=m, n=n: lambda e:
                                                e.matmul(PS[m][:, n * 512:(n + 1) * 512],
                                                         bk_ap[0:1, m * 128:(m + 1) * 128],
                                                         ones_row_bf[0:1, n * 512:(n + 1) * 512],
                                                         start=False, stop=True))(),
                                     inc="t" if n == 1 else None)
                k5_evacs[m] = S.emit("scalar", (lambda m=m, p=p: lambda e:
                                               e.copy(k5[:, m * T + p * P: m * T + (p + 1) * P],
                                                      PS[m][:]))(),
                                     waits=[("t", k5_last)], inc="s")
            wv_ring = [("g", 0)]
            v5_last = None
            for k in range(NK):
                slot = 0
                dst = a16(WVR_O, 512)
                wm = S.emit("gpsimd", (lambda dst=dst, k=k: lambda e:
                                       e.dma_start(dst, wvT_in[k]))(),
                            waits=[wv_ring[slot]], inc="w", n=16)
                for tt in range(8):
                    psv = PS[tt // 2][:, (tt % 2) * 512:(tt % 2 + 1) * 512]
                    w = [("w", wm), ("v", lnb_marks[k])]
                    if k == 0:
                        w.append(("s", k5_evacs[tt // 2]))
                    v5_last = S.emit("tensor", (lambda k=k, tt=tt, dst=dst, psv=psv: lambda e:
                                                e.matmul(psv, lnc(k, tt * 128, (tt + 1) * 128),
                                                         dst[:], start=(k == 0), stop=False))(),
                                     waits=w, inc="t" if tt == 7 else None)
                wv_ring[slot] = ("t", v5_last)
            v5_evacs = [None] * 8
            for tt in range(8):
                psv = PS[tt // 2][:, (tt % 2) * 512:(tt % 2 + 1) * 512]
                v5_last = S.emit("tensor", (lambda tt=tt, psv=psv: lambda e:
                                            e.matmul(psv, ones_row_bf[0:1, 0:128],
                                                     bv_ap[0:1, 0:512],
                                                     start=False, stop=True))(), inc="t")
                v5_evacs[tt] = S.emit("scalar", (lambda tt=tt, psv=psv, p=p: lambda e:
                                                 e.copy(v5[:, (p * 8 + tt) * QD:(p * 8 + tt + 1) * QD],
                                                        psv))(),
                                      waits=[("t", v5_last)], inc="s")

            pbuf = lnb[:, PB_O:PB_O + 2048]
            ptbuf = lnb[:, PT_O + 0:PT_O + 2048]
            def zsv(kd, lo, n):
                return lnb[:, ZS_O + kd * P + lo: ZS_O + kd * P + lo + n]
            def pjv(kd, lo, n):
                return lnb[:, PJ_O + kd * 2048 + lo: PJ_O + kd * 2048 + lo + n]

            pj_done = None
            for jc in range(4):
                for half in range(2):
                    tmk = None
                    for r in range(4):
                        slot, wm = dma_w1_block(projt_in[r * 4 + jc])
                        slotap = w1_slot_ap(slot)
                        for pc in range(2):
                            w = [("w", wm)] if pc == 0 else []
                            if r == 0 and pc == 0:
                                if jc == 0 and half == 0:
                                    w.append(("s", v5_evacs[7]))
                                if pj_done is not None:
                                    w.append(("v", pj_done))
                            tmk = S.emit("tensor",
                                         (lambda r=r, half=half, pc=pc, slotap=slotap: lambda e:
                                          e.matmul(PS[3][:, pc * 512:(pc + 1) * 512], identb[:],
                                                   slotap[:, half * 1024 + pc * 512:
                                                          half * 1024 + (pc + 1) * 512],
                                                   start=(r == 0), stop=(r == 3)))(),
                                         waits=w, inc="t" if pc == 1 else None)
                        w1ring[slot] = ("t", tmk)
                    pj_done = S.emit("vector", (lambda jc=jc, half=half, tmk=tmk: lambda e:
                                               e.tensor_copy(pjv(jc, half * 1024, 1024),
                                                             PS[3][:]))(),
                                     waits=[("t", tmk)], inc="v")

            slot_last = {0: ("g", 0), 1: ("g", 0)}
            def tr_slot(sl):
                return (PS[2][:, 0:64].bitcast(BF16) if sl == 0
                        else PS[2][:, 512:576].bitcast(BF16))
            prev_exp = None
            prev_z_evac = None
            last_zs_mark = None
            for qt in range(8):
                kmax = p * P + (qt + 1) * 128
                ntile = kmax // 128
                nkc = (kmax + 511) // 512
                tmk = None
                for kd in range(4):
                    for kc in range(nkc):
                        width = min(512, kmax - kc * 512)
                        psv = PS[kc // 2][:, (kc % 2) * 512:(kc % 2) * 512 + width]
                        w = []
                        if kd == 0 and kc == 0:
                            w.append(("s", v5_evacs[7]))
                            if prev_exp is not None:
                                w.append(("s", prev_exp))
                        tmk = S.emit("tensor",
                                     (lambda kd=kd, kc=kc, width=width, psv=psv, qt=qt: lambda e:
                                      e.matmul(psv, qs_ap(kd, qt * 128, 128),
                                               k5[:, kd * T + kc * 512: kd * T + kc * 512 + width],
                                               start=(kd == 0), stop=False))(),
                                     waits=w, inc=None)
                doff = kmax - 128
                kc_d = doff // 512
                off_d = doff % 512
                psv_d = PS[kc_d // 2][:, (kc_d % 2) * 512 + off_d:(kc_d % 2) * 512 + off_d + 128]
                tmk = S.emit("tensor", (lambda psv_d=psv_d: lambda e:
                                        e.matmul(psv_d, identb[:], cmaskb[:],
                                                 start=False, stop=True))(), inc="t")
                w1_ = min(kmax, 1024)
                vm = S.emit("vector", (lambda w1_=w1_: lambda e:
                                       e.reduce_max(smx[:, 0:1], PS[0][:, 0:w1_], axis=AX.X))(),
                            waits=[("t", tmk)], inc="v")
                if kmax > 1024:
                    vm = S.emit("vector", (lambda kmax=kmax: lambda e:
                                           e.reduce_max(smx[:, 1:2], PS[1][:, 0:kmax - 1024],
                                                        axis=AX.X))(), inc="v")
                    vm = S.emit("vector", lambda e: e.tensor_max(smx[:, 0:1], smx[:, 0:1],
                                                                smx[:, 1:2]), inc="v")
                vm = S.emit("vector", lambda e: e.tensor_scalar_mul(smx[:, 2:3], smx[:, 0:1], -1.0),
                            inc="v")
                em = S.emit("scalar", (lambda w1_=w1_: lambda e:
                                       e.activation(pbuf[:, 0:w1_], PS[0][:, 0:w1_], AF.Exp,
                                                    bias=smx[:, 2:3], accum_out=smx[:, 3:4]))(),
                            waits=[("v", vm)], inc="s")
                ell_col = 3
                if kmax > 1024:
                    em = S.emit("scalar", (lambda kmax=kmax: lambda e:
                                           e.activation(pbuf[:, 1024:kmax], PS[1][:, 0:kmax - 1024],
                                                        AF.Exp, bias=smx[:, 2:3],
                                                        accum_out=smx[:, 4:5]))(), inc="s")
                    S.emit("vector", lambda e: e.tensor_add(smx[:, 3:4], smx[:, 3:4], smx[:, 4:5]),
                           waits=[("s", em)], inc="v")
                prev_exp = em
                vm = S.emit("vector", lambda e: e.tensor_scalar_mul(smx[:, 5:6], smx[:, 3:4],
                                                                   SQRT_HS),
                            waits=[("s", em)], inc="v")
                rec_mark = S.emit("vector", lambda e: e.reciprocal(smx[:, 6:7], smx[:, 5:6]),
                                  inc="v")
                pt_evacs = [None] * ntile
                for i in range(ntile):
                    sl = i % 2
                    w = [("s", em), slot_last[sl]]
                    tmk = S.emit("tensor", (lambda i=i, sl=sl: lambda e:
                                            e.transpose(tr_slot(sl), pbuf[:, i * 128:(i + 1) * 128],
                                                        identb[:]))(),
                                 waits=w, inc="t")
                    pt_evacs[i] = S.emit("vector", (lambda i=i, sl=sl: lambda e:
                                                    e.tensor_copy(ptbuf[:, i * 128:(i + 1) * 128],
                                                                  tr_slot(sl)))(),
                                         waits=[("t", tmk)], inc="v")
                    slot_last[sl] = ("v", pt_evacs[i])
                tmk = None
                for i in range(ntile):
                    w = [("v", pt_evacs[i])]
                    if i == 0:
                        w.append(("s", v5_evacs[7]))
                        w.append(("v", pj_done))
                        if prev_z_evac is not None:
                            w.append(("v", prev_z_evac))
                    tmk = S.emit("tensor", (lambda i=i, ntile=ntile: lambda e:
                                            e.matmul(PS[3][:, 0:512],
                                                     ptbuf[:, i * 128:(i + 1) * 128],
                                                     v5[:, i * QD:(i + 1) * QD],
                                                     start=(i == 0), stop=(i == ntile - 1)))(),
                                 waits=w, inc="t" if i == ntile - 1 else None)
                prev_z_evac = S.emit("vector", lambda e: e.tensor_scalar_mul(A16[:, ZR_O:ZR_O + QD],
                                                                            PS[3][:, 0:512],
                                                                            smx[:, 6:7]),
                                     waits=[("t", tmk), ("v", rec_mark)], inc="v")
                for kd in range(4):
                    sl = kd % 2
                    w = [("v", prev_z_evac), slot_last[sl]]
                    tmk = S.emit("tensor", (lambda kd=kd, sl=sl: lambda e:
                                            e.transpose(tr_slot(sl), A16[:, ZR_O + kd * 128:ZR_O + (kd + 1) * 128],
                                                        identb[:]))(),
                                 waits=w, inc="t")
                    last_zs_mark = S.emit("vector", (lambda kd=kd, sl=sl, qt=qt: lambda e:
                                                     e.tensor_copy(zsv(kd, qt * 128, 128),
                                                                   tr_slot(sl)))(),
                                          waits=[("t", tmk)], inc="v")
                    slot_last[sl] = ("v", last_zs_mark)
            prev_add = [None, None]
            for m in range(NM2):
                psum = PS[m % 2]
                tmk = None
                for kd in range(4):
                    for n in range(2):
                        w = []
                        if kd == 0 and n == 0:
                            w.append(("v", last_zs_mark))
                            w.append(("s", prev_exp))
                            if prev_add[m % 2] is not None:
                                w.append(("v", prev_add[m % 2]))
                        tmk = S.emit("tensor", (lambda m=m, kd=kd, n=n, psum=psum: lambda e:
                                                e.matmul(psum[:, n * 512:(n + 1) * 512],
                                                         pjv(kd, m * 128, 128),
                                                         zsv(kd, n * 512, 512),
                                                         start=(kd == 0), stop=(kd == 3)))(),
                                     waits=w, inc="t" if (kd == 3 and n == 1) else None)
                prev_add[m % 2] = S.emit("vector", (lambda m=m, psum=psum: lambda e:
                                                    e.tensor_add(xc(m), xc(m), psum[:]))(),
                                         waits=[("t", tmk)], inc="v")

        def emit_out(p):
            slot_v = [("g", 0), ("g", 0)]
            dma_marks = {}
            for tt in range(8):
                outv = a32(OUT_O + (tt % 2) * 2048, 2048)
                vm = None
                for c in range(NK):
                    sl = c % 2
                    psv = PS[sl][:, 0:128]
                    w = [slot_v[sl]]
                    if c < 2 and tt >= 2:
                        w.append(("y", dma_marks[tt - 2]))
                    tmk = S.emit("tensor", (lambda c=c, tt=tt, psv=psv: lambda e:
                                            e.transpose(psv,
                                                        x_sb[:, c * P + tt * 128: c * P + (tt + 1) * 128],
                                                        identf[:]))(),
                                 waits=w, inc="t")
                    vm = S.emit("vector", (lambda c=c, psv=psv, outv=outv: lambda e:
                                           e.tensor_copy(outv[:, c * 128:(c + 1) * 128], psv))(),
                                waits=[("t", tmk)], inc="v")
                    slot_v[sl] = ("v", vm)
                dma_marks[tt] = S.emit("sync", (lambda tt=tt, p=p, outv=outv: lambda e:
                                                e.dma_start(yout[p * P + tt * 128:
                                                                 p * P + (tt + 1) * 128, :], outv))(),
                                       waits=[("v", vm)], inc="y", n=16)

        def emit_xload(p):
            marks = []
            for c in range(NK):
                m = S.emit("sync", (lambda c=c, p=p: lambda e:
                                    e.dma_start(x_sb[:, c * P:(c + 1) * P], xin[p, c]))(),
                           inc="y", n=16)
                marks.append(("y", m))
            return marks

        x_marks0 = emit_xload(0)
        prelude()
        S.barrier()
        npass = NPASS if stage == "full" else 1
        for p in range(npass):
            x_marks = x_marks0 if p == 0 else emit_xload(p)
            if p == 0:
                tap("xload", lambda: x_sb[:])
            gate_tm = emit_gate_mms([[xm] for xm in x_marks])
            vb = emit_ln(x_marks)
            if stage == "ln1":
                lm = emit_ln_apply()
                tap("lnb", lambda: lnb[:])
                tap("rb", lambda: rb[:])
                tap("nb", lambda: nb[:])
                break
            if stage == "gate":
                emit_gate(vb, gate_tm)
                lm = emit_ln_apply()
                tap("lnb", lambda: lnb[:])
                tap("rb", lambda: rb[:])
                tap("nb", lambda: nb[:])
                tap("cw", lambda: cw_tm[:])
                tap("glog", lambda: gsc[:])
                tap("ltm", lambda: cwsc[:, 0:64])
                tap("ee", lambda: cwsc[:, 64:128])
                tap("inds", lambda: cwsc[:, 128:192])
                tap("m1m2", lambda: cwsc[:, 192:216])
                tap("gws", lambda: gwsum[:])
                break
            emit_ffn(vb, gate_tm)
            if p == 0:
                tap("cw", lambda: cw_tm[:])
                tap("x1", lambda: x_sb[:])
            if stage == "ffn1":
                break
            S.barrier()
            vb = emit_ln(("y", S.c["y"]))
            lm = emit_ln_apply()
            emit_attn(p, lm, vb)
            if p == 0:
                tap("qs", lambda: A16[:, QS_O:QS_O + 3 * P])
                tap("k5", lambda: k5[:])
                tap("v5", lambda: v5[:])
                tap("zs", lambda: lnb[:, ZS_O:ZS_O + 4 * P])
                tap("x2", lambda: x_sb[:])
            if stage == "attn":
                tap("wqst", lambda: A16[:, WQST_O:WQST_O + 16 * 512])
                tap("bqs", lambda: bias_bf[0:1, 0:1536])
                break
            S.barrier()
            gate_tm = emit_gate_mms([[] for _ in range(NK)])
            vb = emit_ln(("y", S.c["y"]))
            emit_ffn(vb, gate_tm)
            S.barrier()
            emit_out(p)
            S.barrier()
        S.barrier()
        S.emit("sync", None, waits=[("y", S.c["y"])])

        @block.tensor
        def _(e):
            S.replay("tensor", e, sems)

        @block.vector
        def _(e):
            S.replay("vector", e, sems)

        @block.scalar
        def _(e):
            S.replay("scalar", e, sems)

        @block.gpsimd
        def _(e):
            S.replay("gpsimd", e, sems)

        @block.sync
        def _(e):
            S.replay("sync", e, sems)

    return nc


def prep_inputs(L0_attn_x, L0_attn_WQ, L0_attn_bQ, L0_attn_WK, L0_attn_bK,
                L0_attn_WV, L0_attn_bV, L0_attn_proj, L0_ffn_gate_w,
                L0_ffn_w1, L0_ffn_w2, L0_ffn_sw1, L0_ffn_sw2, L0_ffn_sw3):
    C = np.ascontiguousarray
    f = np.float32
    w1 = np.asarray(L0_ffn_w1, f)
    w2 = np.asarray(L0_ffn_w2, f)
    shared = {
        "w1t": C(w1.reshape(NE, NK, 128, NM1, 128).transpose(0, 3, 2, 1, 4)
                 .reshape(NE, NM1, 128, NK * 128)),
        "w2t": C(w2.reshape(NE, NJ, 128, NM2, 128).transpose(0, 3, 2, 1, 4)
                 .reshape(NE, NM2, 128, NJ * 128)),
        "sw1t": C(np.asarray(L0_ffn_sw1, f).reshape(NJ, 128, NK, 128)
                  .transpose(0, 3, 2, 1).reshape(NJ, 128, NK * 128)),
        "sw2t": C(np.asarray(L0_ffn_sw2, f).reshape(NJ, 128, NK, 128)
                  .transpose(0, 3, 2, 1).reshape(NJ, 128, NK * 128)),
        "sw3t": C(np.asarray(L0_ffn_sw3, f).reshape(NM2, 128, NJ, 128)
                  .transpose(0, 3, 2, 1).reshape(NM2, 128, NJ * 128)),
        "gwt_in": C(np.asarray(L0_ffn_gate_w, f).T.reshape(NK, 128, NE)
                    .transpose(1, 0, 2).reshape(128, NK * NE)),
        "wqT_in": C(np.asarray(L0_attn_WQ, f).T.reshape(NK, 128, D)),
        "wkT_in": C(np.asarray(L0_attn_WK, f).T.reshape(NK, 128, QD)),
        "wvT_in": C(np.asarray(L0_attn_WV, f).T.reshape(NK, 128, QD)),
        "projt_in": C(np.asarray(L0_attn_proj, f).reshape(NK, 128, D)),
        "bq_in": C(np.asarray(L0_attn_bQ, f).reshape(1, D)),
        "bk_in": C(np.asarray(L0_attn_bK, f).reshape(1, QD)),
        "bv_in": C(np.asarray(L0_attn_bV, f).reshape(1, QD)),
        "cmask_in": C((np.triu(np.ones((128, 128), f), 1) * NEG).astype(f)),
        "ident_in": np.eye(128, dtype=f),
    }
    x = np.asarray(L0_attn_x, f)
    in_maps = []
    for b in range(NCORE):
        xT = C(x[b].T)
        xr = C(xT.reshape(NK, 128, NPASS, P).transpose(2, 0, 1, 3))
        m = dict(shared)
        m["xin"] = xr
        in_maps.append(m)
    return in_maps


def kernel(**inputs):
    in_maps = prep_inputs(**inputs)
    nc = build_nc()
    res = run_bass_kernel_spmd(nc, in_maps, core_ids=list(range(NCORE)))
    out = np.stack([res.results[i]["yout"] for i in range(NCORE)], axis=0)
    return out


# version 07
# speedup vs baseline: 1.1750x; 1.1521x over previous
import numpy as np
import concourse.bass as bass
import concourse.mybir as mybir
from concourse.bass_utils import run_bass_kernel_spmd

F32 = mybir.dt.float32
BF16 = mybir.dt.bfloat16
AF = mybir.ActivationFunctionType
ALU = mybir.AluOpType
AX = mybir.AxisListType

NCORE = 8
D = 2048
T = 2048
P = 1024
NPASS = 2
NK = 16
NE = 8
NJ = 11
NM1 = 22
NM2 = 16
QD = 512
SQRT_HS = float(np.sqrt(np.float32(128.0)))
EPS = 1e-5
NEG = -1.0e30

ENGINES = ("tensor", "vector", "scalar", "gpsimd", "sync")
SEMS = ("t", "v", "s", "w", "y", "g")


class Sched:
    def __init__(self):
        self.items = {e: [] for e in ENGINES}
        self.c = {s: 0 for s in SEMS}
        self._waited = {e: {s: 0 for s in SEMS} for e in ENGINES}

    def emit(self, eng, fn, waits=(), inc=None, n=1):
        if eng == "vector":
            waits = list(waits) + [("v", self.c["v"])]
        elif eng == "scalar":
            waits = list(waits) + [("s", self.c["s"])]
        wl = []
        for s, v in waits:
            if v is None or v <= 0:
                continue
            if v > self._waited[eng][s]:
                self._waited[eng][s] = v
                wl.append((s, v))
        if inc is not None:
            self.c[inc] += n
        self.items[eng].append((fn, tuple(wl), inc, n))
        return self.c[inc] if inc is not None else None

    def barrier(self):
        snap = dict(self.c)
        for e in ENGINES:
            w = [(s, snap[s]) for s in SEMS if snap[s] > self._waited[e][s]]
            if w:
                self.emit(e, None, waits=w)

    def check_deadlock(self):
        import collections
        pos = {e: 0 for e in ENGINES}
        cnt = {s: 0 for s in SEMS}
        n = sum(len(v) for v in self.items.values())
        done = 0
        while done < n:
            progressed = False
            for e in ENGINES:
                while pos[e] < len(self.items[e]):
                    fn, waits, inc, k = self.items[e][pos[e]]
                    if all(cnt[s] >= v for s, v in waits):
                        if inc is not None:
                            cnt[inc] += k
                        pos[e] += 1
                        done += 1
                        progressed = True
                    else:
                        break
            if not progressed:
                for e in ENGINES:
                    if pos[e] < len(self.items[e]):
                        fn, waits, inc, k = self.items[e][pos[e]]
                        print(f"STUCK {e} at item {pos[e]}: waits {waits}, have "
                              f"{[(s, cnt[s]) for s, v in waits]}")
                raise RuntimeError("schedule deadlock")
        return True

    def replay(self, eng_name, eng, sems):
        for fn, waits, inc, n in self.items[eng_name]:
            for s, v in waits:
                eng.wait_ge(sems[s], v)
            if fn is None:
                continue
            inst = fn(eng)
            if inc is not None:
                inst.then_inc(sems[inc], n)


def build_nc(taps=(), stage="full"):
    taps = set(taps)
    nc = bass.Bass("TRN2", target_bir_lowering=False)

    xin = nc.dram_tensor("xin", [NPASS, NK, 128, P], F32, kind="ExternalInput").ap()
    w1t = nc.dram_tensor("w1t", [NE, NM1, 128, NK * 128], F32, kind="ExternalInput").ap()
    w2t = nc.dram_tensor("w2t", [NE, NM2, 128, NJ * 128], F32, kind="ExternalInput").ap()
    sw1t = nc.dram_tensor("sw1t", [NJ, 128, NK * 128], F32, kind="ExternalInput").ap()
    sw2t = nc.dram_tensor("sw2t", [NJ, 128, NK * 128], F32, kind="ExternalInput").ap()
    sw3t = nc.dram_tensor("sw3t", [NM2, 128, NJ * 128], F32, kind="ExternalInput").ap()
    gwt_in = nc.dram_tensor("gwt_in", [128, NK * NE], F32, kind="ExternalInput").ap()
    wqT_in = nc.dram_tensor("wqT_in", [NK, 128, D], F32, kind="ExternalInput").ap()
    wkT_in = nc.dram_tensor("wkT_in", [NK, 128, QD], F32, kind="ExternalInput").ap()
    wvT_in = nc.dram_tensor("wvT_in", [NK, 128, QD], F32, kind="ExternalInput").ap()
    projt_in = nc.dram_tensor("projt_in", [NK, 128, D], F32, kind="ExternalInput").ap()
    bq_in = nc.dram_tensor("bq_in", [1, D], F32, kind="ExternalInput").ap()
    bk_in = nc.dram_tensor("bk_in", [1, QD], F32, kind="ExternalInput").ap()
    bv_in = nc.dram_tensor("bv_in", [1, QD], F32, kind="ExternalInput").ap()
    cmask_in = nc.dram_tensor("cmask_in", [128, 128], F32, kind="ExternalInput").ap()
    ident_in = nc.dram_tensor("ident_in", [128, 128], F32, kind="ExternalInput").ap()
    yout = nc.dram_tensor("yout", [T, D], F32, kind="ExternalOutput").ap()

    tap_drams = {}
    def tapd(name, shape):
        if name in taps:
            tap_drams[name] = nc.dram_tensor("tap_" + name, list(shape), F32,
                                             kind="ExternalOutput").ap()
    tapd("lnb", (128, NK * P)); tapd("rb", (128, P)); tapd("nb", (128, P))
    tapd("cw", (128, 64)); tapd("act", (128, NJ * P)); tapd("x1", (128, NK * P))
    tapd("x2", (128, NK * P)); tapd("qs", (128, 3 * P)); tapd("k5", (128, 4 * T))
    tapd("v5", (128, 16 * QD)); tapd("zs", (128, 4 * P)); tapd("xload", (128, NK * P))
    tapd("glog", (128, P)); tapd("ltm", (128, 64)); tapd("ee", (128, 64))
    tapd("wqst", (128, 16 * 512)); tapd("bqs", (1, 1536))
    tapd("inds", (128, 64)); tapd("m1m2", (128, 24)); tapd("gws", (128, 1))

    from contextlib import ExitStack
    with ExitStack() as stack:
        E = stack.enter_context

        x_sb = E(nc.sbuf_tensor("x_sb", [128, NK * P], F32))
        lnb = E(nc.sbuf_tensor("lnb", [128, NK * P], BF16))
        k5 = E(nc.sbuf_tensor("k5", [128, 4 * T], BF16))
        v5 = E(nc.sbuf_tensor("v5", [128, 16 * QD], BF16))
        rb = E(nc.sbuf_tensor("rb", [128, P], F32))
        nb = E(nc.sbuf_tensor("nb", [128, P], F32))
        xb = E(nc.sbuf_tensor("xb", [128, 2 * P], BF16))
        gsc = E(nc.sbuf_tensor("gsc", [128, P], F32))
        wkv = gsc.bitcast(BF16)
        cw_tm = E(nc.sbuf_tensor("cw_tm", [128, 64], F32))
        gwt = E(nc.sbuf_tensor("gwt", [128, NK * NE], F32))
        gwsum = E(nc.sbuf_tensor("gwsum", [128, 1], F32))
        identb = E(nc.sbuf_tensor("identb", [128, 128], BF16))
        identf = E(nc.sbuf_tensor("identf", [128, 128], F32))
        cmaskb = E(nc.sbuf_tensor("cmaskb", [128, 128], BF16))
        bias_bf = E(nc.sbuf_tensor("bias_bf", [1, 1536], BF16))
        ones_row_bf = E(nc.sbuf_tensor("ones_row_bf", [1, P], BF16))
        ones_col_bf = E(nc.sbuf_tensor("ones_col_bf", [128, 1], BF16))
        ones1_f = E(nc.sbuf_tensor("ones1_f", [1, 128], F32))
        ones128_f = E(nc.sbuf_tensor("ones128_f", [128, 1], F32))
        eps_col = E(nc.sbuf_tensor("eps_col", [128, 1], F32))
        smx = E(nc.sbuf_tensor("smx", [128, 8], F32))

        A16 = E(nc.sbuf_tensor("A16", [128, 20224], BF16))
        A32 = E(nc.sbuf_tensor("A32", [128, 4096], F32))
        cwsc = A32[:, 0:224]
        bqstage = A16[0:1, 0:2048]

        PS = [E(nc.psum_tensor(f"PS{i}", [128, 1024], F32)) for i in range(4)]
        sems = {s: E(nc.semaphore("sem_" + s)) for s in SEMS}
        block = E(nc.Block())
        S = Sched()

        ACT_O = 0
        W1R_O = 11264
        W1R_SLOTS = 3
        W2R_O = W1R_O + W1R_SLOTS * 2048
        W2R_SLOTS = 2
        WKR_O = W2R_O
        WVR_O = W2R_O + 512
        QS3_O = W2R_O + 1024
        ZR_O = W2R_O + 2048
        WQST_O = 0
        QS_O = 8192
        PB_O = 0
        PT_O = 2048
        ZS_O = 4096
        PJ_O = 8192
        SG_O = 0
        CWB_O = 2048
        TMP_O = 0
        OUT_O = 0

        def a16(off, n):
            return A16[:, off:off + n]

        def qs_ap(m, lo=0, n_=P):
            off = (QS_O + m * P) if m < 3 else QS3_O
            return A16[:, off + lo: off + lo + n_]

        def a32(off, n):
            return A32[:, off:off + n]

        def xc(c, lo=0, hi=P):
            return x_sb[:, c * P + lo: c * P + hi]

        def lnc(c, lo=0, hi=P):
            return lnb[:, c * P + lo: c * P + hi]

        bqs_ap = bias_bf[0:1, 0:512]
        bk_ap = bias_bf[0:1, 512:1024]
        bv_ap = bias_bf[0:1, 1024:1536]

        def prelude():
            S.emit("gpsimd", lambda e: e.memset(ones_row_bf[:], 1.0), inc="g")
            S.emit("gpsimd", lambda e: e.memset(ones_col_bf[:], 1.0), inc="g")
            S.emit("gpsimd", lambda e: e.memset(ones1_f[:], 1.0), inc="g")
            S.emit("gpsimd", lambda e: e.memset(ones128_f[:], 1.0), inc="g")
            S.emit("gpsimd", lambda e: e.memset(eps_col[:], EPS), inc="g")
            S.emit("gpsimd", lambda e: e.dma_start(identb[:], ident_in), inc="w", n=16)
            S.emit("gpsimd", lambda e: e.dma_start(identf[:], ident_in), inc="w", n=16)
            S.emit("gpsimd", lambda e: e.dma_start(cmaskb[:], cmask_in), inc="w", n=16)
            S.emit("gpsimd", lambda e: e.dma_start(gwt[:], gwt_in), inc="w", n=16)
            S.emit("gpsimd", lambda e: e.dma_start(bqstage, bq_in), inc="w", n=16)
            S.emit("gpsimd", lambda e: e.dma_start(bias_bf[0:1, 512:1024], bk_in), inc="w", n=16)
            w_mark = S.emit("gpsimd", lambda e: e.dma_start(bias_bf[0:1, 1024:1536], bv_in),
                            inc="w", n=16)
            tm = None
            for r in range(4):
                tm = S.emit("tensor", (lambda r=r: lambda e:
                                       e.matmul(PS[3][0:1, 0:512], ones_row_bf[0:1, 0:1],
                                                bqstage[0:1, r * 512:(r + 1) * 512],
                                                start=(r == 0), stop=(r == 3)))(),
                            waits=[("w", w_mark)], inc="t")
            S.emit("vector", lambda e: e.tensor_copy(bias_bf[0:1, 0:512], PS[3][0:1, 0:512]),
                   waits=[("t", tm)], inc="v")
            tm2 = None
            for c in range(NK):
                tm2 = S.emit("tensor", (lambda c=c: lambda e:
                                        e.matmul(PS[2][0:8, 0:1], gwt[:, c * NE:(c + 1) * NE],
                                                 ones128_f[:], start=(c == 0),
                                                 stop=(c == NK - 1)))(),
                             waits=[("w", w_mark)], inc="t")
            S.emit("vector", lambda e: e.tensor_copy(gwsum[0:8, 0:1], PS[2][0:8, 0:1]),
                   waits=[("t", tm2)], inc="v")

        def tap(name, src_ap_fn):
            if name not in tap_drams:
                return
            dst = tap_drams[name]
            S.barrier()
            S.emit("gpsimd", (lambda dst=dst, f=src_ap_fn: lambda e: e.dma_start(dst, f()))(),
                   inc="w", n=16)
            S.barrier()

        def emit_ln(x_ready):
            s_marks = [None] * NK
            t_marks = [None] * NK
            for c in range(NK):
                w = [x_ready[c] if isinstance(x_ready, list) else x_ready]
                if c >= 2 and t_marks[c - 2] is not None:
                    w.append(("t", t_marks[c - 2]))
                xbv = xb[:, (c % 2) * P:(c % 2 + 1) * P]
                cast_m = S.emit("scalar", (lambda xbv=xbv, c=c: lambda e: e.copy(xbv, xc(c)))(),
                                waits=w, inc="s")
                tsum = None
                for n in range(2):
                    tsum = S.emit("tensor", (lambda xbv=xbv, n=n, c=c: lambda e:
                                             e.matmul(PS[3][0:1, n * 512:(n + 1) * 512],
                                                      ones_col_bf[:],
                                                      xbv[:, n * 512:(n + 1) * 512],
                                                      start=(c == 0), stop=(c == NK - 1)))(),
                                  waits=[("s", cast_m)], inc="t" if n == 1 else None)
                s_marks[c] = S.emit("scalar", (lambda xbv=xbv: lambda e:
                                               e.activation(xbv, xbv, AF.Square))(),
                                    waits=[("t", tsum)], inc="s")
                for n in range(2):
                    t_marks[c] = S.emit("tensor", (lambda xbv=xbv, n=n, c=c: lambda e:
                                                   e.matmul(PS[2][0:1, n * 512:(n + 1) * 512],
                                                            ones_col_bf[:],
                                                            xbv[:, n * 512:(n + 1) * 512],
                                                            start=(c == 0), stop=(c == NK - 1)))(),
                                        waits=[("s", s_marks[c])], inc="t" if n == 1 else None)
            tm_stats = t_marks[NK - 1]
            S1 = A32[0:1, 0:P]
            S2 = A32[0:1, P:2 * P]
            S3 = A32[0:1, 2 * P:3 * P]
            S4 = A32[0:1, 3 * P:4 * P]
            vm_sums = S.emit("vector", lambda e: e.tensor_copy(S1, PS[3][0:1, 0:P]),
                             waits=[("t", tm_stats)], inc="v")
            vm_sumsq = S.emit("vector", lambda e: e.tensor_copy(S2, PS[2][0:1, 0:P]), inc="v")
            vm_m = S.emit("vector", lambda e: e.tensor_scalar_mul(S3, S1, 1.0 / D), inc="v")
            sm = S.emit("scalar", lambda e: e.activation(S4, S3, AF.Square),
                        waits=[("v", vm_m)], inc="s")
            vm_var = S.emit("vector", lambda e: e.scalar_tensor_tensor(S2, S2, 1.0 / D, S4,
                                                                      ALU.mult, ALU.subtract),
                            waits=[("s", sm)], inc="v")
            sm = S.emit("scalar", lambda e: e.activation(S4, S2, AF.Ln, bias=eps_col[0:1, 0:1]),
                        waits=[("v", vm_var)], inc="s")
            sm_r = S.emit("scalar", lambda e: e.activation(S1, S4, AF.Exp, scale=-0.5),
                          waits=[("v", vm_m)], inc="s")
            vm_nmr = S.emit("vector", lambda e: e.scalar_tensor_tensor(S3, S3, -1.0, S1,
                                                                      ALU.mult, ALU.mult),
                            waits=[("s", sm_r)], inc="v")
            for n in range(2):
                S.emit("tensor", (lambda n=n, S1=S1: lambda e:
                                  e.matmul(PS[1][:, n * 512:(n + 1) * 512], ones1_f[:],
                                           S1[0:1, n * 512:(n + 1) * 512],
                                           start=True, stop=True))(),
                       waits=[("s", sm_r)], inc=None)
            tm = None
            for n in range(2):
                tm = S.emit("tensor", (lambda n=n, S3=S3: lambda e:
                                       e.matmul(PS[2][:, n * 512:(n + 1) * 512], ones1_f[:],
                                                S3[0:1, n * 512:(n + 1) * 512],
                                                start=True, stop=True))(),
                            waits=[("v", vm_nmr), ("v", vm_sumsq)], inc="t" if n == 1 else None)
            S.emit("vector", lambda e: e.tensor_copy(rb[:], PS[1][:]), waits=[("t", tm)], inc="v")
            vb = S.emit("vector", lambda e: e.tensor_copy(nb[:], PS[2][:]), inc="v")
            return vb

        def emit_ln_apply():
            lnb_marks = [None] * NK
            tmpv = a32(TMP_O, P)
            for c in range(NK):
                S.emit("vector", (lambda c=c, tmpv=tmpv: lambda e:
                                  e.tensor_mul(tmpv, xc(c), rb[:]))(), inc="v")
                lnb_marks[c] = S.emit("vector", (lambda c=c, tmpv=tmpv: lambda e:
                                                 e.tensor_add(lnc(c), tmpv, nb[:]))(), inc="v")
            return lnb_marks

        w1ring = [("g", 0)] * W1R_SLOTS
        w2ring = [("g", 0)] * W2R_SLOTS
        ring_idx = {"w1": 0, "w2": 0}

        def w1_slot_ap(i):
            return a16(W1R_O + i * 2048, 2048)

        def w2_slot_ap(i):
            return a16(W2R_O + i * 1408, 1408)

        def dma_w1_block(src_ap):
            i = ring_idx["w1"] % W1R_SLOTS
            ring_idx["w1"] += 1
            dst = w1_slot_ap(i)
            wm = S.emit("gpsimd", (lambda dst=dst, src=src_ap: lambda e:
                                   e.dma_start(dst, src))(),
                        waits=[w1ring[i]], inc="w", n=16)
            return i, wm

        def dma_w2_block(src_ap):
            i = ring_idx["w2"] % W2R_SLOTS
            ring_idx["w2"] += 1
            dst = w2_slot_ap(i)
            wm = S.emit("gpsimd", (lambda dst=dst, src=src_ap: lambda e:
                                   e.dma_start(dst, src))(),
                        waits=[w2ring[i]], inc="w", n=16)
            return i, wm

        def emit_gate_mms(x_waits):
            tm = None
            for c in range(NK):
                for n in range(2):
                    w = list(x_waits[c]) if n == 0 else []
                    last = (c == NK - 1 and n == 1)
                    tm = S.emit("tensor", (lambda c=c, n=n: lambda e:
                                           e.matmul(PS[0][0:8, n * 512:(n + 1) * 512],
                                                    gwt[:, c * NE:(c + 1) * NE],
                                                    xc(c, n * 512, (n + 1) * 512),
                                                    start=(c == 0), stop=(c == NK - 1)))(),
                                waits=w, inc="t" if last else None)
            return tm

        def emit_gate(lnb_vb, gate_tm):
            g0 = gsc[0:8, 0:P]
            g1 = A32[0:8, 0:P]
            S.emit("vector", lambda e: e.tensor_mul(A32[0:8, P:2 * P], PS[0][0:8, 0:P], rb[0:8, 0:P]),
                   waits=[("t", gate_tm)], inc="v")
            S.emit("vector", lambda e: e.tensor_scalar_mul(g1, nb[0:8, 0:P], gwsum[0:8, 0:1]),
                   inc="v")
            t1_mark = S.emit("vector", lambda e: e.tensor_add(g0, A32[0:8, P:2 * P], g1), inc="v")
            vm = t1_mark
            tm = None
            for tt in range(8):
                tm = S.emit("tensor", (lambda tt=tt: lambda e:
                                       e.transpose(PS[2][:, tt * 8:(tt + 1) * 8],
                                                   g0[0:8, tt * 128:(tt + 1) * 128],
                                                   identf[0:8, 0:8]))(),
                            waits=[("v", vm)], inc="t" if tt == 7 else None)
            L = cwsc[:, 0:64]
            E_ = cwsc[:, 64:128]
            SCR = cwsc[:, 128:192]
            mA = smx[:, 0:8]
            mB = cwsc[:, 192:200]
            mC = cwsc[:, 200:208]
            vm = S.emit("vector", lambda e: e.tensor_copy(L, PS[2][:, 0:64]),
                        waits=[("t", tm)], inc="v")
            L3 = L.rearrange("p (a b) -> p a b", b=8)
            E3 = E_.rearrange("p (a b) -> p a b", b=8)
            S3 = SCR.rearrange("p (a b) -> p a b", b=8)
            mA3 = mA.rearrange("p (a one) -> p a one", one=1)
            mB3 = mB.rearrange("p (a one) -> p a one", one=1)
            mC3 = mC.rearrange("p (a one) -> p a one", one=1)
            def bcast(small3, big3):
                b1, _ = bass.broadcast_tensor_aps(small3, big3)
                return b1
            S.emit("vector", lambda e: e.reduce_max(mA, L3, axis=AX.X, negate=True), inc="v")
            vm = S.emit("vector", lambda e: e.tensor_tensor(L3, L3, bcast(mA3, L3), ALU.add),
                        inc="v")
            sm = S.emit("scalar", lambda e: e.activation(E_, L, AF.Exp), waits=[("v", vm)], inc="s")
            S.emit("vector", lambda e: e.reduce_max(mA, E3, axis=AX.X), waits=[("s", sm)], inc="v")
            S.emit("vector", lambda e: e.tensor_tensor(S3, E3, bcast(mA3, E3), ALU.is_ge), inc="v")
            S.emit("vector", lambda e: e.tensor_scalar(SCR, SCR, -1.0, 1.0, ALU.mult, ALU.add),
                   inc="v")
            S.emit("vector", lambda e: e.tensor_tensor(SCR, E_, SCR, ALU.mult), inc="v")
            S.emit("vector", lambda e: e.reduce_max(mB, S3, axis=AX.X), inc="v")
            S.emit("vector", lambda e: e.tensor_tensor(S3, E3, bcast(mB3, E3), ALU.is_ge), inc="v")
            S.emit("vector", lambda e: e.scalar_tensor_tensor(mC, mA, 1e-20, mB, ALU.add, ALU.add),
                   inc="v")
            mD = cwsc[:, 208:216]
            mD3 = mD.rearrange("p (a one) -> p a one", one=1)
            S.emit("vector", lambda e: e.reciprocal(mD, mC), inc="v")
            S.emit("vector", lambda e: e.tensor_tensor(SCR, E_, SCR, ALU.mult), inc="v")
            vm = S.emit("vector", lambda e: e.tensor_tensor(
                cw_tm[:].rearrange("p (a b) -> p a b", b=8), S3, bcast(mD3, S3), ALU.mult), inc="v")
            return vm

        def emit_cwb_prep(e_idx, wait_marks):
            tm = None
            for tt in range(8):
                col = tt * 8 + e_idx
                w = list(wait_marks) if tt == 0 else []
                tm = S.emit("tensor", (lambda col=col, tt=tt: lambda e:
                                       e.transpose(PS[3][0:1, tt * 128:(tt + 1) * 128],
                                                   cw_tm[:, col:col + 1], identf[:]))(),
                            waits=w, inc="t" if tt == 7 else None)
            cwrow = gsc[0:1, 0:P]
            vm = S.emit("vector", (lambda cwrow=cwrow: lambda e:
                                   e.tensor_copy(cwrow, PS[3][0:1, 0:P]))(),
                        waits=[("t", tm)], inc="v")
            tm = None
            for n in range(2):
                tm = S.emit("tensor", (lambda n=n, cwrow=cwrow: lambda e:
                                       e.matmul(PS[3][:, n * 512:(n + 1) * 512], ones1_f[:],
                                                cwrow[0:1, n * 512:(n + 1) * 512],
                                                start=True, stop=True))(),
                            waits=[("v", vm)], inc="t" if n == 1 else None)
            dstv = a32(CWB_O + (e_idx % 2) * P, P)
            vm = S.emit("vector", (lambda dstv=dstv: lambda e:
                                   e.tensor_copy(dstv, PS[3][:]))(),
                        waits=[("t", tm)], inc="v")
            return vm

        def emit_expert_w1(eidx, lnb_marks, cwb_mark, psum_free, shared=False):
            act_marks = [None] * NJ
            last_sg = None
            last_tt1 = None
            tmk = None
            for j in range(NJ):
                if shared:
                    blocks = [(sw1t[j], 0), (sw2t[j], 1)]
                else:
                    blocks = [(w1t[eidx, j], 0), (w1t[eidx, NJ + j], 1)]
                tm_gu = [None, None]
                for src, gu in blocks:
                    slot, wm = dma_w1_block(src)
                    slotap = w1_slot_ap(slot)
                    first = True
                    for k in range(NK):
                        for n in range(2):
                            w = [("v", lnb_marks[k])]
                            if first:
                                w.append(("w", wm))
                                if j == 0 and psum_free.get(gu):
                                    w.append(psum_free[gu])
                                if gu == 0 and last_sg is not None:
                                    w.append(("s", last_sg))
                                if gu == 1 and last_tt1 is not None:
                                    w.append(("v", last_tt1))
                                first = False
                            last = (k == NK - 1 and n == 1)
                            tmk = S.emit("tensor",
                                         (lambda gu=gu, n=n, k=k, slotap=slotap: lambda e:
                                          e.matmul(PS[gu][:, n * 512:(n + 1) * 512],
                                                   slotap[:, k * 128:(k + 1) * 128],
                                                   lnc(k, n * 512, (n + 1) * 512),
                                                   start=(k == 0), stop=(k == NK - 1)))(),
                                         waits=w, inc="t" if last else None)
                    w1ring[slot] = ("t", tmk)
                    tm_gu[gu] = tmk
                sgv = a32(SG_O + (j % 2) * P, P)
                last_sg = S.emit("scalar", (lambda sgv=sgv: lambda e:
                                            e.activation(sgv, PS[0][:], AF.Silu))(),
                                 waits=[("t", tm_gu[0])], inc="s")
                if shared:
                    last_tt1 = S.emit("vector", (lambda sgv=sgv, j=j: lambda e:
                                                 e.tensor_mul(a16(ACT_O + j * P, P), sgv, PS[1][:]))(),
                                      waits=[("s", last_sg), ("t", tm_gu[1])], inc="v")
                    act_marks[j] = last_tt1
                else:
                    last_tt1 = S.emit("vector", (lambda sgv=sgv: lambda e:
                                                 e.tensor_mul(PS[2][:], sgv, PS[1][:]))(),
                                      waits=[("s", last_sg), ("t", tm_gu[1])], inc="v")
                    cwbv = a32(CWB_O + (eidx % 2) * P, P)
                    w2_ = [("v", cwb_mark)] if (j == 0 and cwb_mark) else []
                    act_marks[j] = S.emit("vector", (lambda j=j, cwbv=cwbv: lambda e:
                                                     e.tensor_mul(a16(ACT_O + j * P, P),
                                                                  PS[2][:], cwbv))(),
                                          waits=w2_, inc="v")
            return act_marks, last_sg, last_tt1

        def emit_expert_w2(eidx, act_marks, w1_last_sg, w1_last_tt1, shared=False):
            prev_stt = [None, None]
            for m in range(NM2):
                psum = PS[m % 2]
                src = sw3t[m] if shared else w2t[eidx, m]
                slot, wm = dma_w2_block(src)
                slotap = w2_slot_ap(slot)
                first = True
                tmk = None
                for j in range(NJ):
                    for n in range(2):
                        w = [("v", act_marks[j])]
                        if first:
                            w.append(("w", wm))
                            if prev_stt[m % 2] is not None:
                                w.append(("v", prev_stt[m % 2]))
                            if m == 0:
                                w.append(("s", w1_last_sg))
                            if m == 1:
                                w.append(("v", w1_last_tt1))
                            first = False
                        last = (j == NJ - 1 and n == 1)
                        tmk = S.emit("tensor",
                                     (lambda n=n, j=j, slotap=slotap, psum=psum: lambda e:
                                      e.matmul(psum[:, n * 512:(n + 1) * 512],
                                               slotap[:, j * 128:(j + 1) * 128],
                                               a16(ACT_O + j * P + n * 512, 512),
                                               start=(j == 0), stop=(j == NJ - 1)))(),
                                     waits=w, inc="t" if last else None)
                w2ring[slot] = ("t", tmk)
                prev_stt[m % 2] = S.emit("vector", (lambda m=m, psum=psum: lambda e:
                                                    e.scalar_tensor_tensor(xc(m), psum[:], 0.5,
                                                                           xc(m), ALU.mult,
                                                                           ALU.add))(),
                                         waits=[("t", tmk)], inc="v")
            return prev_stt

        def emit_ffn(lnb_vb, gate_tm):
            cw_mark = emit_gate(lnb_vb, gate_tm)
            cwb_marks = {0: emit_cwb_prep(0, [("v", cw_mark)])}
            lnb_marks = emit_ln_apply()
            psum_free = {0: ("v", cw_mark)}
            for e_ in range(NE):
                if e_ + 1 < NE:
                    cwb_marks[e_ + 1] = emit_cwb_prep(e_ + 1, [("v", cwb_marks[e_])])
                am, lsg, ltt = emit_expert_w1(e_, lnb_marks, cwb_marks[e_], psum_free)
                stt = emit_expert_w2(e_, am, lsg, ltt)
                psum_free = {0: ("v", stt[0]), 1: ("v", stt[1])}
            am, lsg, ltt = emit_expert_w1(0, lnb_marks, None, psum_free, shared=True)
            emit_expert_w2(0, am, lsg, ltt, shared=True)

        def emit_attn(p, lnb_marks, lnb_vb):
            wq_marks = [None] * NK
            for k in range(NK):
                slot, wm = dma_w1_block(wqT_in[k])
                slotap = w1_slot_ap(slot)
                ph = (k % 2) * 512
                tmk = None
                for r in range(4):
                    w = [("w", wm)] if r == 0 else []
                    if r == 0:
                        w.append(("v", lnb_vb if k < 2 else wq_marks[k - 2]))
                    tmk = S.emit("tensor", (lambda r=r, slotap=slotap, ph=ph: lambda e:
                                            e.matmul(PS[2][:, ph:ph + 512], identb[:],
                                                     slotap[:, r * 512:(r + 1) * 512],
                                                     start=(r == 0), stop=(r == 3)))(),
                                 waits=w, inc="t" if r == 3 else None)
                w1ring[slot] = ("t", tmk)
                wq_marks[k] = S.emit("vector", (lambda k=k, ph=ph: lambda e:
                                               e.tensor_copy(a16(WQST_O + k * 512, 512),
                                                             PS[2][:, ph:ph + 512]))(),
                                     waits=[("t", tmk)], inc="v")
            qs_evacs = [None] * 4
            qs_tmk3 = None
            for m in range(4):
                psum = PS[m % 2]
                tmk = None
                for k in range(NK):
                    for n in range(2):
                        w = [("v", wq_marks[k]), ("v", lnb_marks[k])]
                        if k == 0 and n == 0 and m >= 2:
                            w.append(("s", qs_evacs[m - 2]))
                        tmk = S.emit("tensor", (lambda m=m, k=k, n=n, psum=psum: lambda e:
                                                e.matmul(psum[:, n * 512:(n + 1) * 512],
                                                         a16(WQST_O + k * 512 + m * 128, 128),
                                                         lnc(k, n * 512, (n + 1) * 512),
                                                         start=(k == 0), stop=False))(),
                                     waits=w, inc=None)
                for n in range(2):
                    tmk = S.emit("tensor", (lambda m=m, n=n, psum=psum: lambda e:
                                            e.matmul(psum[:, n * 512:(n + 1) * 512],
                                                     bqs_ap[0:1, m * 128:(m + 1) * 128],
                                                     ones_row_bf[0:1, n * 512:(n + 1) * 512],
                                                     start=False, stop=True))(),
                                 inc="t" if n == 1 else None)
                qs_evacs[m] = S.emit("scalar", (lambda m=m, psum=psum: lambda e:
                                               e.copy(qs_ap(m), psum[:]))(),
                                     waits=[("t", tmk)], inc="s")
            qs_tmk3 = tmk
            wk_ring = [("g", 0), ("g", 0)]
            k5_last = None
            for k in range(NK):
                slot = k % 2
                dst = wkv[:, slot * 512:(slot + 1) * 512]
                wm = S.emit("gpsimd", (lambda dst=dst, k=k: lambda e:
                                       e.dma_start(dst, wkT_in[k]))(),
                            waits=[wk_ring[slot]], inc="w", n=16)
                for m in range(4):
                    for n in range(2):
                        w = [("w", wm), ("v", lnb_marks[k])]
                        if k == 0 and n == 0:
                            w.append(("s", qs_evacs[m] if m >= 2 else qs_evacs[3]))
                        k5_last = S.emit("tensor",
                                         (lambda k=k, m=m, n=n, dst=dst: lambda e:
                                          e.matmul(PS[m][:, n * 512:(n + 1) * 512],
                                                   dst[:, m * 128:(m + 1) * 128],
                                                   lnc(k, n * 512, (n + 1) * 512),
                                                   start=(k == 0), stop=False))(),
                                         waits=w, inc="t" if (m == 3 and n == 1) else None)
                wk_ring[slot] = ("t", k5_last)
            k5_evacs = [None] * 4
            for m in range(4):
                for n in range(2):
                    k5_last = S.emit("tensor", (lambda m=m, n=n: lambda e:
                                                e.matmul(PS[m][:, n * 512:(n + 1) * 512],
                                                         bk_ap[0:1, m * 128:(m + 1) * 128],
                                                         ones_row_bf[0:1, n * 512:(n + 1) * 512],
                                                         start=False, stop=True))(),
                                     inc="t" if n == 1 else None)
                k5_evacs[m] = S.emit("scalar", (lambda m=m, p=p: lambda e:
                                               e.copy(k5[:, m * T + p * P: m * T + (p + 1) * P],
                                                      PS[m][:]))(),
                                     waits=[("t", k5_last)], inc="s")
            wv_ring = [("g", 0), ("g", 0)]
            v5_last = None
            for k in range(NK):
                slot = k % 2
                dst = wkv[:, 1024 + slot * 512:1024 + (slot + 1) * 512]
                wm = S.emit("gpsimd", (lambda dst=dst, k=k: lambda e:
                                       e.dma_start(dst, wvT_in[k]))(),
                            waits=[wv_ring[slot]], inc="w", n=16)
                for tt in range(8):
                    psv = PS[tt // 2][:, (tt % 2) * 512:(tt % 2 + 1) * 512]
                    w = [("w", wm), ("v", lnb_marks[k])]
                    if k == 0:
                        w.append(("s", k5_evacs[tt // 2]))
                    v5_last = S.emit("tensor", (lambda k=k, tt=tt, dst=dst, psv=psv: lambda e:
                                                e.matmul(psv, lnc(k, tt * 128, (tt + 1) * 128),
                                                         dst[:], start=(k == 0), stop=False))(),
                                     waits=w, inc="t" if tt == 7 else None)
                wv_ring[slot] = ("t", v5_last)
            v5_evacs = [None] * 8
            for tt in range(8):
                psv = PS[tt // 2][:, (tt % 2) * 512:(tt % 2 + 1) * 512]
                v5_last = S.emit("tensor", (lambda tt=tt, psv=psv: lambda e:
                                            e.matmul(psv, ones_row_bf[0:1, 0:128],
                                                     bv_ap[0:1, 0:512],
                                                     start=False, stop=True))(), inc="t")
                v5_evacs[tt] = S.emit("scalar", (lambda tt=tt, psv=psv, p=p: lambda e:
                                                 e.copy(v5[:, (p * 8 + tt) * QD:(p * 8 + tt + 1) * QD],
                                                        psv))(),
                                      waits=[("t", v5_last)], inc="s")

            pbuf = lnb[:, PB_O:PB_O + 2048]
            ptbuf = lnb[:, PT_O + 0:PT_O + 2048]
            def zsv(kd, lo, n):
                return lnb[:, ZS_O + kd * P + lo: ZS_O + kd * P + lo + n]
            def pjv(kd, lo, n):
                return lnb[:, PJ_O + kd * 2048 + lo: PJ_O + kd * 2048 + lo + n]

            pj_done = None
            for jc in range(4):
                for half in range(2):
                    tmk = None
                    for r in range(4):
                        slot, wm = dma_w1_block(projt_in[r * 4 + jc])
                        slotap = w1_slot_ap(slot)
                        for pc in range(2):
                            w = [("w", wm)] if pc == 0 else []
                            if r == 0 and pc == 0:
                                if jc == 0 and half == 0:
                                    w.append(("s", v5_evacs[7]))
                                if pj_done is not None:
                                    w.append(("v", pj_done))
                            tmk = S.emit("tensor",
                                         (lambda r=r, half=half, pc=pc, slotap=slotap: lambda e:
                                          e.matmul(PS[3][:, pc * 512:(pc + 1) * 512], identb[:],
                                                   slotap[:, half * 1024 + pc * 512:
                                                          half * 1024 + (pc + 1) * 512],
                                                   start=(r == 0), stop=(r == 3)))(),
                                         waits=w, inc="t" if pc == 1 else None)
                        w1ring[slot] = ("t", tmk)
                    pj_done = S.emit("vector", (lambda jc=jc, half=half, tmk=tmk: lambda e:
                                               e.tensor_copy(pjv(jc, half * 1024, 1024),
                                                             PS[3][:]))(),
                                     waits=[("t", tmk)], inc="v")

            slot_last = {0: ("g", 0), 1: ("g", 0)}
            def tr_slot(sl):
                return (PS[2][:, 0:64].bitcast(BF16) if sl == 0
                        else PS[2][:, 512:576].bitcast(BF16))
            prev_exp = None
            prev_z_evac = None
            last_zs_mark = None
            for qt in range(8):
                kmax = p * P + (qt + 1) * 128
                ntile = kmax // 128
                nkc = (kmax + 511) // 512
                tmk = None
                for kd in range(4):
                    for kc in range(nkc):
                        width = min(512, kmax - kc * 512)
                        psv = PS[kc // 2][:, (kc % 2) * 512:(kc % 2) * 512 + width]
                        w = []
                        if kd == 0 and kc == 0:
                            w.append(("s", v5_evacs[7]))
                            if prev_exp is not None:
                                w.append(("s", prev_exp))
                        tmk = S.emit("tensor",
                                     (lambda kd=kd, kc=kc, width=width, psv=psv, qt=qt: lambda e:
                                      e.matmul(psv, qs_ap(kd, qt * 128, 128),
                                               k5[:, kd * T + kc * 512: kd * T + kc * 512 + width],
                                               start=(kd == 0), stop=False))(),
                                     waits=w, inc=None)
                doff = kmax - 128
                kc_d = doff // 512
                off_d = doff % 512
                psv_d = PS[kc_d // 2][:, (kc_d % 2) * 512 + off_d:(kc_d % 2) * 512 + off_d + 128]
                tmk = S.emit("tensor", (lambda psv_d=psv_d: lambda e:
                                        e.matmul(psv_d, identb[:], cmaskb[:],
                                                 start=False, stop=True))(), inc="t")
                w1_ = min(kmax, 1024)
                vm = S.emit("vector", (lambda w1_=w1_: lambda e:
                                       e.reduce_max(smx[:, 0:1], PS[0][:, 0:w1_], axis=AX.X))(),
                            waits=[("t", tmk)], inc="v")
                if kmax > 1024:
                    vm = S.emit("vector", (lambda kmax=kmax: lambda e:
                                           e.reduce_max(smx[:, 1:2], PS[1][:, 0:kmax - 1024],
                                                        axis=AX.X))(), inc="v")
                    vm = S.emit("vector", lambda e: e.tensor_max(smx[:, 0:1], smx[:, 0:1],
                                                                smx[:, 1:2]), inc="v")
                vm = S.emit("vector", lambda e: e.tensor_scalar_mul(smx[:, 2:3], smx[:, 0:1], -1.0),
                            inc="v")
                em = S.emit("scalar", (lambda w1_=w1_: lambda e:
                                       e.activation(pbuf[:, 0:w1_], PS[0][:, 0:w1_], AF.Exp,
                                                    bias=smx[:, 2:3], accum_out=smx[:, 3:4]))(),
                            waits=[("v", vm)], inc="s")
                ell_col = 3
                if kmax > 1024:
                    em = S.emit("scalar", (lambda kmax=kmax: lambda e:
                                           e.activation(pbuf[:, 1024:kmax], PS[1][:, 0:kmax - 1024],
                                                        AF.Exp, bias=smx[:, 2:3],
                                                        accum_out=smx[:, 4:5]))(), inc="s")
                    S.emit("vector", lambda e: e.tensor_add(smx[:, 3:4], smx[:, 3:4], smx[:, 4:5]),
                           waits=[("s", em)], inc="v")
                prev_exp = em
                vm = S.emit("vector", lambda e: e.tensor_scalar_mul(smx[:, 5:6], smx[:, 3:4],
                                                                   SQRT_HS),
                            waits=[("s", em)], inc="v")
                rec_mark = S.emit("vector", lambda e: e.reciprocal(smx[:, 6:7], smx[:, 5:6]),
                                  inc="v")
                pt_evacs = [None] * ntile
                for i in range(ntile):
                    sl = i % 2
                    w = [("s", em), slot_last[sl]]
                    tmk = S.emit("tensor", (lambda i=i, sl=sl: lambda e:
                                            e.transpose(tr_slot(sl), pbuf[:, i * 128:(i + 1) * 128],
                                                        identb[:]))(),
                                 waits=w, inc="t")
                    pt_evacs[i] = S.emit("vector", (lambda i=i, sl=sl: lambda e:
                                                    e.tensor_copy(ptbuf[:, i * 128:(i + 1) * 128],
                                                                  tr_slot(sl)))(),
                                         waits=[("t", tmk)], inc="v")
                    slot_last[sl] = ("v", pt_evacs[i])
                tmk = None
                for i in range(ntile):
                    w = [("v", pt_evacs[i])]
                    if i == 0:
                        w.append(("s", v5_evacs[7]))
                        w.append(("v", pj_done))
                        if prev_z_evac is not None:
                            w.append(("v", prev_z_evac))
                    tmk = S.emit("tensor", (lambda i=i, ntile=ntile: lambda e:
                                            e.matmul(PS[3][:, 0:512],
                                                     ptbuf[:, i * 128:(i + 1) * 128],
                                                     v5[:, i * QD:(i + 1) * QD],
                                                     start=(i == 0), stop=(i == ntile - 1)))(),
                                 waits=w, inc="t" if i == ntile - 1 else None)
                prev_z_evac = S.emit("vector", lambda e: e.tensor_scalar_mul(A16[:, ZR_O:ZR_O + QD],
                                                                            PS[3][:, 0:512],
                                                                            smx[:, 6:7]),
                                     waits=[("t", tmk), ("v", rec_mark)], inc="v")
                for kd in range(4):
                    sl = kd % 2
                    w = [("v", prev_z_evac), slot_last[sl]]
                    tmk = S.emit("tensor", (lambda kd=kd, sl=sl: lambda e:
                                            e.transpose(tr_slot(sl), A16[:, ZR_O + kd * 128:ZR_O + (kd + 1) * 128],
                                                        identb[:]))(),
                                 waits=w, inc="t")
                    last_zs_mark = S.emit("vector", (lambda kd=kd, sl=sl, qt=qt: lambda e:
                                                     e.tensor_copy(zsv(kd, qt * 128, 128),
                                                                   tr_slot(sl)))(),
                                          waits=[("t", tmk)], inc="v")
                    slot_last[sl] = ("v", last_zs_mark)
            prev_add = [None, None]
            for m in range(NM2):
                psum = PS[m % 2]
                tmk = None
                for kd in range(4):
                    for n in range(2):
                        w = []
                        if kd == 0 and n == 0:
                            w.append(("v", last_zs_mark))
                            w.append(("s", prev_exp))
                            if prev_add[m % 2] is not None:
                                w.append(("v", prev_add[m % 2]))
                        tmk = S.emit("tensor", (lambda m=m, kd=kd, n=n, psum=psum: lambda e:
                                                e.matmul(psum[:, n * 512:(n + 1) * 512],
                                                         pjv(kd, m * 128, 128),
                                                         zsv(kd, n * 512, 512),
                                                         start=(kd == 0), stop=(kd == 3)))(),
                                     waits=w, inc="t" if (kd == 3 and n == 1) else None)
                prev_add[m % 2] = S.emit("vector", (lambda m=m, psum=psum: lambda e:
                                                    e.tensor_add(xc(m), xc(m), psum[:]))(),
                                         waits=[("t", tmk)], inc="v")

        def emit_out(p):
            slot_v = [("g", 0), ("g", 0)]
            dma_marks = {}
            for tt in range(8):
                outv = a32(OUT_O + (tt % 2) * 2048, 2048)
                vm = None
                for c in range(NK):
                    sl = c % 2
                    psv = PS[sl][:, 0:128]
                    w = [slot_v[sl]]
                    if c < 2 and tt >= 2:
                        w.append(("y", dma_marks[tt - 2]))
                    tmk = S.emit("tensor", (lambda c=c, tt=tt, psv=psv: lambda e:
                                            e.transpose(psv,
                                                        x_sb[:, c * P + tt * 128: c * P + (tt + 1) * 128],
                                                        identf[:]))(),
                                 waits=w, inc="t")
                    vm = S.emit("vector", (lambda c=c, psv=psv, outv=outv: lambda e:
                                           e.tensor_copy(outv[:, c * 128:(c + 1) * 128], psv))(),
                                waits=[("t", tmk)], inc="v")
                    slot_v[sl] = ("v", vm)
                dma_marks[tt] = S.emit("sync", (lambda tt=tt, p=p, outv=outv: lambda e:
                                                e.dma_start(yout[p * P + tt * 128:
                                                                 p * P + (tt + 1) * 128, :], outv))(),
                                       waits=[("v", vm)], inc="y", n=16)

        def emit_xload(p):
            marks = []
            for c in range(NK):
                m = S.emit("sync", (lambda c=c, p=p: lambda e:
                                    e.dma_start(x_sb[:, c * P:(c + 1) * P], xin[p, c]))(),
                           inc="y", n=16)
                marks.append(("y", m))
            return marks

        x_marks0 = emit_xload(0)
        prelude()
        S.barrier()
        npass = NPASS if stage == "full" else 1
        for p in range(npass):
            x_marks = x_marks0 if p == 0 else emit_xload(p)
            if p == 0:
                tap("xload", lambda: x_sb[:])
            gate_tm = emit_gate_mms([[xm] for xm in x_marks])
            vb = emit_ln(x_marks)
            if stage == "ln1":
                lm = emit_ln_apply()
                tap("lnb", lambda: lnb[:])
                tap("rb", lambda: rb[:])
                tap("nb", lambda: nb[:])
                break
            if stage == "gate":
                emit_gate(vb, gate_tm)
                lm = emit_ln_apply()
                tap("lnb", lambda: lnb[:])
                tap("rb", lambda: rb[:])
                tap("nb", lambda: nb[:])
                tap("cw", lambda: cw_tm[:])
                tap("glog", lambda: gsc[:])
                tap("ltm", lambda: cwsc[:, 0:64])
                tap("ee", lambda: cwsc[:, 64:128])
                tap("inds", lambda: cwsc[:, 128:192])
                tap("m1m2", lambda: cwsc[:, 192:216])
                tap("gws", lambda: gwsum[:])
                break
            emit_ffn(vb, gate_tm)
            if p == 0:
                tap("cw", lambda: cw_tm[:])
                tap("x1", lambda: x_sb[:])
            if stage == "ffn1":
                break
            S.barrier()
            vb = emit_ln(("y", S.c["y"]))
            lm = emit_ln_apply()
            emit_attn(p, lm, vb)
            if p == 0:
                tap("qs", lambda: A16[:, QS_O:QS_O + 3 * P])
                tap("k5", lambda: k5[:])
                tap("v5", lambda: v5[:])
                tap("zs", lambda: lnb[:, ZS_O:ZS_O + 4 * P])
                tap("x2", lambda: x_sb[:])
            if stage == "attn":
                tap("wqst", lambda: A16[:, WQST_O:WQST_O + 16 * 512])
                tap("bqs", lambda: bias_bf[0:1, 0:1536])
                break
            S.barrier()
            gate_tm = emit_gate_mms([[] for _ in range(NK)])
            vb = emit_ln(("y", S.c["y"]))
            emit_ffn(vb, gate_tm)
            S.barrier()
            emit_out(p)
            S.barrier()
        S.barrier()
        S.emit("sync", None, waits=[("y", S.c["y"])])
        S.check_deadlock()

        @block.tensor
        def _(e):
            S.replay("tensor", e, sems)

        @block.vector
        def _(e):
            S.replay("vector", e, sems)

        @block.scalar
        def _(e):
            S.replay("scalar", e, sems)

        @block.gpsimd
        def _(e):
            S.replay("gpsimd", e, sems)

        @block.sync
        def _(e):
            S.replay("sync", e, sems)

    return nc


def prep_inputs(L0_attn_x, L0_attn_WQ, L0_attn_bQ, L0_attn_WK, L0_attn_bK,
                L0_attn_WV, L0_attn_bV, L0_attn_proj, L0_ffn_gate_w,
                L0_ffn_w1, L0_ffn_w2, L0_ffn_sw1, L0_ffn_sw2, L0_ffn_sw3):
    C = np.ascontiguousarray
    f = np.float32
    w1 = np.asarray(L0_ffn_w1, f)
    w2 = np.asarray(L0_ffn_w2, f)
    shared = {
        "w1t": C(w1.reshape(NE, NK, 128, NM1, 128).transpose(0, 3, 2, 1, 4)
                 .reshape(NE, NM1, 128, NK * 128)),
        "w2t": C(w2.reshape(NE, NJ, 128, NM2, 128).transpose(0, 3, 2, 1, 4)
                 .reshape(NE, NM2, 128, NJ * 128)),
        "sw1t": C(np.asarray(L0_ffn_sw1, f).reshape(NJ, 128, NK, 128)
                  .transpose(0, 3, 2, 1).reshape(NJ, 128, NK * 128)),
        "sw2t": C(np.asarray(L0_ffn_sw2, f).reshape(NJ, 128, NK, 128)
                  .transpose(0, 3, 2, 1).reshape(NJ, 128, NK * 128)),
        "sw3t": C(np.asarray(L0_ffn_sw3, f).reshape(NM2, 128, NJ, 128)
                  .transpose(0, 3, 2, 1).reshape(NM2, 128, NJ * 128)),
        "gwt_in": C(np.asarray(L0_ffn_gate_w, f).T.reshape(NK, 128, NE)
                    .transpose(1, 0, 2).reshape(128, NK * NE)),
        "wqT_in": C(np.asarray(L0_attn_WQ, f).T.reshape(NK, 128, D)),
        "wkT_in": C(np.asarray(L0_attn_WK, f).T.reshape(NK, 128, QD)),
        "wvT_in": C(np.asarray(L0_attn_WV, f).T.reshape(NK, 128, QD)),
        "projt_in": C(np.asarray(L0_attn_proj, f).reshape(NK, 128, D)),
        "bq_in": C(np.asarray(L0_attn_bQ, f).reshape(1, D)),
        "bk_in": C(np.asarray(L0_attn_bK, f).reshape(1, QD)),
        "bv_in": C(np.asarray(L0_attn_bV, f).reshape(1, QD)),
        "cmask_in": C((np.triu(np.ones((128, 128), f), 1) * NEG).astype(f)),
        "ident_in": np.eye(128, dtype=f),
    }
    x = np.asarray(L0_attn_x, f)
    in_maps = []
    for b in range(NCORE):
        xT = C(x[b].T)
        xr = C(xT.reshape(NK, 128, NPASS, P).transpose(2, 0, 1, 3))
        m = dict(shared)
        m["xin"] = xr
        in_maps.append(m)
    return in_maps


def kernel(**inputs):
    in_maps = prep_inputs(**inputs)
    nc = build_nc()
    res = run_bass_kernel_spmd(nc, in_maps, core_ids=list(range(NCORE)))
    out = np.stack([res.results[i]["yout"] for i in range(NCORE)], axis=0)
    return out


# version 09
# speedup vs baseline: 1.2032x; 1.0240x over previous
import numpy as np
import concourse.bass as bass
import concourse.mybir as mybir
from concourse.bass_utils import run_bass_kernel_spmd

F32 = mybir.dt.float32
BF16 = mybir.dt.bfloat16
AF = mybir.ActivationFunctionType
ALU = mybir.AluOpType
AX = mybir.AxisListType

NCORE = 8
D = 2048
T = 2048
P = 1024
NPASS = 2
NK = 16
NE = 8
NJ = 11
NM1 = 22
NM2 = 16
QD = 512
SQRT_HS = float(np.sqrt(np.float32(128.0)))
EPS = 1e-5
NEG = -1.0e30

ENGINES = ("tensor", "vector", "scalar", "gpsimd", "sync")
SEMS = ("t", "v", "s", "w", "y", "g")


class Sched:
    def __init__(self):
        self.items = {e: [] for e in ENGINES}
        self.c = {s: 0 for s in SEMS}
        self._waited = {e: {s: 0 for s in SEMS} for e in ENGINES}

    def emit(self, eng, fn, waits=(), inc=None, n=1):
        if eng == "vector":
            waits = list(waits) + [("v", self.c["v"])]
        elif eng == "scalar":
            waits = list(waits) + [("s", self.c["s"])]
        wl = []
        for s, v in waits:
            if v is None or v <= 0:
                continue
            if v > self._waited[eng][s]:
                self._waited[eng][s] = v
                wl.append((s, v))
        if inc is not None:
            self.c[inc] += n
        self.items[eng].append((fn, tuple(wl), inc, n))
        return self.c[inc] if inc is not None else None

    def barrier(self):
        snap = dict(self.c)
        for e in ENGINES:
            w = [(s, snap[s]) for s in SEMS if snap[s] > self._waited[e][s]]
            if w:
                self.emit(e, None, waits=w)

    def check_deadlock(self):
        import collections
        pos = {e: 0 for e in ENGINES}
        cnt = {s: 0 for s in SEMS}
        n = sum(len(v) for v in self.items.values())
        done = 0
        while done < n:
            progressed = False
            for e in ENGINES:
                while pos[e] < len(self.items[e]):
                    fn, waits, inc, k = self.items[e][pos[e]]
                    if all(cnt[s] >= v for s, v in waits):
                        if inc is not None:
                            cnt[inc] += k
                        pos[e] += 1
                        done += 1
                        progressed = True
                    else:
                        break
            if not progressed:
                for e in ENGINES:
                    if pos[e] < len(self.items[e]):
                        fn, waits, inc, k = self.items[e][pos[e]]
                        print(f"STUCK {e} at item {pos[e]}: waits {waits}, have "
                              f"{[(s, cnt[s]) for s, v in waits]}")
                raise RuntimeError("schedule deadlock")
        return True

    def replay(self, eng_name, eng, sems):
        for fn, waits, inc, n in self.items[eng_name]:
            for s, v in waits:
                eng.wait_ge(sems[s], v)
            if fn is None:
                continue
            inst = fn(eng)
            if inc is not None:
                inst.then_inc(sems[inc], n)


def build_nc(taps=(), stage="full"):
    taps = set(taps)
    nc = bass.Bass("TRN2", target_bir_lowering=False)

    xin = nc.dram_tensor("xin", [NPASS, NK, 128, P], F32, kind="ExternalInput").ap()
    w1t = nc.dram_tensor("w1t", [NE, NM1, 128, NK * 128], F32, kind="ExternalInput").ap()
    w2t = nc.dram_tensor("w2t", [NE, NM2, 128, NJ * 128], F32, kind="ExternalInput").ap()
    sw1t = nc.dram_tensor("sw1t", [NJ, 128, NK * 128], F32, kind="ExternalInput").ap()
    sw2t = nc.dram_tensor("sw2t", [NJ, 128, NK * 128], F32, kind="ExternalInput").ap()
    sw3t = nc.dram_tensor("sw3t", [NM2, 128, NJ * 128], F32, kind="ExternalInput").ap()
    gwt_in = nc.dram_tensor("gwt_in", [128, NK * NE], F32, kind="ExternalInput").ap()
    wqT_in = nc.dram_tensor("wqT_in", [NK, 128, D], F32, kind="ExternalInput").ap()
    wkT_in = nc.dram_tensor("wkT_in", [NK, 128, QD], F32, kind="ExternalInput").ap()
    wvT_in = nc.dram_tensor("wvT_in", [NK, 128, QD], F32, kind="ExternalInput").ap()
    projt_in = nc.dram_tensor("projt_in", [NK, 128, D], F32, kind="ExternalInput").ap()
    bq_in = nc.dram_tensor("bq_in", [1, D], F32, kind="ExternalInput").ap()
    bk_in = nc.dram_tensor("bk_in", [1, QD], F32, kind="ExternalInput").ap()
    bv_in = nc.dram_tensor("bv_in", [1, QD], F32, kind="ExternalInput").ap()
    cmask_in = nc.dram_tensor("cmask_in", [128, 128], F32, kind="ExternalInput").ap()
    ident_in = nc.dram_tensor("ident_in", [128, 128], F32, kind="ExternalInput").ap()
    yout = nc.dram_tensor("yout", [T, D], F32, kind="ExternalOutput").ap()

    tap_drams = {}
    def tapd(name, shape):
        if name in taps:
            tap_drams[name] = nc.dram_tensor("tap_" + name, list(shape), F32,
                                             kind="ExternalOutput").ap()
    tapd("lnb", (128, NK * P)); tapd("rb", (128, P)); tapd("nb", (128, P))
    tapd("cw", (128, 64)); tapd("act", (128, NJ * P)); tapd("x1", (128, NK * P))
    tapd("x2", (128, NK * P)); tapd("qs", (128, 3 * P)); tapd("k5", (128, 4 * T))
    tapd("v5", (128, 16 * QD)); tapd("zs", (128, 4 * P)); tapd("xload", (128, NK * P))
    tapd("glog", (128, P)); tapd("ltm", (128, 64)); tapd("ee", (128, 64))
    tapd("wqst", (128, 16 * 512)); tapd("bqs", (1, 1536))
    tapd("inds", (128, 64)); tapd("m1m2", (128, 24)); tapd("gws", (128, 1))

    from contextlib import ExitStack
    with ExitStack() as stack:
        E = stack.enter_context

        x_sb = E(nc.sbuf_tensor("x_sb", [128, NK * P], F32))
        lnb = E(nc.sbuf_tensor("lnb", [128, NK * P], BF16))
        k5 = E(nc.sbuf_tensor("k5", [128, 4 * T], BF16))
        v5 = E(nc.sbuf_tensor("v5", [128, 16 * QD], BF16))
        rb = E(nc.sbuf_tensor("rb", [128, P], F32))
        nb = E(nc.sbuf_tensor("nb", [128, P], F32))
        xb = E(nc.sbuf_tensor("xb", [128, 2 * P], BF16))
        gsc = E(nc.sbuf_tensor("gsc", [128, P], F32))
        wkv = gsc.bitcast(BF16)
        cw_tm = E(nc.sbuf_tensor("cw_tm", [128, 64], F32))
        gwt = E(nc.sbuf_tensor("gwt", [128, NK * NE], F32))
        gwsum = E(nc.sbuf_tensor("gwsum", [128, 1], F32))
        identb = E(nc.sbuf_tensor("identb", [128, 128], BF16))
        identf = E(nc.sbuf_tensor("identf", [128, 128], F32))
        cmaskb = E(nc.sbuf_tensor("cmaskb", [128, 128], BF16))
        bias_bf = E(nc.sbuf_tensor("bias_bf", [1, 1536], BF16))
        ones_row_bf = E(nc.sbuf_tensor("ones_row_bf", [1, P], BF16))
        ones_col_bf = E(nc.sbuf_tensor("ones_col_bf", [128, 1], BF16))
        ones1_f = E(nc.sbuf_tensor("ones1_f", [1, 128], F32))
        ones128_f = E(nc.sbuf_tensor("ones128_f", [128, 1], F32))
        eps_col = E(nc.sbuf_tensor("eps_col", [128, 1], F32))
        smx = E(nc.sbuf_tensor("smx", [128, 8], F32))

        A16 = E(nc.sbuf_tensor("A16", [128, 20224], BF16))
        A32 = E(nc.sbuf_tensor("A32", [128, 4096], F32))
        cwsc = A32[:, 0:224]
        bqstage = A16[0:1, 0:2048]

        PS = [E(nc.psum_tensor(f"PS{i}", [128, 1024], F32)) for i in range(4)]
        sems = {s: E(nc.semaphore("sem_" + s)) for s in SEMS}
        block = E(nc.Block())
        S = Sched()

        ACT_O = 0
        W1R_O = 11264
        W1R_SLOTS = 3
        W2R_O = W1R_O + W1R_SLOTS * 2048
        W2R_SLOTS = 2
        WKR_O = W2R_O
        WVR_O = W2R_O + 512
        QS3_O = W2R_O + 1024
        ZR_O = W2R_O + 2048
        WQST_O = 0
        QS_O = 8192
        PB_O = 0
        PT_O = 2048
        ZS_O = 4096
        PJ_O = 8192
        SG_O = 0
        CWB_O = 2048
        TMP_O = 0
        OUT_O = 0

        def a16(off, n):
            return A16[:, off:off + n]

        def qs_ap(m, lo=0, n_=P):
            off = (QS_O + m * P) if m < 3 else QS3_O
            return A16[:, off + lo: off + lo + n_]

        def a32(off, n):
            return A32[:, off:off + n]

        def xc(c, lo=0, hi=P):
            return x_sb[:, c * P + lo: c * P + hi]

        def lnc(c, lo=0, hi=P):
            return lnb[:, c * P + lo: c * P + hi]

        bqs_ap = bias_bf[0:1, 0:512]
        bk_ap = bias_bf[0:1, 512:1024]
        bv_ap = bias_bf[0:1, 1024:1536]

        def prelude():
            S.emit("gpsimd", lambda e: e.memset(ones_row_bf[:], 1.0), inc="g")
            S.emit("gpsimd", lambda e: e.memset(ones_col_bf[:], 1.0), inc="g")
            S.emit("gpsimd", lambda e: e.memset(ones1_f[:], 1.0), inc="g")
            S.emit("gpsimd", lambda e: e.memset(ones128_f[:], 1.0), inc="g")
            S.emit("gpsimd", lambda e: e.memset(eps_col[:], EPS), inc="g")
            S.emit("gpsimd", lambda e: e.dma_start(identb[:], ident_in), inc="w", n=16)
            S.emit("gpsimd", lambda e: e.dma_start(identf[:], ident_in), inc="w", n=16)
            S.emit("gpsimd", lambda e: e.dma_start(cmaskb[:], cmask_in), inc="w", n=16)
            S.emit("gpsimd", lambda e: e.dma_start(gwt[:], gwt_in), inc="w", n=16)
            S.emit("gpsimd", lambda e: e.dma_start(bqstage, bq_in), inc="w", n=16)
            S.emit("gpsimd", lambda e: e.dma_start(bias_bf[0:1, 512:1024], bk_in), inc="w", n=16)
            w_mark = S.emit("gpsimd", lambda e: e.dma_start(bias_bf[0:1, 1024:1536], bv_in),
                            inc="w", n=16)
            tm = None
            for r in range(4):
                tm = S.emit("tensor", (lambda r=r: lambda e:
                                       e.matmul(PS[3][0:1, 0:512], ones_row_bf[0:1, 0:1],
                                                bqstage[0:1, r * 512:(r + 1) * 512],
                                                start=(r == 0), stop=(r == 3)))(),
                            waits=[("w", w_mark)], inc="t")
            S.emit("vector", lambda e: e.tensor_copy(bias_bf[0:1, 0:512], PS[3][0:1, 0:512]),
                   waits=[("t", tm)], inc="v")
            tm2 = None
            for c in range(NK):
                tm2 = S.emit("tensor", (lambda c=c: lambda e:
                                        e.matmul(PS[2][0:8, 0:1], gwt[:, c * NE:(c + 1) * NE],
                                                 ones128_f[:], start=(c == 0),
                                                 stop=(c == NK - 1)))(),
                             waits=[("w", w_mark)], inc="t")
            S.emit("vector", lambda e: e.tensor_copy(gwsum[0:8, 0:1], PS[2][0:8, 0:1]),
                   waits=[("t", tm2)], inc="v")

        def tap(name, src_ap_fn):
            if name not in tap_drams:
                return
            dst = tap_drams[name]
            S.barrier()
            S.emit("gpsimd", (lambda dst=dst, f=src_ap_fn: lambda e: e.dma_start(dst, f()))(),
                   inc="w", n=16)
            S.barrier()

        lnst = {"tm": [None] * NK}

        def emit_ln_stats_chunk(c, xw):
            xbv = xb[:, (c % 2) * P:(c % 2 + 1) * P]
            w = list(xw)
            if c >= 2 and lnst["tm"][c - 2] is not None:
                w.append(("t", lnst["tm"][c - 2]))
            cast_m = S.emit("scalar", (lambda xbv=xbv, c=c: lambda e: e.copy(xbv, xc(c)))(),
                            waits=w, inc="s")
            tsum = None
            for n in range(2):
                tsum = S.emit("tensor", (lambda xbv=xbv, n=n, c=c: lambda e:
                                         e.matmul(PS[3][0:1, n * 512:(n + 1) * 512],
                                                  ones_col_bf[:],
                                                  xbv[:, n * 512:(n + 1) * 512],
                                                  start=(c == 0), stop=(c == NK - 1)))(),
                              waits=[("s", cast_m)], inc="t" if n == 1 else None)
            sm = S.emit("scalar", (lambda xbv=xbv: lambda e:
                                   e.activation(xbv, xbv, AF.Square))(),
                        waits=[("t", tsum)], inc="s")
            for n in range(2):
                lnst["tm"][c] = S.emit("tensor", (lambda xbv=xbv, n=n, c=c: lambda e:
                                                  e.matmul(PS[2][0:1, n * 512:(n + 1) * 512],
                                                           ones_col_bf[:],
                                                           xbv[:, n * 512:(n + 1) * 512],
                                                           start=(c == 0),
                                                           stop=(c == NK - 1)))(),
                                       waits=[("s", sm)], inc="t" if n == 1 else None)
            return lnst["tm"][c]

        def emit_ln(x_ready):
            for c in range(NK):
                w = [x_ready[c] if isinstance(x_ready, list) else x_ready]
                emit_ln_stats_chunk(c, w)
            return emit_ln_tail()

        def emit_ln_tail():
            tm_stats = lnst["tm"][NK - 1]
            S1 = A32[0:1, 0:P]
            S2 = A32[0:1, P:2 * P]
            S3 = A32[0:1, 2 * P:3 * P]
            S4 = A32[0:1, 3 * P:4 * P]
            vm_sums = S.emit("vector", lambda e: e.tensor_copy(S1, PS[3][0:1, 0:P]),
                             waits=[("t", tm_stats)], inc="v")
            vm_sumsq = S.emit("vector", lambda e: e.tensor_copy(S2, PS[2][0:1, 0:P]), inc="v")
            vm_m = S.emit("vector", lambda e: e.tensor_scalar_mul(S3, S1, 1.0 / D), inc="v")
            sm = S.emit("scalar", lambda e: e.activation(S4, S3, AF.Square),
                        waits=[("v", vm_m)], inc="s")
            vm_var = S.emit("vector", lambda e: e.scalar_tensor_tensor(S2, S2, 1.0 / D, S4,
                                                                      ALU.mult, ALU.subtract),
                            waits=[("s", sm)], inc="v")
            sm = S.emit("scalar", lambda e: e.activation(S4, S2, AF.Ln, bias=eps_col[0:1, 0:1]),
                        waits=[("v", vm_var)], inc="s")
            sm_r = S.emit("scalar", lambda e: e.activation(S1, S4, AF.Exp, scale=-0.5),
                          waits=[("v", vm_m)], inc="s")
            vm_nmr = S.emit("vector", lambda e: e.scalar_tensor_tensor(S3, S3, -1.0, S1,
                                                                      ALU.mult, ALU.mult),
                            waits=[("s", sm_r)], inc="v")
            for n in range(2):
                S.emit("tensor", (lambda n=n, S1=S1: lambda e:
                                  e.matmul(PS[1][:, n * 512:(n + 1) * 512], ones1_f[:],
                                           S1[0:1, n * 512:(n + 1) * 512],
                                           start=True, stop=True))(),
                       waits=[("s", sm_r)], inc=None)
            tm = None
            for n in range(2):
                tm = S.emit("tensor", (lambda n=n, S3=S3: lambda e:
                                       e.matmul(PS[2][:, n * 512:(n + 1) * 512], ones1_f[:],
                                                S3[0:1, n * 512:(n + 1) * 512],
                                                start=True, stop=True))(),
                            waits=[("v", vm_nmr), ("v", vm_sumsq)], inc="t" if n == 1 else None)
            S.emit("vector", lambda e: e.tensor_copy(rb[:], PS[1][:]), waits=[("t", tm)], inc="v")
            vb = S.emit("vector", lambda e: e.tensor_copy(nb[:], PS[2][:]), inc="v")
            return vb

        def emit_ln_apply():
            lnb_marks = [None] * NK
            tmpv = a32(TMP_O, P)
            for c in range(NK):
                S.emit("vector", (lambda c=c, tmpv=tmpv: lambda e:
                                  e.tensor_mul(tmpv, xc(c), rb[:]))(), inc="v")
                lnb_marks[c] = S.emit("vector", (lambda c=c, tmpv=tmpv: lambda e:
                                                 e.tensor_add(lnc(c), tmpv, nb[:]))(), inc="v")
            return lnb_marks

        w1ring = [("g", 0)] * W1R_SLOTS
        w2ring = [("g", 0)] * W2R_SLOTS
        ring_idx = {"w1": 0, "w2": 0}

        def w1_slot_ap(i):
            return a16(W1R_O + i * 2048, 2048)

        def w2_slot_ap(i):
            return a16(W2R_O + i * 1408, 1408)

        def dma_w1_block(src_ap):
            i = ring_idx["w1"] % W1R_SLOTS
            ring_idx["w1"] += 1
            dst = w1_slot_ap(i)
            wm = S.emit("gpsimd", (lambda dst=dst, src=src_ap: lambda e:
                                   e.dma_start(dst, src))(),
                        waits=[w1ring[i]], inc="w", n=16)
            return i, wm

        def dma_w2_block(src_ap):
            i = ring_idx["w2"] % W2R_SLOTS
            ring_idx["w2"] += 1
            dst = w2_slot_ap(i)
            wm = S.emit("gpsimd", (lambda dst=dst, src=src_ap: lambda e:
                                   e.dma_start(dst, src))(),
                        waits=[w2ring[i]], inc="w", n=16)
            return i, wm

        def emit_gate_mms(x_waits):
            tm = None
            for c in range(NK):
                for n in range(2):
                    w = list(x_waits[c]) if n == 0 else []
                    last = (c == NK - 1 and n == 1)
                    tm = S.emit("tensor", (lambda c=c, n=n: lambda e:
                                           e.matmul(PS[0][0:8, n * 512:(n + 1) * 512],
                                                    gwt[:, c * NE:(c + 1) * NE],
                                                    xc(c, n * 512, (n + 1) * 512),
                                                    start=(c == 0), stop=(c == NK - 1)))(),
                                waits=w, inc="t" if last else None)
            return tm

        def emit_gate(lnb_vb, gate_tm):
            g0 = gsc[0:8, 0:P]
            g1 = A32[0:8, 0:P]
            ps0_free = S.emit("vector",
                              lambda e: e.tensor_mul(A32[0:8, P:2 * P], PS[0][0:8, 0:P],
                                                     rb[0:8, 0:P]),
                              waits=[("t", gate_tm)], inc="v")
            S.emit("vector", lambda e: e.tensor_scalar_mul(g1, nb[0:8, 0:P], gwsum[0:8, 0:1]),
                   inc="v")
            t1_mark = S.emit("vector", lambda e: e.tensor_add(g0, A32[0:8, P:2 * P], g1), inc="v")
            vm = t1_mark
            tm = None
            for tt in range(8):
                tm = S.emit("tensor", (lambda tt=tt: lambda e:
                                       e.transpose(PS[2][:, tt * 8:(tt + 1) * 8],
                                                   g0[0:8, tt * 128:(tt + 1) * 128],
                                                   identf[0:8, 0:8]))(),
                            waits=[("v", vm)], inc="t" if tt == 7 else None)
            L = cwsc[:, 0:64]
            E_ = cwsc[:, 64:128]
            SCR = cwsc[:, 128:192]
            mA = smx[:, 0:8]
            mB = cwsc[:, 192:200]
            mC = cwsc[:, 200:208]
            vm = S.emit("vector", lambda e: e.tensor_copy(L, PS[2][:, 0:64]),
                        waits=[("t", tm)], inc="v")
            L3 = L.rearrange("p (a b) -> p a b", b=8)
            E3 = E_.rearrange("p (a b) -> p a b", b=8)
            S3 = SCR.rearrange("p (a b) -> p a b", b=8)
            mA3 = mA.rearrange("p (a one) -> p a one", one=1)
            mB3 = mB.rearrange("p (a one) -> p a one", one=1)
            mC3 = mC.rearrange("p (a one) -> p a one", one=1)
            def bcast(small3, big3):
                b1, _ = bass.broadcast_tensor_aps(small3, big3)
                return b1
            S.emit("vector", lambda e: e.reduce_max(mA, L3, axis=AX.X, negate=True), inc="v")
            vm = S.emit("vector", lambda e: e.tensor_tensor(L3, L3, bcast(mA3, L3), ALU.add),
                        inc="v")
            sm = S.emit("scalar", lambda e: e.activation(E_, L, AF.Exp), waits=[("v", vm)], inc="s")
            S.emit("vector", lambda e: e.reduce_max(mA, E3, axis=AX.X), waits=[("s", sm)], inc="v")
            S.emit("vector", lambda e: e.tensor_tensor(S3, E3, bcast(mA3, E3), ALU.is_ge), inc="v")
            S.emit("vector", lambda e: e.tensor_scalar(SCR, SCR, -1.0, 1.0, ALU.mult, ALU.add),
                   inc="v")
            S.emit("vector", lambda e: e.tensor_tensor(SCR, E_, SCR, ALU.mult), inc="v")
            S.emit("vector", lambda e: e.reduce_max(mB, S3, axis=AX.X), inc="v")
            S.emit("vector", lambda e: e.tensor_tensor(S3, E3, bcast(mB3, E3), ALU.is_ge), inc="v")
            S.emit("vector", lambda e: e.scalar_tensor_tensor(mC, mA, 1e-20, mB, ALU.add, ALU.add),
                   inc="v")
            mD = cwsc[:, 208:216]
            mD3 = mD.rearrange("p (a one) -> p a one", one=1)
            S.emit("vector", lambda e: e.reciprocal(mD, mC), inc="v")
            S.emit("vector", lambda e: e.tensor_tensor(SCR, E_, SCR, ALU.mult), inc="v")
            vm = S.emit("vector", lambda e: e.tensor_tensor(
                cw_tm[:].rearrange("p (a b) -> p a b", b=8), S3, bcast(mD3, S3), ALU.mult), inc="v")
            return vm, ps0_free

        cwrow = gsc[0:1, 0:P]

        def emit_cwb_tr(e_idx, wait_marks):
            tm = None
            for tt in range(8):
                col = tt * 8 + e_idx
                w = list(wait_marks) if tt == 0 else []
                tm = S.emit("tensor", (lambda col=col, tt=tt: lambda e:
                                       e.transpose(PS[3][0:1, tt * 128:(tt + 1) * 128],
                                                   cw_tm[:, col:col + 1], identf[:]))(),
                            waits=w, inc="t" if tt == 7 else None)
            vm = S.emit("vector", (lambda: lambda e:
                                   e.tensor_copy(cwrow, PS[3][0:1, 0:P]))(),
                        waits=[("t", tm)], inc="v")
            return vm

        def emit_cwb_bc(e_idx, vm):
            tm = None
            for n in range(2):
                tm = S.emit("tensor", (lambda n=n: lambda e:
                                       e.matmul(PS[3][:, n * 512:(n + 1) * 512], ones1_f[:],
                                                cwrow[0:1, n * 512:(n + 1) * 512],
                                                start=True, stop=True))(),
                            waits=[("v", vm)] if n == 0 else [], inc="t" if n == 1 else None)
            dstv = a32(CWB_O + (e_idx % 2) * P, P)
            vm2 = S.emit("vector", (lambda dstv=dstv: lambda e:
                                    e.tensor_copy(dstv, PS[3][:]))(),
                         waits=[("t", tm)], inc="v")
            return vm2

        def emit_cwb_prep(e_idx, wait_marks):
            return emit_cwb_bc(e_idx, emit_cwb_tr(e_idx, wait_marks))

        def emit_expert_w1(eidx, lnb_marks, cwb_mark, psum_free, shared=False, inject=None):
            act_marks = [None] * NJ
            last_sg = None
            last_tt1 = None
            tmk = None
            for j in range(NJ):
                if shared:
                    blocks = [(sw1t[j], 0), (sw2t[j], 1)]
                else:
                    blocks = [(w1t[eidx, j], 0), (w1t[eidx, NJ + j], 1)]
                tm_gu = [None, None]
                for src, gu in blocks:
                    slot, wm = dma_w1_block(src)
                    slotap = w1_slot_ap(slot)
                    first = True
                    for k in range(NK):
                        for n in range(2):
                            w = [("v", lnb_marks[k])]
                            if first:
                                w.append(("w", wm))
                                if j == 0 and psum_free.get(gu):
                                    w.append(psum_free[gu])
                                if gu == 0 and last_sg is not None:
                                    w.append(("s", last_sg))
                                if gu == 1 and last_tt1 is not None:
                                    w.append(("v", last_tt1))
                                first = False
                            last = (k == NK - 1 and n == 1)
                            tmk = S.emit("tensor",
                                         (lambda gu=gu, n=n, k=k, slotap=slotap: lambda e:
                                          e.matmul(PS[gu][:, n * 512:(n + 1) * 512],
                                                   slotap[:, k * 128:(k + 1) * 128],
                                                   lnc(k, n * 512, (n + 1) * 512),
                                                   start=(k == 0), stop=(k == NK - 1)))(),
                                         waits=w, inc="t" if last else None)
                    w1ring[slot] = ("t", tmk)
                    tm_gu[gu] = tmk
                sgv = a32(SG_O + (j % 2) * P, P)
                last_sg = S.emit("scalar", (lambda sgv=sgv: lambda e:
                                            e.activation(sgv, PS[0][:], AF.Silu))(),
                                 waits=[("t", tm_gu[0])], inc="s")
                if shared:
                    last_tt1 = S.emit("vector", (lambda sgv=sgv, j=j: lambda e:
                                                 e.tensor_mul(a16(ACT_O + j * P, P), sgv, PS[1][:]))(),
                                      waits=[("s", last_sg), ("t", tm_gu[1])], inc="v")
                    act_marks[j] = last_tt1
                else:
                    last_tt1 = S.emit("vector", (lambda sgv=sgv: lambda e:
                                                 e.tensor_mul(PS[2][:], sgv, PS[1][:]))(),
                                      waits=[("s", last_sg), ("t", tm_gu[1])], inc="v")
                    cwbv = a32(CWB_O + (eidx % 2) * P, P)
                    w2_ = [("v", cwb_mark)] if (j == 0 and cwb_mark) else []
                    act_marks[j] = S.emit("vector", (lambda j=j, cwbv=cwbv: lambda e:
                                                     e.tensor_mul(a16(ACT_O + j * P, P),
                                                                  PS[2][:], cwbv))(),
                                          waits=w2_, inc="v")
                if inject and j in inject:
                    inject[j]()
            return act_marks, last_sg, last_tt1

        def emit_expert_w2(eidx, act_marks, w1_last_sg, w1_last_tt1, shared=False,
                           ln_stats=False):
            prev_stt = [None, None]
            stts = [None] * NM2
            for m in range(NM2):
                psum = PS[m % 2]
                src = sw3t[m] if shared else w2t[eidx, m]
                slot, wm = dma_w2_block(src)
                slotap = w2_slot_ap(slot)
                first = True
                tmk = None
                for j in range(NJ):
                    for n in range(2):
                        w = [("v", act_marks[j])]
                        if first:
                            w.append(("w", wm))
                            if prev_stt[m % 2] is not None:
                                w.append(("v", prev_stt[m % 2]))
                            if m == 0:
                                w.append(("s", w1_last_sg))
                            if m == 1:
                                w.append(("v", w1_last_tt1))
                            first = False
                        last = (j == NJ - 1 and n == 1)
                        tmk = S.emit("tensor",
                                     (lambda n=n, j=j, slotap=slotap, psum=psum: lambda e:
                                      e.matmul(psum[:, n * 512:(n + 1) * 512],
                                               slotap[:, j * 128:(j + 1) * 128],
                                               a16(ACT_O + j * P + n * 512, 512),
                                               start=(j == 0), stop=(j == NJ - 1)))(),
                                     waits=w, inc="t" if last else None)
                w2ring[slot] = ("t", tmk)
                prev_stt[m % 2] = S.emit("vector", (lambda m=m, psum=psum: lambda e:
                                                    e.scalar_tensor_tensor(xc(m), psum[:], 0.5,
                                                                           xc(m), ALU.mult,
                                                                           ALU.add))(),
                                         waits=[("t", tmk)], inc="v")
                stts[m] = prev_stt[m % 2]
                if ln_stats:
                    emit_ln_stats_chunk(m, [("v", stts[m])])
            return prev_stt, stts

        def emit_ffn(lnb_vb, gate_tm, ln_stats=False):
            cw_mark, ps0_free = emit_gate(lnb_vb, gate_tm)
            cwb_marks = {0: emit_cwb_prep(0, [("v", cw_mark)])}
            lnb_marks = emit_ln_apply()
            psum_free = {0: ("v", ps0_free)}
            cwb_tr = {}
            for e_ in range(NE):
                inj = {}
                if e_ + 1 < NE:
                    e1 = e_ + 1

                    def inj1(e1=e1):
                        cwb_tr[e1] = emit_cwb_tr(e1, [("v", cwb_marks[e1 - 1])])

                    def inj3(e1=e1):
                        cwb_marks[e1] = emit_cwb_bc(e1, cwb_tr[e1])

                    inj = {1: inj1, 3: inj3}
                am, lsg, ltt = emit_expert_w1(e_, lnb_marks, cwb_marks[e_], psum_free,
                                              inject=inj)
                stt, _ = emit_expert_w2(e_, am, lsg, ltt)
                psum_free = {0: ("v", stt[0]), 1: ("v", stt[1])}
            am, lsg, ltt = emit_expert_w1(0, lnb_marks, None, psum_free, shared=True)
            emit_expert_w2(0, am, lsg, ltt, shared=True, ln_stats=ln_stats)

        def emit_attn(p, lnb_marks, lnb_vb):
            wq_marks = [None] * NK
            for k in range(NK):
                slot, wm = dma_w1_block(wqT_in[k])
                slotap = w1_slot_ap(slot)
                ph = (k % 2) * 512
                tmk = None
                for r in range(4):
                    w = [("w", wm)] if r == 0 else []
                    if r == 0:
                        w.append(("v", lnb_vb if k < 2 else wq_marks[k - 2]))
                    tmk = S.emit("tensor", (lambda r=r, slotap=slotap, ph=ph: lambda e:
                                            e.matmul(PS[2][:, ph:ph + 512], identb[:],
                                                     slotap[:, r * 512:(r + 1) * 512],
                                                     start=(r == 0), stop=(r == 3)))(),
                                 waits=w, inc="t" if r == 3 else None)
                w1ring[slot] = ("t", tmk)
                wq_marks[k] = S.emit("vector", (lambda k=k, ph=ph: lambda e:
                                               e.tensor_copy(a16(WQST_O + k * 512, 512),
                                                             PS[2][:, ph:ph + 512]))(),
                                     waits=[("t", tmk)], inc="v")
            qs_evacs = [None] * 4
            qs_tmk3 = None
            for m in range(4):
                psum = PS[m % 2]
                tmk = None
                for k in range(NK):
                    for n in range(2):
                        w = [("v", wq_marks[k]), ("v", lnb_marks[k])]
                        if k == 0 and n == 0 and m >= 2:
                            w.append(("s", qs_evacs[m - 2]))
                        tmk = S.emit("tensor", (lambda m=m, k=k, n=n, psum=psum: lambda e:
                                                e.matmul(psum[:, n * 512:(n + 1) * 512],
                                                         a16(WQST_O + k * 512 + m * 128, 128),
                                                         lnc(k, n * 512, (n + 1) * 512),
                                                         start=(k == 0), stop=False))(),
                                     waits=w, inc=None)
                for n in range(2):
                    tmk = S.emit("tensor", (lambda m=m, n=n, psum=psum: lambda e:
                                            e.matmul(psum[:, n * 512:(n + 1) * 512],
                                                     bqs_ap[0:1, m * 128:(m + 1) * 128],
                                                     ones_row_bf[0:1, n * 512:(n + 1) * 512],
                                                     start=False, stop=True))(),
                                 inc="t" if n == 1 else None)
                qs_evacs[m] = S.emit("scalar", (lambda m=m, psum=psum: lambda e:
                                               e.copy(qs_ap(m), psum[:]))(),
                                     waits=[("t", tmk)], inc="s")
            qs_tmk3 = tmk
            wk_ring = [("g", 0), ("g", 0)]
            k5_last = None
            for k in range(NK):
                slot = k % 2
                dst = wkv[:, slot * 512:(slot + 1) * 512]
                wm = S.emit("gpsimd", (lambda dst=dst, k=k: lambda e:
                                       e.dma_start(dst, wkT_in[k]))(),
                            waits=[wk_ring[slot]], inc="w", n=16)
                for m in range(4):
                    for n in range(2):
                        w = [("w", wm), ("v", lnb_marks[k])]
                        if k == 0 and n == 0:
                            w.append(("s", qs_evacs[m] if m >= 2 else qs_evacs[3]))
                        k5_last = S.emit("tensor",
                                         (lambda k=k, m=m, n=n, dst=dst: lambda e:
                                          e.matmul(PS[m][:, n * 512:(n + 1) * 512],
                                                   dst[:, m * 128:(m + 1) * 128],
                                                   lnc(k, n * 512, (n + 1) * 512),
                                                   start=(k == 0), stop=False))(),
                                         waits=w, inc="t" if (m == 3 and n == 1) else None)
                wk_ring[slot] = ("t", k5_last)
            k5_evacs = [None] * 4
            for m in range(4):
                for n in range(2):
                    k5_last = S.emit("tensor", (lambda m=m, n=n: lambda e:
                                                e.matmul(PS[m][:, n * 512:(n + 1) * 512],
                                                         bk_ap[0:1, m * 128:(m + 1) * 128],
                                                         ones_row_bf[0:1, n * 512:(n + 1) * 512],
                                                         start=False, stop=True))(),
                                     inc="t" if n == 1 else None)
                k5_evacs[m] = S.emit("scalar", (lambda m=m, p=p: lambda e:
                                               e.copy(k5[:, m * T + p * P: m * T + (p + 1) * P],
                                                      PS[m][:]))(),
                                     waits=[("t", k5_last)], inc="s")
            wv_ring = [("g", 0), ("g", 0)]
            v5_last = None
            for k in range(NK):
                slot = k % 2
                dst = wkv[:, 1024 + slot * 512:1024 + (slot + 1) * 512]
                wm = S.emit("gpsimd", (lambda dst=dst, k=k: lambda e:
                                       e.dma_start(dst, wvT_in[k]))(),
                            waits=[wv_ring[slot]], inc="w", n=16)
                for tt in range(8):
                    psv = PS[tt // 2][:, (tt % 2) * 512:(tt % 2 + 1) * 512]
                    w = [("w", wm), ("v", lnb_marks[k])]
                    if k == 0:
                        w.append(("s", k5_evacs[tt // 2]))
                    v5_last = S.emit("tensor", (lambda k=k, tt=tt, dst=dst, psv=psv: lambda e:
                                                e.matmul(psv, lnc(k, tt * 128, (tt + 1) * 128),
                                                         dst[:], start=(k == 0), stop=False))(),
                                     waits=w, inc="t" if tt == 7 else None)
                wv_ring[slot] = ("t", v5_last)
            v5_evacs = [None] * 8
            for tt in range(8):
                psv = PS[tt // 2][:, (tt % 2) * 512:(tt % 2 + 1) * 512]
                v5_last = S.emit("tensor", (lambda tt=tt, psv=psv: lambda e:
                                            e.matmul(psv, ones_row_bf[0:1, 0:128],
                                                     bv_ap[0:1, 0:512],
                                                     start=False, stop=True))(), inc="t")
                v5_evacs[tt] = S.emit("scalar", (lambda tt=tt, psv=psv, p=p: lambda e:
                                                 e.copy(v5[:, (p * 8 + tt) * QD:(p * 8 + tt + 1) * QD],
                                                        psv))(),
                                      waits=[("t", v5_last)], inc="s")

            pbuf = lnb[:, PB_O:PB_O + 2048]
            ptbuf = lnb[:, PT_O + 0:PT_O + 2048]
            def zsv(kd, lo, n):
                return lnb[:, ZS_O + kd * P + lo: ZS_O + kd * P + lo + n]
            def pjv(kd, lo, n):
                return lnb[:, PJ_O + kd * 2048 + lo: PJ_O + kd * 2048 + lo + n]

            pj_copy = {0: None, 1: None}
            for jc in range(4):
                marks = [None, None]
                for r in range(4):
                    slot, wm = dma_w1_block(projt_in[r * 4 + jc])
                    slotap = w1_slot_ap(slot)
                    for half in range(2):
                        ps = PS[3 - half]
                        for pc in range(2):
                            w = []
                            if half == 0 and pc == 0:
                                w.append(("w", wm))
                            if r == 0 and pc == 0:
                                if jc == 0:
                                    w.append(("s", v5_evacs[7 - 2 * half]))
                                else:
                                    w.append(("v", pj_copy[half]))
                            marks[half] = S.emit("tensor",
                                         (lambda r=r, half=half, pc=pc, slotap=slotap, ps=ps: lambda e:
                                          e.matmul(ps[:, pc * 512:(pc + 1) * 512], identb[:],
                                                   slotap[:, half * 1024 + pc * 512:
                                                          half * 1024 + (pc + 1) * 512],
                                                   start=(r == 0), stop=(r == 3)))(),
                                         waits=w, inc="t" if pc == 1 else None)
                    w1ring[slot] = ("t", marks[1])
                for half in range(2):
                    pj_copy[half] = S.emit("vector",
                                           (lambda jc=jc, half=half, m=marks[half]: lambda e:
                                            e.tensor_copy(pjv(jc, half * 1024, 1024),
                                                          PS[3 - half][:]))(),
                                           waits=[("t", marks[half])], inc="v")
            pj_done = pj_copy[1]

            slot_last = {0: ("v", pj_copy[1]), 1: ("v", pj_copy[1])}
            def tr_slot(sl):
                return (PS[2][:, 0:64].bitcast(BF16) if sl == 0
                        else PS[2][:, 512:576].bitcast(BF16))
            prev_exp = None
            prev_z_evac = None
            last_zs_mark = None
            for qt in range(8):
                kmax = p * P + (qt + 1) * 128
                ntile = kmax // 128
                nkc = (kmax + 511) // 512
                tmk = None
                for kd in range(4):
                    for kc in range(nkc):
                        width = min(512, kmax - kc * 512)
                        psv = PS[kc // 2][:, (kc % 2) * 512:(kc % 2) * 512 + width]
                        w = []
                        if kd == 0 and kc == 0:
                            w.append(("s", v5_evacs[7]))
                            if prev_exp is not None:
                                w.append(("s", prev_exp))
                        tmk = S.emit("tensor",
                                     (lambda kd=kd, kc=kc, width=width, psv=psv, qt=qt: lambda e:
                                      e.matmul(psv, qs_ap(kd, qt * 128, 128),
                                               k5[:, kd * T + kc * 512: kd * T + kc * 512 + width],
                                               start=(kd == 0), stop=False))(),
                                     waits=w, inc=None)
                doff = kmax - 128
                kc_d = doff // 512
                off_d = doff % 512
                psv_d = PS[kc_d // 2][:, (kc_d % 2) * 512 + off_d:(kc_d % 2) * 512 + off_d + 128]
                tmk = S.emit("tensor", (lambda psv_d=psv_d: lambda e:
                                        e.matmul(psv_d, identb[:], cmaskb[:],
                                                 start=False, stop=True))(), inc="t")
                w1_ = min(kmax, 1024)
                vm = S.emit("vector", (lambda w1_=w1_: lambda e:
                                       e.reduce_max(smx[:, 0:1], PS[0][:, 0:w1_], axis=AX.X))(),
                            waits=[("t", tmk)], inc="v")
                if kmax > 1024:
                    vm = S.emit("vector", (lambda kmax=kmax: lambda e:
                                           e.reduce_max(smx[:, 1:2], PS[1][:, 0:kmax - 1024],
                                                        axis=AX.X))(), inc="v")
                    vm = S.emit("vector", lambda e: e.tensor_max(smx[:, 0:1], smx[:, 0:1],
                                                                smx[:, 1:2]), inc="v")
                vm = S.emit("vector", lambda e: e.tensor_scalar_mul(smx[:, 2:3], smx[:, 0:1], -1.0),
                            inc="v")
                em = S.emit("scalar", (lambda w1_=w1_: lambda e:
                                       e.activation(pbuf[:, 0:w1_], PS[0][:, 0:w1_], AF.Exp,
                                                    bias=smx[:, 2:3], accum_out=smx[:, 3:4]))(),
                            waits=[("v", vm)], inc="s")
                ell_col = 3
                if kmax > 1024:
                    em = S.emit("scalar", (lambda kmax=kmax: lambda e:
                                           e.activation(pbuf[:, 1024:kmax], PS[1][:, 0:kmax - 1024],
                                                        AF.Exp, bias=smx[:, 2:3],
                                                        accum_out=smx[:, 4:5]))(), inc="s")
                    S.emit("vector", lambda e: e.tensor_add(smx[:, 3:4], smx[:, 3:4], smx[:, 4:5]),
                           waits=[("s", em)], inc="v")
                prev_exp = em
                vm = S.emit("vector", lambda e: e.tensor_scalar_mul(smx[:, 5:6], smx[:, 3:4],
                                                                   SQRT_HS),
                            waits=[("s", em)], inc="v")
                rec_mark = S.emit("vector", lambda e: e.reciprocal(smx[:, 6:7], smx[:, 5:6]),
                                  inc="v")
                pt_evacs = [None] * ntile
                for i in range(ntile):
                    sl = i % 2
                    w = [("s", em), slot_last[sl]]
                    tmk = S.emit("tensor", (lambda i=i, sl=sl: lambda e:
                                            e.transpose(tr_slot(sl), pbuf[:, i * 128:(i + 1) * 128],
                                                        identb[:]))(),
                                 waits=w, inc="t")
                    pt_evacs[i] = S.emit("vector", (lambda i=i, sl=sl: lambda e:
                                                    e.tensor_copy(ptbuf[:, i * 128:(i + 1) * 128],
                                                                  tr_slot(sl)))(),
                                         waits=[("t", tmk)], inc="v")
                    slot_last[sl] = ("v", pt_evacs[i])
                tmk = None
                for i in range(ntile):
                    w = [("v", pt_evacs[i])]
                    if i == 0:
                        w.append(("s", v5_evacs[7]))
                        w.append(("v", pj_done))
                        if prev_z_evac is not None:
                            w.append(("v", prev_z_evac))
                    tmk = S.emit("tensor", (lambda i=i, ntile=ntile: lambda e:
                                            e.matmul(PS[3][:, 0:512],
                                                     ptbuf[:, i * 128:(i + 1) * 128],
                                                     v5[:, i * QD:(i + 1) * QD],
                                                     start=(i == 0), stop=(i == ntile - 1)))(),
                                 waits=w, inc="t" if i == ntile - 1 else None)
                prev_z_evac = S.emit("vector", lambda e: e.tensor_scalar_mul(A16[:, ZR_O:ZR_O + QD],
                                                                            PS[3][:, 0:512],
                                                                            smx[:, 6:7]),
                                     waits=[("t", tmk), ("v", rec_mark)], inc="v")
                for kd in range(4):
                    sl = kd % 2
                    w = [("v", prev_z_evac), slot_last[sl]]
                    tmk = S.emit("tensor", (lambda kd=kd, sl=sl: lambda e:
                                            e.transpose(tr_slot(sl), A16[:, ZR_O + kd * 128:ZR_O + (kd + 1) * 128],
                                                        identb[:]))(),
                                 waits=w, inc="t")
                    last_zs_mark = S.emit("vector", (lambda kd=kd, sl=sl, qt=qt: lambda e:
                                                     e.tensor_copy(zsv(kd, qt * 128, 128),
                                                                   tr_slot(sl)))(),
                                          waits=[("t", tmk)], inc="v")
                    slot_last[sl] = ("v", last_zs_mark)
            prev_add = [None, None]
            adds = [None] * NM2
            for m in range(NM2):
                psum = PS[m % 2]
                tmk = None
                for kd in range(4):
                    for n in range(2):
                        w = []
                        if kd == 0 and n == 0:
                            w.append(("v", last_zs_mark))
                            w.append(("s", prev_exp))
                            if prev_add[m % 2] is not None:
                                w.append(("v", prev_add[m % 2]))
                        tmk = S.emit("tensor", (lambda m=m, kd=kd, n=n, psum=psum: lambda e:
                                                e.matmul(psum[:, n * 512:(n + 1) * 512],
                                                         pjv(kd, m * 128, 128),
                                                         zsv(kd, n * 512, 512),
                                                         start=(kd == 0), stop=(kd == 3)))(),
                                     waits=w, inc="t" if (kd == 3 and n == 1) else None)
                prev_add[m % 2] = S.emit("vector", (lambda m=m, psum=psum: lambda e:
                                                    e.tensor_add(xc(m), xc(m), psum[:]))(),
                                         waits=[("t", tmk)], inc="v")
                adds[m] = prev_add[m % 2]
                emit_ln_stats_chunk(m, [("v", adds[m])])
            gate_tm2 = None
            for c in range(NK):
                for n in range(2):
                    w = []
                    if n == 0:
                        if c == 0:
                            w.append(("v", adds[14]))
                        w.append(("v", adds[c]))
                    last = (c == NK - 1 and n == 1)
                    gate_tm2 = S.emit("tensor", (lambda c=c, n=n: lambda e:
                                                 e.matmul(PS[0][0:8, n * 512:(n + 1) * 512],
                                                          gwt[:, c * NE:(c + 1) * NE],
                                                          xc(c, n * 512, (n + 1) * 512),
                                                          start=(c == 0), stop=(c == NK - 1)))(),
                                      waits=w, inc="t" if last else None)
            return gate_tm2

        def emit_out(p):
            slot_v = [("g", 0), ("g", 0)]
            dma_marks = {}
            for tt in range(8):
                outv = a32(OUT_O + (tt % 2) * 2048, 2048)
                vm = None
                for c in range(NK):
                    sl = c % 2
                    psv = PS[sl][:, 0:128]
                    w = [slot_v[sl]]
                    if c < 2 and tt >= 2:
                        w.append(("y", dma_marks[tt - 2]))
                    tmk = S.emit("tensor", (lambda c=c, tt=tt, psv=psv: lambda e:
                                            e.transpose(psv,
                                                        x_sb[:, c * P + tt * 128: c * P + (tt + 1) * 128],
                                                        identf[:]))(),
                                 waits=w, inc="t")
                    vm = S.emit("vector", (lambda c=c, psv=psv, outv=outv: lambda e:
                                           e.tensor_copy(outv[:, c * 128:(c + 1) * 128], psv))(),
                                waits=[("t", tmk)], inc="v")
                    slot_v[sl] = ("v", vm)
                dma_marks[tt] = S.emit("sync", (lambda tt=tt, p=p, outv=outv: lambda e:
                                                e.dma_start(yout[p * P + tt * 128:
                                                                 p * P + (tt + 1) * 128, :], outv))(),
                                       waits=[("v", vm)], inc="y", n=16)

        def emit_xload(p):
            marks = []
            for c in range(NK):
                m = S.emit("sync", (lambda c=c, p=p: lambda e:
                                    e.dma_start(x_sb[:, c * P:(c + 1) * P], xin[p, c]))(),
                           inc="y", n=16)
                marks.append(("y", m))
            return marks

        x_marks0 = emit_xload(0)
        prelude()
        S.barrier()
        npass = NPASS if stage == "full" else 1
        for p in range(npass):
            x_marks = x_marks0 if p == 0 else emit_xload(p)
            if p == 0:
                tap("xload", lambda: x_sb[:])
            gate_tm = emit_gate_mms([[xm] for xm in x_marks])
            vb = emit_ln(x_marks)
            if stage == "ln1":
                lm = emit_ln_apply()
                tap("lnb", lambda: lnb[:])
                tap("rb", lambda: rb[:])
                tap("nb", lambda: nb[:])
                break
            if stage == "gate":
                emit_gate(vb, gate_tm)
                lm = emit_ln_apply()
                tap("lnb", lambda: lnb[:])
                tap("rb", lambda: rb[:])
                tap("nb", lambda: nb[:])
                tap("cw", lambda: cw_tm[:])
                tap("glog", lambda: gsc[:])
                tap("ltm", lambda: cwsc[:, 0:64])
                tap("ee", lambda: cwsc[:, 64:128])
                tap("inds", lambda: cwsc[:, 128:192])
                tap("m1m2", lambda: cwsc[:, 192:216])
                tap("gws", lambda: gwsum[:])
                break
            emit_ffn(vb, gate_tm, ln_stats=True)
            if p == 0:
                tap("cw", lambda: cw_tm[:])
                tap("x1", lambda: x_sb[:])
            if stage == "ffn1":
                break
            S.barrier()
            vb = emit_ln_tail()
            lm = emit_ln_apply()
            gate_tm2 = emit_attn(p, lm, vb)
            if p == 0:
                tap("qs", lambda: A16[:, QS_O:QS_O + 3 * P])
                tap("k5", lambda: k5[:])
                tap("v5", lambda: v5[:])
                tap("zs", lambda: lnb[:, ZS_O:ZS_O + 4 * P])
                tap("x2", lambda: x_sb[:])
            if stage == "attn":
                tap("wqst", lambda: A16[:, WQST_O:WQST_O + 16 * 512])
                tap("bqs", lambda: bias_bf[0:1, 0:1536])
                break
            S.barrier()
            vb = emit_ln_tail()
            emit_ffn(vb, gate_tm2)
            S.barrier()
            emit_out(p)
            S.barrier()
        S.barrier()
        S.emit("sync", None, waits=[("y", S.c["y"])])
        S.check_deadlock()

        @block.tensor
        def _(e):
            S.replay("tensor", e, sems)

        @block.vector
        def _(e):
            S.replay("vector", e, sems)

        @block.scalar
        def _(e):
            S.replay("scalar", e, sems)

        @block.gpsimd
        def _(e):
            S.replay("gpsimd", e, sems)

        @block.sync
        def _(e):
            S.replay("sync", e, sems)

    return nc


def prep_inputs(L0_attn_x, L0_attn_WQ, L0_attn_bQ, L0_attn_WK, L0_attn_bK,
                L0_attn_WV, L0_attn_bV, L0_attn_proj, L0_ffn_gate_w,
                L0_ffn_w1, L0_ffn_w2, L0_ffn_sw1, L0_ffn_sw2, L0_ffn_sw3):
    C = np.ascontiguousarray
    f = np.float32
    w1 = np.asarray(L0_ffn_w1, f)
    w2 = np.asarray(L0_ffn_w2, f)
    shared = {
        "w1t": C(w1.reshape(NE, NK, 128, NM1, 128).transpose(0, 3, 2, 1, 4)
                 .reshape(NE, NM1, 128, NK * 128)),
        "w2t": C(w2.reshape(NE, NJ, 128, NM2, 128).transpose(0, 3, 2, 1, 4)
                 .reshape(NE, NM2, 128, NJ * 128)),
        "sw1t": C(np.asarray(L0_ffn_sw1, f).reshape(NJ, 128, NK, 128)
                  .transpose(0, 3, 2, 1).reshape(NJ, 128, NK * 128)),
        "sw2t": C(np.asarray(L0_ffn_sw2, f).reshape(NJ, 128, NK, 128)
                  .transpose(0, 3, 2, 1).reshape(NJ, 128, NK * 128)),
        "sw3t": C(np.asarray(L0_ffn_sw3, f).reshape(NM2, 128, NJ, 128)
                  .transpose(0, 3, 2, 1).reshape(NM2, 128, NJ * 128)),
        "gwt_in": C(np.asarray(L0_ffn_gate_w, f).T.reshape(NK, 128, NE)
                    .transpose(1, 0, 2).reshape(128, NK * NE)),
        "wqT_in": C(np.asarray(L0_attn_WQ, f).T.reshape(NK, 128, D)),
        "wkT_in": C(np.asarray(L0_attn_WK, f).T.reshape(NK, 128, QD)),
        "wvT_in": C(np.asarray(L0_attn_WV, f).T.reshape(NK, 128, QD)),
        "projt_in": C(np.asarray(L0_attn_proj, f).reshape(NK, 128, D)),
        "bq_in": C(np.asarray(L0_attn_bQ, f).reshape(1, D)),
        "bk_in": C(np.asarray(L0_attn_bK, f).reshape(1, QD)),
        "bv_in": C(np.asarray(L0_attn_bV, f).reshape(1, QD)),
        "cmask_in": C((np.triu(np.ones((128, 128), f), 1) * NEG).astype(f)),
        "ident_in": np.eye(128, dtype=f),
    }
    x = np.asarray(L0_attn_x, f)
    in_maps = []
    for b in range(NCORE):
        xT = C(x[b].T)
        xr = C(xT.reshape(NK, 128, NPASS, P).transpose(2, 0, 1, 3))
        m = dict(shared)
        m["xin"] = xr
        in_maps.append(m)
    return in_maps


def kernel(**inputs):
    in_maps = prep_inputs(**inputs)
    nc = build_nc()
    res = run_bass_kernel_spmd(nc, in_maps, core_ids=list(range(NCORE)))
    out = np.stack([res.results[i]["yout"] for i in range(NCORE)], axis=0)
    return out
